# version 12
# baseline (speedup 1.0000x reference)
import math
import numpy as np
import ml_dtypes
import concourse.bass as bass
import concourse.mybir as mybir
from concourse.bass_utils import run_bass_kernel_spmd

F32 = mybir.dt.float32
BF16 = mybir.dt.bfloat16
F16 = mybir.dt.float16
AF = mybir.ActivationFunctionType
ALU = mybir.AluOpType
AX = mybir.AxisListType

BF = np.dtype(ml_dtypes.bfloat16)

D = 1024
T = 4096
NG = 8
TT = 512
L1FFN_L = 2
NTT = T // TT


ENGINES = ("sync", "tensor", "vector", "scalar", "gpsimd")


class _Op:
    __slots__ = ("engine", "kind", "emit", "deps", "idx", "sem_val", "forced", "tag",
                 "pool_idx", "pool_use")

    def __init__(self, engine, kind, emit, tag=""):
        self.engine = engine
        self.kind = kind
        self.emit = emit
        self.deps = set()
        self.idx = 0
        self.sem_val = None
        self.forced = False
        self.tag = tag
        self.pool_idx = 0
        self.pool_use = 0


class Sched:
    def __init__(self, nc):
        self.nc = nc
        self.ops = {e: [] for e in ENGINES}
        self.all_ops = []
        self.key_w = {}
        self.key_r = {}
        self.barrier_tails = None
        self.barrier_pending = set()
        self.dma_counts = {}
        self.last_dma = {}

    def _dep(self, op, d):
        if d is None or d is op:
            return
        op.deps.add(d)

    def op(self, engine, reads, writes, emit, kind="c", tag=""):
        o = _Op(engine, kind, emit, tag)
        o.idx = len(self.all_ops)
        if self.barrier_tails is not None and engine in self.barrier_pending:
            for t in self.barrier_tails:
                self._dep(o, t)
            self.barrier_pending.discard(engine)
        for k in reads:
            self._dep(o, self.key_w.get(k))
        for k in writes:
            self._dep(o, self.key_w.get(k))
            for r in self.key_r.get(k, ()):
                self._dep(o, r)
        for k in reads:
            self.key_r.setdefault(k, []).append(o)
        for k in writes:
            self.key_w[k] = o
            self.key_r[k] = []
        self.ops[engine].append(o)
        self.all_ops.append(o)
        return o

    DMA_POOL = 6

    def dma(self, queue, reads, writes, emit, tag=""):
        o = self.op(queue, reads, writes, emit, kind="d", tag=tag)
        o.forced = True
        n = self.dma_counts.get(queue, 0)
        self.dma_counts[queue] = n + 1
        o.pool_idx = n % self.DMA_POOL
        o.pool_use = n // self.DMA_POOL
        o.sem_val = 16 * (o.pool_use + 1)
        self.last_dma[(queue, o.pool_idx)] = o
        return o

    def barrier(self):
        tails = []
        for e in ENGINES:
            for o in reversed(self.ops[e]):
                if o.kind == "c":
                    o.forced = True
                    tails.append(o)
                    break
        tails.extend(self.last_dma.values())
        self.barrier_tails = tails
        self.barrier_pending = set(ENGINES)
        self.key_w.clear()
        self.key_r.clear()

    STRICT_SAME_ENGINE = True

    def finalize(self):
        for o in self.all_ops:
            if o.kind == "c":
                o.forced = True
        for e in ENGINES:
            cnt_c = 0
            for o in self.ops[e]:
                if o.kind == "c" and o.forced:
                    cnt_c += 1
                    o.sem_val = cnt_c

    def _semkey(self, o):
        if o.kind == "c":
            return ("c", o.engine)
        return ("d", o.engine, o.pool_idx)

    def emit_all(self, stack):
        nc = self.nc
        sems = {}
        for e in ENGINES:
            if any(o.kind == "c" and o.forced for o in self.ops[e]):
                sems[("c", e)] = stack.enter_context(nc.semaphore(f"s_{e}"))
            nq = min(self.dma_counts.get(e, 0), self.DMA_POOL)
            for i in range(nq):
                sems[("d", e, i)] = stack.enter_context(nc.semaphore(f"q_{e}{i}"))
        self.sems = sems
        block = stack.enter_context(nc.Block())

        def mk(e):
            ops = self.ops[e]

            def body(eng):
                waited = {}

                def wait(sk, v):
                    if waited.get(sk, 0) >= v:
                        return
                    eng.wait_ge(sems[sk], v)
                    waited[sk] = v

                for o in ops:
                    wants = {}
                    if o.kind == "d" and o.pool_use > 0:
                        sk = self._semkey(o)
                        wants[sk] = 16 * o.pool_use
                    for d in o.deps:
                        if (not self.STRICT_SAME_ENGINE) and d.kind == "c" and d.engine == o.engine:
                            continue
                        sk = self._semkey(d)
                        v = d.sem_val
                        if v is None:
                            raise RuntimeError(f"dep without sem: {d.tag}")
                        if wants.get(sk, 0) < v:
                            wants[sk] = v
                    for sk, v in sorted(wants.items(), key=lambda kv: str(kv[0])):
                        wait(sk, v)
                    inst = o.emit(eng)
                    if o.forced:
                        if inst is None:
                            raise RuntimeError(f"op {o.tag} forced but emitted nothing")
                        inst.then_inc(sems[self._semkey(o)], 16 if o.kind == "d" else 1)

            return body

        for e in ENGINES:
            if self.ops[e]:
                getattr(block, e)(mk(e))




class Prog:

    def __init__(self):
        self.nc = bass.Bass("TRN2", target_bir_lowering=False, debug=False)
        self.s = Sched(self.nc)
        self.dram = {}
        self.inputs = {}
        self.outputs = []
        self._stack = None
        self.phase_stack = None
        self.phase_names = []
        self._phase_cms = {}
        self._global_cms = []
        self.buf = {}
        self.psum = {}
        self._uid = 0

    def uid(self):
        self._uid += 1
        return self._uid

    def input(self, name, percore):
        if name in self.inputs:
            return self.dram[name]
        if not isinstance(percore, list):
            percore = [percore] * 8
        a = percore[0]
        t = self.nc.dram_tensor(name, list(a.shape), mybir.dt.from_np(a.dtype), kind="ExternalInput").ap()
        self.inputs[name] = percore
        self.dram[name] = t
        return t

    def scratch(self, name, shape, dtype):
        if name in self.dram:
            return self.dram[name]
        t = self.nc.dram_tensor(name, list(shape), dtype, kind="Internal").ap()
        self.dram[name] = t
        return t

    def output(self, name, shape, dtype):
        if name in self.dram:
            return self.dram[name]
        t = self.nc.dram_tensor(name, list(shape), dtype, kind="ExternalOutput").ap()
        self.outputs.append(name)
        self.dram[name] = t
        return t

    def sbuf(self, name, shape, dtype, scope="phase", side="left"):
        if name in self.buf:
            return self.buf[name]
        if not hasattr(self, "_name_ver"):
            self._name_ver = {}
        v = self._name_ver.get(name, 0)
        self._name_ver[name] = v + 1
        real = name if v == 0 else f"{name}__{v}"
        cm = self.nc.sbuf_tensor(real, list(shape), dtype, side=side)
        ap = cm.__enter__().ap()
        self.buf[name] = ap
        if (scope == "global" or self.phase_stack is None) and side == "left":
            self._global_cms.append(cm)
        else:
            self._phase_cms[name] = cm
        return ap

    def free_buf(self, name):
        cm = self._phase_cms.pop(name, None)
        if cm is not None:
            cm.__exit__(None, None, None)
        self.buf.pop(name, None)

    def close_phase(self):
        for n in reversed(list(self._phase_cms)):
            self.free_buf(n)
        self.phase_stack = True

    def psum_tile(self, name, shape=(128, 512), dtype=F32):
        if name in self.psum:
            return self.psum[name]
        ap = self._stack.enter_context(self.nc.psum_tensor(name, list(shape), dtype)).ap()
        self.psum[name] = ap
        return ap

    def bank(self, i, rows=128, cols=512):
        if "PS" not in self.psum:
            self.psum["PS"] = self._stack.enter_context(
                self.nc.psum_tensor("PS", [128, 4096], F32)).ap()
        return self.psum["PS"][0:rows, i * 512:i * 512 + cols]

    def banks(self, i0, n):
        if "PS" not in self.psum:
            self.bank(0)
        return self.psum["PS"][:, i0 * 512:(i0 + n) * 512]

    def fconst(self, val):
        val = float(val)
        if not hasattr(self, "_fc"):
            self._fc = {}
            self._fc_arr = np.zeros((128, 64), np.float32)
            self.input("c_fvals", self._fc_arr)
            t = self.sbuf("fvals", (128, 64), F32, scope="global")
            self.s.dma("sync", [], ["fvals"],
                       lambda e: e.dma_start(out=t[...], in_=self.dram["c_fvals"][...]),
                       tag="ld_fvals")
        if val not in self._fc:
            idx = len(self._fc)
            assert idx < 64
            self._fc[val] = idx
            self._fc_arr[:, idx] = val
        return self.buf["fvals"][:, self._fc[val]:self._fc[val] + 1]


def to_np(dt):
    return mybir.dt.np(dt)




def emit_store_x(p):
    nc, s = p.nc, p.s
    x = p.buf["x"]
    yh = p.output("y", (D, T), F32)
    for g in range(NG):
        s.dma("sync", [("x", g)], [("out", g)],
              lambda e, g=g: e.dma_start(out=yh[g * 128:(g + 1) * 128, :], in_=x[:, g, :]),
              tag=f"stx{g}")
    s.op("gpsimd", [("out", g_) for g_ in range(NG)], [],
         lambda e: e.nop(), tag="final_wait")


def emit_const(p, name, arr, scope="global"):
    if name in p.buf:
        return p.buf[name]
    h = p.input("c_" + name, np.ascontiguousarray(arr))
    t = p.sbuf(name, arr.shape, mybir.dt.from_np(arr.dtype), scope=scope)
    p.s.dma("sync", [], [name], lambda e: e.dma_start(out=t[...], in_=h[...]), tag="ld_" + name)
    return t


def _consts(p):
    emit_const(p, "ones_col", np.ones((1, 128), np.float32))
    emit_const(p, "ones_k", np.ones((128, 1), np.float32))
    emit_const(p, "eye_bf", np.eye(128, dtype=BF))
    emit_const(p, "eye_f32", np.eye(128, dtype=np.float32))
    emit_const(p, "onerow_bf", np.ones((1, TT), BF))
    emit_const(p, "onecol_bf", np.ones((128, 1), BF))
    p.fconst(0.0)
    p.sbuf("ln_muB", (128, 1, TT), F32, scope="global")
    p.sbuf("ln_rB", (128, 1, TT), F32, scope="global")
    p.sbuf("ln_rows", (1, 3, TT), F32, scope="global")
    p.sbuf("ln_sq", (128, 1, TT), F32, scope="global")
    p.sbuf("lntmp", (128, 1, TT), F32, scope="global")


def emit_ln_tile(p, u, tt, src=None, src_keys=None, eps=1e-5, nt=TT, src_offs=None):
    nc, s = p.nc, p.s
    if src is None:
        src = p.buf["x"]
        src_keys = [("x", g) for g in range(NG)]
    if src_offs is None:
        src_offs = [0] * NG
    b = 0
    t0 = tt * nt
    muB = p.sbuf("ln_muB", (128, 1, TT), F32, scope="global")
    rB = p.sbuf("ln_rB", (128, 1, TT), F32, scope="global")
    rows = p.sbuf("ln_rows", (1, 3, TT), F32, scope="global")
    sq = p.sbuf("ln_sq", (128, 1, TT), F32, scope="global")
    ps1 = p.bank(5, rows=1, cols=TT)
    ps2 = p.bank(6, rows=1, cols=TT)
    psb = p.bank(4, cols=TT)
    ones_k = p.buf["ones_k"]
    ones_col = p.buf["ones_col"]

    def mm_s1(e):
        last = None
        for g in range(NG):
            o_ = src_offs[g]
            last = e.matmul(ps1[:, :nt], ones_k[:, :], src[:, g, t0 + o_:t0 + o_ + nt],
                            start=(g == 0), stop=(g == NG - 1))
        return last
    s.op("tensor", src_keys, [("psb", 5)], mm_s1, tag=f"lns1_{u}{tt}")
    for g in range(NG):
        s.op("scalar", [src_keys[g]], [("ln_sq", 0)],
             lambda e, g=g, o_=src_offs[g]: e.square(out=sq[:, 0, :nt],
                                                     in_=src[:, g, t0 + o_:t0 + o_ + nt]),
             tag=f"lnsq{u}{tt}_{g}")
        s.op("tensor", [("ln_sq", 0)], [("psb", 6)],
             lambda e, g=g: e.matmul(ps2[:, :nt], ones_k[:, :], sq[:, 0, :nt],
                                     start=(g == 0), stop=(g == NG - 1)),
             tag=f"lns2_{u}{tt}_{g}")
    s.op("vector", [("psb", 5)], [("ln_mu_row",)],
         lambda e: e.tensor_scalar_mul(rows[:, 0, :nt], ps1[:, :nt], 1.0 / D), tag=f"lnmu{u}{tt}")
    s.op("vector", [("ln_mu_row",)], [("ln_var_row",)],
         lambda e: e.tensor_mul(rows[:, 1, :nt], rows[:, 0, :nt], rows[:, 0, :nt]),
         tag=f"lnv0{u}{tt}")
    s.op("vector", [("psb", 6), ("ln_var_row",)], [("ln_var_row",)],
         lambda e: e.scalar_tensor_tensor(out=rows[:, 1, :nt], in0=ps2[:, :nt], scalar=1.0 / D,
                                          in1=rows[:, 1, :nt], op0=ALU.mult, op1=ALU.subtract),
         tag=f"lnv1{u}{tt}")
    c_eps = p.fconst(eps)
    c_nh = p.fconst(-0.5)
    s.op("scalar", [("ln_var_row",), "fvals"], [("ln_r_row",)],
         lambda e: e.activation(out=rows[:, 2, :nt], in_=rows[:, 1, :nt], func=AF.Ln,
                                bias=c_eps[0:1, :]),
         tag=f"lnln{u}{tt}")
    s.op("scalar", [("ln_r_row",), "fvals"], [("ln_r_row2",)],
         lambda e: e.activation(out=rows[:, 1, :nt], in_=rows[:, 2, :nt], func=AF.Exp,
                                scale=c_nh[0:1, :]),
         tag=f"lnexp{u}{tt}")
    s.op("tensor", [("ln_mu_row",)], [("psb", 4)],
         lambda e: e.matmul(psb[:, :nt], ones_col[:, :], rows[:, 0, :nt], start=True, stop=True),
         tag=f"lnbm{u}{tt}")
    s.op("vector", [("psb", 4)], [("lnmuB", b)],
         lambda e: e.tensor_copy(out=muB[:, b, :nt], in_=psb[:, :nt]), tag=f"lnbe{u}{tt}")
    s.op("tensor", [("ln_r_row2",), ("lnmuB", b)], [("psb", 4)],
         lambda e: e.matmul(psb[:, :nt], ones_col[:, :], rows[:, 1, :nt], start=True, stop=True),
         tag=f"lnbm2{u}{tt}")
    s.op("vector", [("psb", 4)], [("lnB", b)],
         lambda e: e.tensor_copy(out=rB[:, b, :nt], in_=psb[:, :nt]), tag=f"lnbe2{u}{tt}")


def emit_ln_apply(p, u, tt, g, dst_sl, dkey, src=None, src_key=None, nt=TT,
                  tmp_name="lntmp", sub=None, src_off=0):
    nc, s = p.nc, p.s
    if src is None:
        src = p.buf["x"]
        src_key = ("x", g)
    b = 0
    t0 = tt * nt
    off, w = sub if sub is not None else (0, nt)
    muB = p.buf["ln_muB"]
    rB = p.buf["ln_rB"]
    veng = "vector" if g % 2 == 0 else "gpsimd"
    if veng == "vector":
        tmp = p.sbuf(tmp_name, (128, 1, TT), F32, scope="global")
    else:
        tmp_name = "ln_sq"
        tmp = p.sbuf("ln_sq", (128, 1, TT), F32, scope="global")
    tb = 0
    s.op(veng, [src_key, ("lnmuB", b), ("lnB", b)], [(tmp_name, tb)],
         lambda e: e.tensor_sub(tmp[:, tb, :w], src[:, g, t0 + off + src_off:t0 + off + src_off + w],
                                muB[:, b, off:off + w]),
         tag=f"lna{u}{tt}_{g}_{off}")
    s.op(veng, [(tmp_name, tb), ("lnB", b)], [dkey],
         lambda e: e.tensor_mul(dst_sl, tmp[:, tb, :w], rB[:, b, off:off + w]),
         tag=f"lnb{u}{tt}_{g}_{off}")




def emit_l3ffn(p, inp, call):
    nc, s = p.nc, p.s
    x = p.buf["x"]
    u = f"l3f{call}"
    wT = p.input("l3f_wT", np.ascontiguousarray(inp["L3_ffn_weight"].T))
    swT = p.sbuf("l3f_swT", (128, NG, D), BF16)
    wstg = p.sbuf("l3f_wstg", (128, 2, D), F32)
    if True:
        for kg in range(NG):
            s.dma("gpsimd", [], [("l3f_wstg", kg % 2)],
                  lambda e, kg=kg: e.dma_start(out=wstg[:, kg % 2, :], in_=wT[kg * 128:(kg + 1) * 128, :]),
                  tag=f"{u}ldw{kg}")
            s.op("scalar", [("l3f_wstg", kg % 2)], [("l3f_swT", kg)],
                 lambda e, kg=kg: e.sign(out=swT[:, kg, :], in_=wstg[:, kg % 2, :]),
                 tag=f"{u}sgn{kg}")
    xn = p.sbuf("l3f_xn", (128, NG, TT), BF16)
    for tt in range(NTT):
        t0 = tt * TT
        emit_ln_tile(p, u, tt)
        for g in range(NG):
            emit_ln_apply(p, u, tt, g, xn[:, g, :], ("l3f_xn", g, u))
        for m in range(NG):
            ps = p.bank(m % 4)
            def mm(e, m=m, t0=t0, ps=ps):
                last = None
                for kg in range(NG):
                    last = e.matmul(ps[:, :], swT[:, kg, m * 128:(m + 1) * 128], xn[:, kg, :],
                                    start=(kg == 0), stop=(kg == NG - 1))
                return last
            s.op("tensor",
                 [("l3f_xn", g, u) for g in range(NG)] + [("l3f_swT", kg) for kg in range(NG)],
                 [("pmm", m % 4)], mm, tag=f"{u}mm{tt}_{m}")
            s.op("vector", [("pmm", m % 4)], [("x", m)],
                 lambda e, m=m, t0=t0, ps=ps: e.scalar_tensor_tensor(
                     out=x[:, m, t0:t0 + TT], in0=ps[:, :], scalar=0.5, in1=x[:, m, t0:t0 + TT],
                     op0=ALU.mult, op1=ALU.add),
                 tag=f"{u}acc{tt}_{m}")


def emit_to_bf16(p, name, host_arr, logical_shape=None):
    nc, s = p.nc, p.s
    key = name + "_bf"
    if key in p.dram:
        return p.dram[key]
    n = host_arr.size
    C = 4096
    while n % (128 * C) != 0:
        C //= 2
    R = n // C
    h2 = np.ascontiguousarray(host_arr).reshape(R, C)
    h = p.input(name, h2)
    dst = p.scratch(key, list(logical_shape or host_arr.shape), BF16)
    dst2 = dst.rearrange(" ".join(f"a{i}" for i in range(len(dst.shape))) + " -> ("
                         + " ".join(f"a{i}" for i in range(len(dst.shape))) + ")").rearrange(
        "(r c) -> r c", c=C)
    srcdt = mybir.dt.from_np(host_arr.dtype)
    sn = "w_cvt_stg_" + str(srcdt)
    src_t = p.sbuf(sn, (128, 2, 4096), srcdt, side="right")
    stgb = p.sbuf("w_cvt_stgb", (128, 2, 4096), BF16, side="right")
    for r in range(R // 128):
        b = r % 2
        s.dma("sync", [("wcvt", b)], [("wcvt_in", b)],
              lambda e, r=r, b=b: e.dma_start(
                  out=src_t[:, b, :C], in_=h[r * 128:(r + 1) * 128, :]),
              tag=f"cv_ld_{name}_{r}")
        s.op("scalar", [("wcvt_in", b)], [("wcvt", b)],
             lambda e, r=r, b=b: e.copy(out=stgb[:, b, :C], in_=src_t[:, b, :C]),
             tag=f"cv_{name}_{r}")
        s.dma("gpsimd", [("wcvt", b)], [("wcvt_out", b)],
              lambda e, r=r, b=b: e.dma_start(out=dst2[r * 128:(r + 1) * 128, :],
                                              in_=stgb[:, b, :C]),
              tag=f"cv_st_{name}_{r}")
    return dst


def _free_cvt(p, u):
    p.s.barrier()
    p.free_buf(u + "_brst")
    p.free_buf("w_cvt_stgb")
    p.free_buf("w_cvt_stg_" + str(F32))
    p.free_buf("w_cvt_stg_" + str(F16))




def emit_l0ffn(p, inp, call):
    nc, s = p.nc, p.s
    x = p.buf["x"]
    u = f"l0f{call}"
    w1 = np.asarray(inp["L0_ffn_w1"])
    w2 = np.asarray(inp["L0_ffn_w2"])
    w1t = np.ascontiguousarray(w1.reshape(8, 8, 128, 32, 128).transpose(0, 3, 2, 1, 4))
    w1bf = emit_to_bf16(p, "l0f_w1", w1t, (8, 32, 128, 8, 128))
    w2bf = emit_to_bf16(p, "l0f_w2", w2, (8, 32, 128, 1024))
    gwbf = emit_to_bf16(p, "l0f_gw", np.ascontiguousarray(np.asarray(inp["L0_ffn_gate_w"])))
    b2bf = emit_to_bf16(p, "l0f_b2", np.ascontiguousarray(np.asarray(inp["L0_ffn_b2"])))
    b1h = p.input("l0f_b1", np.ascontiguousarray(
        np.asarray(inp["L0_ffn_b1"]).reshape(8, 32, 128).transpose(0, 2, 1)))
    _free_cvt(p, u)

    gwb = p.sbuf(u + "_gwb", (128, NG, 8), BF16)
    b1c = p.sbuf(u + "_b1c", (128, 8, 32), F32)
    b2b = p.sbuf(u + "_b2b", (8, D), BF16)
    s.dma("sync", [], [(u, "gwb")],
          lambda e: e.dma_start(out=gwb[...], in_=gwbf.rearrange("(g p) e -> p g e", p=128)),
          tag=u + "ldgw")
    s.dma("sync", [], [(u, "b1c")], lambda e: e.dma_start(out=b1c[...], in_=b1h[...]),
          tag=u + "ldb1")
    s.dma("sync", [], [(u, "b2b")], lambda e: e.dma_start(out=b2b[...], in_=b2bf[...]),
          tag=u + "ldb2")

    xnt = p.sbuf(u + "_xnt", (128, NG, TT), BF16)
    glog = p.sbuf(u + "_gl", (128, 4, 8), F32)
    gm = p.sbuf(u + "_gm", (128, 4, 5), F32)
    geq = p.sbuf(u + "_geq", (128, 3, 4, 8), F32)
    wT = p.sbuf(u + "_wT", (8, TT), BF16)
    h_sb = p.sbuf(u + "_h", (128, 32, TT), BF16)
    y_acc = p.sbuf(u + "_y", (128, 4, D), BF16)
    w1s = p.sbuf(u + "_w1s", (128, 2, NG, 128), BF16)
    w2s = p.sbuf(u + "_w2s", (128, 2, D), BF16)

    for tt in range(NTT):
        t0 = tt * TT
        emit_ln_tile(p, u, tt)
        for g in range(NG):
            emit_ln_apply(p, u, tt, g, xnt[:, g, :], (u, "xnt", g))
        xk = [(u, "xnt", g) for g in range(NG)]
        def mmg(e):
            last = None
            for mt in range(4):
                for kg in range(NG):
                    last = e.matmul(p.bank(4)[:, mt * 8:mt * 8 + 8],
                                    xnt[:, kg, mt * 128:(mt + 1) * 128], gwb[:, kg, :],
                                    start=(kg == 0), stop=(kg == NG - 1),
                                    skip_group_check=True)
            return last
        s.op("tensor", xk + [(u, "gwb")], [("psb", 4)], mmg, tag=f"{u}gmm{tt}")
        s.op("vector", [("psb", 4)], [(u, "gl")],
             lambda e: e.tensor_copy(out=glog.rearrange("p a b -> p (a b)"),
                                     in_=p.bank(4)[:, 0:32]), tag=f"{u}gev{tt}")
        for mt in range(4):
            gl = glog[:, mt, :]
            ge = "vector"
            s.op("vector", [(u, "gl")], [(u, "gm", mt, 0)],
                 lambda e, gl=gl, mt=mt: e.reduce_max(gm[:, mt, 0:1], gl, axis=AX.X),
                 tag=f"{u}m1_{tt}{mt}")
            s.op("vector", [(u, "gl"), (u, "gm", mt, 0)], [(u, "geq", 1, mt)],
                 lambda e, gl=gl, mt=mt: e.tensor_scalar(out=geq[:, 1, mt, :], in0=gl,
                                                         scalar1=gm[:, mt, 0:1], scalar2=None,
                                                         op0=ALU.is_equal),
                 tag=f"{u}eq1_{tt}{mt}")
            s.op(ge, [(u, "geq", 1, mt), (u, "gl")], [(u, "geq", 2, mt)],
                 lambda e, gl=gl, mt=mt: e.scalar_tensor_tensor(
                     out=geq[:, 2, mt, :], in0=geq[:, 1, mt, :], scalar=-1e30, in1=gl,
                     op0=ALU.mult, op1=ALU.add),
                 tag=f"{u}msk_{tt}{mt}")
            s.op("vector", [(u, "geq", 2, mt)], [(u, "gm", mt, 1)],
                 lambda e, mt=mt: e.reduce_max(gm[:, mt, 1:2], geq[:, 2, mt, :], axis=AX.X),
                 tag=f"{u}m2_{tt}{mt}")
            s.op(ge, [(u, "gm", mt, 0), (u, "gm", mt, 1)], [(u, "gm", mt, 4)],
                 lambda e, mt=mt: e.tensor_sub(gm[:, mt, 4:5], gm[:, mt, 0:1], gm[:, mt, 1:2]),
                 tag=f"{u}d_{tt}{mt}")
            s.op("scalar", [(u, "gm", mt, 4)], [(u, "gm", mt, 2)],
                 lambda e, mt=mt: e.activation(out=gm[:, mt, 2:3], in_=gm[:, mt, 4:5],
                                               func=AF.Sigmoid),
                 tag=f"{u}s1_{tt}{mt}")
            s.op(ge, [(u, "gm", mt, 2)], [(u, "gm", mt, 3)],
                 lambda e, mt=mt: e.tensor_scalar(out=gm[:, mt, 3:4], in0=gm[:, mt, 2:3],
                                                  scalar1=-1.0, scalar2=1.0,
                                                  op0=ALU.mult, op1=ALU.add),
                 tag=f"{u}s2_{tt}{mt}")
            s.op("vector", [(u, "geq", 1, mt), (u, "gl"), (u, "gm", mt, 1)], [(u, "geq", 0, mt)],
                 lambda e, gl=gl, mt=mt: e.tensor_scalar(out=geq[:, 0, mt, :], in0=gl,
                                                         scalar1=gm[:, mt, 1:2], scalar2=None,
                                                         op0=ALU.is_equal),
                 tag=f"{u}eq2_{tt}{mt}")
            s.op(ge, [(u, "geq", 0, mt), (u, "geq", 1, mt)], [(u, "geq", 2, mt)],
                 lambda e, mt=mt: e.tensor_mul(geq[:, 2, mt, :], geq[:, 0, mt, :], geq[:, 1, mt, :]),
                 tag=f"{u}ee_{tt}{mt}")
            s.op(ge, [(u, "geq", 0, mt), (u, "geq", 2, mt)], [(u, "geq", 0, mt)],
                 lambda e, mt=mt: e.tensor_sub(geq[:, 0, mt, :], geq[:, 0, mt, :], geq[:, 2, mt, :]),
                 tag=f"{u}ep_{tt}{mt}")
            s.op("vector", [(u, "geq", 0, mt), (u, "gm", mt, 3)], [(u, "geq", 0, mt)],
                 lambda e, mt=mt: e.tensor_scalar_mul(geq[:, 0, mt, :], geq[:, 0, mt, :],
                                                      gm[:, mt, 3:4]),
                 tag=f"{u}w2_{tt}{mt}")
            s.op("vector", [(u, "geq", 1, mt), (u, "gm", mt, 2), (u, "geq", 0, mt)], [(u, "gw_w", mt)],
                 lambda e, mt=mt: e.scalar_tensor_tensor(
                     out=geq[:, 1, mt, :], in0=geq[:, 1, mt, :], scalar=gm[:, mt, 2:3],
                     in1=geq[:, 0, mt, :], op0=ALU.mult, op1=ALU.add),
                 tag=f"{u}w_{tt}{mt}")
            wbk = 5 + mt % 2
            s.op("tensor", [(u, "gw_w", mt)], [("psb", wbk)],
                 lambda e, mt=mt, wbk=wbk: e.transpose(p.bank(wbk, rows=8, cols=128),
                                                       geq[:, 1, mt, :],
                                                       p.buf["eye_f32"][...]),
                 tag=f"{u}wT_{tt}{mt}")
            s.op("vector" if mt % 2 == 0 else "scalar", [("psb", wbk)], [(u, "wT")],
                 lambda e, mt=mt, wbk=wbk: (e.tensor_copy if mt % 2 == 0 else e.copy)(
                     out=wT[:, mt * 128:(mt + 1) * 128],
                     in_=p.bank(wbk, rows=8, cols=128)),
                 tag=f"{u}wTe_{tt}{mt}")
        for mt in range(4):
            for ds in range(2):
                bk = mt * 2 + ds
                s.op("tensor", [(u, "wT"), (u, "b2b")], [("psb", bk)],
                     lambda e, mt=mt, ds=ds, bk=bk: e.matmul(
                         p.bank(bk), wT[:, mt * 128:(mt + 1) * 128],
                         b2b[:, ds * 512:(ds + 1) * 512], start=True, stop=True),
                     tag=f"{u}b2m{tt}_{mt}{ds}")
                s.op("vector" if ds == 0 else "scalar", [("psb", bk)], [(u, "y", mt, ds)],
                     lambda e, mt=mt, ds=ds, bk=bk: (e.tensor_copy if ds == 0 else e.copy)(
                         out=y_acc[:, mt, ds * 512:(ds + 1) * 512], in_=p.bank(bk)),
                     tag=f"{u}b2e{tt}_{mt}{ds}")
        for ex in range(8):
            for fs in range(32):
                wb = fs % 2
                s.dma("sync", [(u, "w1s", wb)], [(u, "w1s", wb)],
                      lambda e, ex=ex, fs=fs, wb=wb: e.dma_start(
                          out=w1s[:, wb], in_=w1bf[ex, fs]),
                      tag=f"{u}lw1_{tt}_{ex}_{fs}")
                bk = 4 + fs % 2
                def mmh(e, fs=fs, wb=wb, bk=bk):
                    last = None
                    for kg in range(NG):
                        last = e.matmul(p.bank(bk), w1s[:, wb, kg, :], xnt[:, kg, :],
                                        start=(kg == 0), stop=(kg == NG - 1))
                    return last
                s.op("tensor", [(u, "w1s", wb)] + xk, [("psb", bk)], mmh,
                     tag=f"{u}h{tt}_{ex}_{fs}")
                s.op("scalar", [("psb", bk), (u, "b1c")], [(u, "h", fs)],
                     lambda e, ex=ex, fs=fs, bk=bk: e.activation(
                         out=h_sb[:, fs, :], in_=p.bank(bk), func=AF.Gelu,
                         bias=b1c[:, ex, fs:fs + 1]),
                     tag=f"{u}g{tt}_{ex}_{fs}")
            for fs in range(32):
                wb = fs % 2
                s.dma("gpsimd" if fs % 2 == 0 else "scalar",
                      [(u, "w2s", wb)], [(u, "w2s", wb)],
                      lambda e, ex=ex, fs=fs, wb=wb: e.dma_start(
                          out=w2s[:, wb, :], in_=w2bf[ex, fs]),
                      tag=f"{u}lw2_{tt}_{ex}_{fs}")
                def mmy(e, fs=fs, wb=wb):
                    last = None
                    for mt in range(4):
                        for ds in range(2):
                            last = e.matmul(p.bank(mt * 2 + ds),
                                            h_sb[:, fs, mt * 128:(mt + 1) * 128],
                                            w2s[:, wb, ds * 512:(ds + 1) * 512],
                                            start=(fs == 0), stop=(fs == 31),
                                            skip_group_check=True)
                    return last
                s.op("tensor", [(u, "w2s", wb), (u, "h", fs)],
                     [("psb", b) for b in range(8)], mmy, tag=f"{u}y{tt}_{ex}_{fs}")
            for mt, ds in ((2, 0), (2, 1), (0, 0), (0, 1), (1, 0), (1, 1), (3, 0), (3, 1)):
                    bk = mt * 2 + ds
                    s.op("vector",
                         [("psb", bk), (u, "gw_w", mt), (u, "y", mt, ds)],
                         [(u, "y", mt, ds)],
                         lambda e, ex=ex, mt=mt, ds=ds, bk=bk: e.scalar_tensor_tensor(
                             out=y_acc[:, mt, ds * 512:(ds + 1) * 512], in0=p.bank(bk),
                             scalar=geq[:, 1, mt, ex:ex + 1],
                             in1=y_acc[:, mt, ds * 512:(ds + 1) * 512],
                             op0=ALU.mult, op1=ALU.add),
                         tag=f"{u}ya{tt}_{ex}_{mt}{ds}")
        for mt in range(4):
            for g in range(NG):
                bk = g % 2
                s.op("tensor", [(u, "y", mt, g // 4)], [("psb", bk)],
                     lambda e, mt=mt, g=g, bk=bk: e.transpose(
                         p.bank(bk).bitcast(BF16)[:, :128], y_acc[:, mt, g * 128:(g + 1) * 128],
                         p.buf["eye_bf"][...]),
                     tag=f"{u}yT{tt}_{mt}{g}")
                s.op("vector", [("psb", bk)], [("x", g)],
                     lambda e, mt=mt, g=g, bk=bk, t0=t0: e.scalar_tensor_tensor(
                         out=x[:, g, t0 + mt * 128:t0 + (mt + 1) * 128],
                         in0=p.bank(bk).bitcast(BF16)[:, :128], scalar=0.5,
                         in1=x[:, g, t0 + mt * 128:t0 + (mt + 1) * 128],
                         op0=ALU.mult, op1=ALU.add),
                     tag=f"{u}xa{tt}_{mt}{g}")




def emit_mha(p, u, cfg):
    nc, s = p.nc, p.s
    x = p.buf["x"]
    wq_bf = emit_to_bf16(p, u + "_wq", cfg["wq"])
    wk_bf = emit_to_bf16(p, u + "_wk", cfg["wk"])
    wv_bf = emit_to_bf16(p, u + "_wv", cfg["wv"])
    wo_bf = emit_to_bf16(p, u + "_wo", cfg["wo"])
    has_bias = cfg.get("bias_row") is not None
    if has_bias:
        brh = p.input(u + "_br", np.ascontiguousarray(cfg["bias_row"]))
        brst = p.sbuf(u + "_brst", (1, 4 * D), mybir.dt.from_np(cfg["bias_row"].dtype), side="right")
        brow = p.sbuf(u + "_brow", (1, 4 * D), BF16)
        s.dma("sync", [], [(u, "brst")], lambda e: e.dma_start(out=brst[...], in_=brh[...]),
              tag=u + "ldb")
        s.op("scalar", [(u, "brst")], [(u, "brow")],
             lambda e: e.copy(out=brow[...], in_=brst[...]), tag=u + "cvb")
    s.barrier()
    p.free_buf(u + "_brst")
    p.free_buf("w_cvt_stgb")
    p.free_buf("w_cvt_stg_" + str(F32))
    p.free_buf("w_cvt_stg_" + str(F16))
    onerow = p.buf["onerow_bf"]
    onecol_bf = p.buf["onecol_bf"]
    tri = emit_const(p, u + "_tri",
                     np.triu(np.full((128, 128), cfg["mask_val"], BF), k=1), scope="phase")
    xnh = p.scratch(u + "_xnh", (D, T), BF16)
    oh = p.scratch(u + "_oh", (D, T), BF16)
    xnh_r = xnh.rearrange("(g p) t -> p g t", p=128)
    oh_r = oh.rearrange("(g p) t -> p g t", p=128)
    qsc = p.fconst(cfg["q_scale"])

    xs = p.sbuf(u + "_xs", (128, 1, NG, TT), BF16)
    qb = p.sbuf(u + "_qb", (128, T), BF16)
    kb = p.sbuf(u + "_kb", (128, T), BF16)
    vb = p.sbuf(u + "_vb", (128, 32, 128), BF16)
    pbuf = p.sbuf(u + "_pb", (128, T), BF16)
    pts = p.sbuf(u + "_pts", (128, 4, 128), BF16)
    ost = p.sbuf(u + "_ost", (128, 2, 128), BF16)
    mcol = p.sbuf(u + "_mcol", (128, 1), F32)
    scol = p.sbuf(u + "_scol", (128, 1), F32)
    rcol = p.sbuf(u + "_rcol", (128, 1), F32)
    rrow = p.sbuf(u + "_rrow", (1, 128), F32)
    rbs = p.sbuf(u + "_rbs", (64, 128), F32)
    wsb = p.sbuf(u + "_wsb", (128, NG, 3, 128), BF16)

    for tt in range(NTT):
        emit_ln_tile(p, u, tt)
        for g in range(NG):
            emit_ln_apply(p, u, tt, g, xs[:, 0, g, :], (u, "xs", 0))
        s.dma("sync", [(u, "xs", 0)], [(u, "xnh", tt)],
              lambda e, tt=tt: e.dma_start(out=xnh_r[:, :, tt * TT:(tt + 1) * TT],
                                           in_=xs[:, 0]),
              tag=f"{u}stxn{tt}")

    for hp in range(8):
        c0 = hp * 128
        for kg in range(NG):
            for wi, wsrc in enumerate((wq_bf, wk_bf, wv_bf)):
                s.dma("scalar", [(u, "wsb", kg, wi)], [(u, "wsb", kg, wi)],
                      lambda e, kg=kg, wi=wi, wsrc=wsrc, c0=c0: e.dma_start(
                          out=wsb[:, kg, wi, :], in_=wsrc[kg * 128:(kg + 1) * 128, c0:c0 + 128]),
                      tag=f"{u}ldw{hp}_{kg}_{wi}")
        for tt in range(NTT):
            bsl = 0
            t0 = tt * TT
            s.dma("gpsimd", [(u, "xnh", tt), (u, "xs", bsl)], [(u, "xs", bsl)],
                  lambda e, tt=tt, bsl=bsl: e.dma_start(
                      out=xs[:, bsl], in_=xnh_r[:, :, tt * TT:(tt + 1) * TT]),
                  tag=f"{u}ldxn{hp}_{tt}")
            wk_keys = [(u, "wsb", kg, wi) for kg in range(NG) for wi in range(3)]
            for wi, dstb, bkr in ((0, qb, 0), (1, kb, 1)):
                def mmqk(e, wi=wi, bkr=bkr, bsl=bsl, c0=c0):
                    ps = p.bank(bkr)
                    last = None
                    for kg in range(NG):
                        last = e.matmul(ps, wsb[:, kg, wi, :], xs[:, bsl, kg, :],
                                        start=(kg == 0), stop=(not has_bias and kg == NG - 1))
                    if has_bias:
                        last = e.matmul(ps, brow[0:1, wi * D + c0:wi * D + c0 + 128],
                                        onerow[:, :], start=False, stop=True)
                    return last
                s.op("tensor", wk_keys + [(u, "xs", bsl), (u, "brow")], [("psb", bkr)],
                     mmqk, tag=f"{u}qk{hp}_{tt}_{wi}")
                sc = qsc if wi == 0 else None
                def evqk(e, wi=wi, bkr=bkr, dstb=dstb, t0=t0, sc=sc):
                    if sc is not None:
                        return e.activation(out=dstb[:, t0:t0 + TT], in_=p.bank(bkr),
                                            func=AF.Copy, scale=sc)
                    return e.copy(out=dstb[:, t0:t0 + TT], in_=p.bank(bkr))
                s.op("scalar", [("psb", bkr), "fvals"], [(u, "qb" if wi == 0 else "kb")],
                     evqk, tag=f"{u}eqk{hp}_{tt}_{wi}")
            for j in range(4):
                bkr = 2 + j % 2
                def mmv(e, j=j, bkr=bkr, bsl=bsl, c0=c0):
                    ps = p.bank(bkr, cols=128)
                    last = None
                    for kg in range(NG):
                        last = e.matmul(ps, xs[:, bsl, kg, j * 128:(j + 1) * 128], wsb[:, kg, 2, :],
                                        start=(kg == 0), stop=(not has_bias and kg == NG - 1))
                    if has_bias:
                        last = e.matmul(ps, onerow[:, :128], brow[0:1, 2 * D + c0:2 * D + c0 + 128],
                                        start=False, stop=True)
                    return last
                s.op("tensor", wk_keys + [(u, "xs", bsl), (u, "brow")], [("psb", bkr)],
                     mmv, tag=f"{u}v{hp}_{tt}_{j}")
                s.op("scalar", [("psb", bkr)], [(u, "vb")],
                     lambda e, j=j, tt=tt, bkr=bkr: e.copy(out=vb[:, tt * 4 + j, :],
                                                           in_=p.bank(bkr, cols=128)),
                     tag=f"{u}ev{hp}_{tt}_{j}")
        for h in (0, 1):
            r0 = h * 64
            for qt in range(32):
                L = (qt + 1) * 128
                nb = (L + 511) // 512
                PS = p.psum["PS"]
                def mms(e, qt=qt, L=L, nb=nb, r0=r0):
                    last = None
                    for b in range(nb):
                        w = min(512, L - b * 512)
                        last = e.matmul(PS[:, b * 512:b * 512 + w],
                                        qb[r0:r0 + 64, qt * 128:(qt + 1) * 128],
                                        kb[r0:r0 + 64, b * 512:b * 512 + w],
                                        start=True, stop=True)
                    return last
                s.op("tensor", [(u, "qb"), (u, "kb")], [("psb", b) for b in range(nb)],
                     mms, tag=f"{u}S{hp}_{h}_{qt}")
                s.op("vector", [u + "_tri", ("psb", nb - 1)], [("psb", nb - 1)],
                     lambda e, L=L: e.tensor_add(PS[:, L - 128:L], PS[:, L - 128:L], tri[...]),
                     tag=f"{u}msk{hp}_{h}_{qt}")
                s.op("vector", [("psb", b) for b in range(nb)], [(u, "mcol")],
                     lambda e, L=L: e.reduce_max(mcol[:, 0:1], PS[:, :L], axis=AX.X, negate=True),
                     tag=f"{u}max{hp}_{h}_{qt}")
                s.op("scalar", [(u, "mcol")] + [("psb", b) for b in range(nb)],
                     [(u, "pb"), (u, "scol")],
                     lambda e, L=L: e.activation(out=pbuf[:, :L], in_=PS[:, :L], func=AF.Exp,
                                                 bias=mcol[:, 0:1], accum_out=scol[:, 0:1]),
                     tag=f"{u}exp{hp}_{h}_{qt}")
                nblk = qt + 1
                def emit_T(c):
                    bb = 2 + c % 4
                    s.op("tensor", [(u, "pb")], [("psb", bb)],
                         lambda e, c=c, bb=bb: e.transpose(
                             p.bank(bb).bitcast(BF16)[:, :128],
                             pbuf[:, c * 128:(c + 1) * 128], p.buf["eye_bf"][...]),
                         tag=f"{u}T{hp}_{h}_{qt}_{c}")
                    s.op("vector", [("psb", bb)], [(u, "pts", c % 4)],
                         lambda e, c=c, bb=bb: e.tensor_copy(
                             out=pts[:, c % 4, :], in_=p.bank(bb).bitcast(BF16)[:, :128]),
                         tag=f"{u}TE{hp}_{h}_{qt}_{c}")
                def emit_PV(c):
                    s.op("tensor", [(u, "pts", c % 4), (u, "vb")], [("psb", 7)],
                         lambda e, c=c, r0=r0, qt=qt: e.matmul(
                             p.bank(7, rows=64, cols=128),
                             vb[:, c, r0:r0 + 64], pts[:, c % 4, :],
                             start=(c == 0), stop=(c == qt),
                             skip_group_check=True),
                         tag=f"{u}PV{hp}_{h}_{qt}_{c}")
                for c in range(nblk):
                    emit_T(c)
                    if c >= 1:
                        emit_PV(c - 1)
                emit_PV(nblk - 1)
                s.op("vector", [(u, "scol")], [(u, "rcol")],
                     lambda e: e.reciprocal(rcol[:, :], scol[:, :]),
                     tag=f"{u}ric{hp}_{h}_{qt}")
                s.op("tensor", [(u, "rcol")], [("psb", 6)],
                     lambda e: e.transpose(p.bank(6, rows=1, cols=128), rcol[:, :],
                                           p.buf["eye_f32"][...]),
                     tag=f"{u}rcT{hp}_{h}_{qt}")
                s.op("scalar", [("psb", 6)], [(u, "rrow")],
                     lambda e: e.copy(out=rrow[:, :], in_=p.bank(6, rows=1, cols=128)),
                     tag=f"{u}rce{hp}_{h}_{qt}")
                s.op("tensor", [(u, "rrow")], [("psb", 4)],
                     lambda e: e.matmul(p.bank(4, rows=64, cols=128), p.buf["ones_col"][:, :64],
                                        rrow[:, :], start=True, stop=True),
                     tag=f"{u}rb{hp}_{h}_{qt}")
                s.op("vector", [("psb", 4)], [(u, "rbs")],
                     lambda e: e.tensor_copy(out=rbs[:, :], in_=p.bank(4, rows=64, cols=128)),
                     tag=f"{u}rbe{hp}_{h}_{qt}")
                ob2 = qt % 2
                s.op("vector", [("psb", 7), (u, "rbs"), (u, "ost", h, ob2)], [(u, "ost", h, ob2)],
                     lambda e, r0=r0, ob2=ob2: e.tensor_mul(ost[r0:r0 + 64, ob2, :],
                                                            p.bank(7, rows=64, cols=128), rbs[:, :]),
                     tag=f"{u}oe{hp}_{h}_{qt}")
                s.dma("sync", [(u, "ost", h, ob2)], [(u, "oh", hp)],
                      lambda e, hp=hp, r0=r0, qt=qt, ob2=ob2: e.dma_start(
                          out=oh[hp * 128 + r0:hp * 128 + r0 + 64, qt * 128:(qt + 1) * 128],
                          in_=ost[r0:r0 + 64, ob2, :]),
                      tag=f"{u}sto{hp}_{h}_{qt}")

    if cfg.get("skip_out"):
        return
    for tt in range(NTT):
        bsl = 0
        t0 = tt * TT
        s.dma("gpsimd", [(u, "oh", hp_) for hp_ in range(8)] + [(u, "xs", bsl)], [(u, "xs", bsl)],
              lambda e, tt=tt, bsl=bsl: e.dma_start(
                  out=xs[:, bsl], in_=oh_r[:, :, tt * TT:(tt + 1) * TT]),
              tag=f"{u}ldo{tt}")
        for m in range(NG):
            wb2 = m % 2
            s.dma("scalar", [(u, "wos", wb2)], [(u, "wos", wb2)],
                  lambda e, m=m, wb2=wb2: e.dma_start(
                      out=wsb[:, :, wb2, :], in_=wo_bf.rearrange("(g p) d -> p g d", p=128)[:, :, m * 128:(m + 1) * 128]),
                  tag=f"{u}ldwo{tt}_{m}")
            def mmo(e, m=m, wb2=wb2, bsl=bsl):
                ps = p.bank(m % 4)
                last = None
                for kg in range(NG):
                    last = e.matmul(ps, wsb[:, kg, wb2, :], xs[:, bsl, kg, :],
                                    start=(kg == 0), stop=(not has_bias and kg == NG - 1))
                if has_bias:
                    last = e.matmul(ps, brow[0:1, 3 * D + m * 128:3 * D + (m + 1) * 128],
                                    onerow[:, :], start=False, stop=True)
                return last
            s.op("tensor", [(u, "wos", wb2), (u, "xs", bsl), (u, "brow")], [("psb", m % 4)],
                 mmo, tag=f"{u}o{tt}_{m}")
            s.op("vector", [("psb", m % 4)], [("x", m)],
                 lambda e, m=m, t0=t0: e.tensor_add(x[:, m, t0:t0 + TT], x[:, m, t0:t0 + TT],
                                                    p.bank(m % 4)),
                 tag=f"{u}acc{tt}_{m}")


def emit_l0attn(p, inp):
    f = lambda k: np.asarray(inp["L0_attn_" + k])
    bias_row = np.concatenate([f("bq"), f("bk"), f("bv"), f("bo")])[None, :]
    cfg = dict(wq=f("wq").T, wk=f("wk").T, wv=f("wv").T, wo=f("wo").T,
               bias_row=np.ascontiguousarray(bias_row), q_scale=float(np.float16(0.125)),
               mask_val=-10000.0)
    emit_mha(p, "l0a", cfg)


def emit_l3attn(p, inp):
    wqkv = np.asarray(inp["L3_attn_w_qkv"])
    wq = np.ascontiguousarray(wqkv.reshape(1024, 16, 3, 64)[:, :, 0, :].reshape(1024, 1024))
    wk = np.ascontiguousarray(wqkv.reshape(1024, 16, 3, 64)[:, :, 1, :].reshape(1024, 1024))
    wv = np.ascontiguousarray(wqkv.reshape(1024, 16, 3, 64)[:, :, 2, :].reshape(1024, 1024))
    cfg = dict(wq=wq, wk=wk, wv=wv, wo=np.asarray(inp["L3_attn_w_o"]),
               bias_row=None, q_scale=1.0 / 32.0, mask_val=-1e30)
    emit_mha(p, "l3a", cfg)




def emit_l2ffn(p, inp, call):
    nc, s = p.nc, p.s
    x = p.buf["x"]
    u = f"l2f{call}"
    W = 2040
    binh = p.input("l2f_bin", np.ascontiguousarray(np.asarray(inp["L2_ffn_b_in"])[None, :]))
    bins = p.sbuf(u + "_bins", (1, 2040), F32, side="right")
    s.dma("sync", [], [(u, "bins")], lambda e: e.dma_start(out=bins[...], in_=binh[...]),
          tag=u + "ldbin")
    winT = emit_to_bf16(p, "l2f_winT", np.ascontiguousarray(np.asarray(inp["L2_ffn_W_in"]).T))
    wpad = np.zeros((2048, 1024), np.float32)
    wpad[:W] = np.asarray(inp["L2_ffn_W_out"]).T
    wot = np.ascontiguousarray(wpad.reshape(16, 128, 8, 128).transpose(2, 1, 0, 3))
    woutT = emit_to_bf16(p, "l2f_woutT", wot, (8, 128, 16, 128))
    _free_cvt(p, u)
    binb = p.sbuf(u + "_binb", (1, W), BF16)
    s.op("scalar", [(u, "bins")], [(u, "binb")],
         lambda e: e.copy(out=binb[...], in_=bins[...]), tag=u + "cvbin")
    p.s.barrier()
    p.free_buf(u + "_bins")
    onerow = p.buf["onerow_bf"]

    xnt = p.sbuf(u + "_xnt", (128, NG, 128), BF16)
    acts = p.sbuf(u + "_acts", (128, 2048), BF16)
    dec = p.sbuf(u + "_dec", (128, W), BF16)
    vis = p.sbuf(u + "_vis", (128, W), BF16)
    mT = p.sbuf(u + "_mT", (128, 16, 128), BF16)
    wsl = p.sbuf(u + "_wsl", (128, 2, 16, 128), BF16)
    win_s = p.sbuf(u + "_win", (128, NG, W), BF16)
    NSL = (512, 512, 512, 504)
    for kg in range(NG):
        s.dma("gpsimd", [], [(u, "win", kg)],
              lambda e, kg=kg: e.dma_start(out=win_s[:, kg, :],
                                           in_=winT[kg * 128:(kg + 1) * 128, :]),
              tag=f"{u}ldwin{kg}")
    s.op("vector", [], [(u, "acts")],
         lambda e: e.memset(acts[:, W:], 0.0), tag=u + "mz")
    for tt in range(NTT):
        t0 = tt * TT
        emit_ln_tile(p, u, tt)
        for mt in range(4):
            tc = t0 + mt * 128
            for g in range(NG):
                emit_ln_apply(p, u, tt, g, xnt[:, g, :], (u, "xnt", g), sub=(mt * 128, 128))
            xk = [(u, "xnt", g) for g in range(NG)]
            def mml(e, mt=mt):
                last = None
                for ns in range(4):
                    n0 = 512 * ns
                    nw = NSL[ns]
                    ps = p.psum["PS"][:, ns * 512:ns * 512 + nw]
                    for kg in range(NG):
                        e.matmul(ps, xnt[:, kg, :], win_s[:, kg, n0:n0 + nw],
                                 start=(kg == 0), stop=False, skip_group_check=True)
                    last = e.matmul(ps, onerow[:, :128], binb[:, n0:n0 + nw],
                                    start=False, stop=True, skip_group_check=True)
                return last
            s.op("tensor", xk + [(u, "win", kg) for kg in range(NG)] + [(u, "binb")],
                 [("psb", b) for b in range(4)], mml, tag=f"{u}mml{tt}_{mt}")
            if globals().get("L2F_MM_ONLY"):
                s.op("vector", [("psb", b) for b in range(4)], [(u, "acts")],
                     lambda e: e.tensor_copy(out=acts[:, 0:W], in_=p.psum["PS"][:, 0:W]),
                     tag=f"{u}dbgev{tt}_{mt}")
                continue
            PSW = p.psum["PS"][:, 0:W]
            s.op("scalar", [("psb", b) for b in range(4)], [(u, "acts")],
                 lambda e: e.activation(out=acts[:, 0:W], in_=PSW, func=AF.Silu),
                 tag=f"{u}silu{tt}_{mt}")
            s.op("scalar", [("psb", b) for b in range(4)], [(u, "dec")],
                 lambda e: e.sign(out=dec[...], in_=PSW), tag=f"{u}dec{tt}_{mt}")
            s.op("vector", [(u, "dec")], [(u, "dec")],
                 lambda e: e.tensor_relu(out=dec[...], in_=dec[...]), tag=f"{u}dec2{tt}_{mt}")
            v3 = vis.rearrange("p (tr n) -> p tr n", n=255)
            d3 = dec.rearrange("p (tr n) -> p tr n", n=255)
            if globals().get("L2F_NOTREE"):
                s.op("vector", [], [(u, "vis")],
                     lambda e: e.memset(vis[...], 1.0), tag=f"{u}vr{tt}_{mt}")
            else:
                s.op("vector", [], [(u, "vis")],
                     lambda e: e.memset(v3[:, :, 0:1], 1.0), tag=f"{u}vr{tt}_{mt}")
            for dl in ([] if globals().get("L2F_NOTREE") else range(7)):
                cp = (1 << dl) - 1
                npl = (1 << (dl + 1)) - 1
                n = 1 << dl
                s.op("vector", [(u, "vis"), (u, "dec")], [(u, "vis")],
                     lambda e, cp=cp, npl=npl, n=n: e.tensor_mul(
                         v3[:, :, npl + 1:npl + 2 * n:2], v3[:, :, cp:cp + n], d3[:, :, cp:cp + n]),
                     tag=f"{u}vo{tt}_{mt}_{dl}")
                s.op("vector", [(u, "vis")], [(u, "vis")],
                     lambda e, cp=cp, npl=npl, n=n: e.tensor_sub(
                         v3[:, :, npl:npl + 2 * n:2], v3[:, :, cp:cp + n],
                         v3[:, :, npl + 1:npl + 2 * n:2]),
                     tag=f"{u}ve{tt}_{mt}_{dl}")
            s.op("vector", [(u, "acts"), (u, "vis")], [(u, "acts")],
                 lambda e: e.tensor_mul(acts[:, 0:W], acts[:, 0:W], vis[...]), tag=f"{u}mm{tt}_{mt}")
            if globals().get("L2F_SKIP_OUT"):
                continue
            for jp in range(0, 16, 2):
                def mmt(e, jp=jp):
                    last = None
                    for i in range(2):
                        j = jp + i
                        bv = p.bank(4 + i).bitcast(BF16)
                        last = e.transpose(bv[:, :128], acts[:, j * 128:(j + 1) * 128],
                                           p.buf["eye_bf"][...])
                    return last
                s.op("tensor", [(u, "acts")], [("psb", 4), ("psb", 5)], mmt,
                     tag=f"{u}T{tt}_{mt}_{jp}")
                def evt(e, jp=jp):
                    last = None
                    for i in range(2):
                        j = jp + i
                        bv = p.bank(4 + i).bitcast(BF16)
                        last = e.tensor_copy(out=mT[:, j, :], in_=bv[:, :128])
                    return last
                s.op("vector", [("psb", 4), ("psb", 5)], [(u, "mT", jp)], evt,
                     tag=f"{u}TE{tt}_{mt}_{jp}")
            for m in range(NG):
                wb = m % 2
                s.dma("sync", [(u, "wsl", wb)], [(u, "wsl", wb)],
                      lambda e, m=m, wb=wb: e.dma_start(out=wsl[:, wb], in_=woutT[m]),
                      tag=f"{u}lwo{tt}_{mt}_{m}")
                def mmo(e, m=m, wb=wb):
                    last = None
                    for j in range(16):
                        last = e.matmul(p.bank(6, cols=128), wsl[:, wb, j, :],
                                        mT[:, j, :], start=(j == 0), stop=(j == 15))
                    return last
                s.op("tensor", [(u, "wsl", wb)] + [(u, "mT", jp) for jp in range(0, 16, 2)],
                     [("psb", 6)], mmo, tag=f"{u}mo{tt}_{mt}_{m}")
                s.op("vector", [("psb", 6)], [("x", m)],
                     lambda e, m=m, tc=tc: e.scalar_tensor_tensor(
                         out=x[:, m, tc:tc + 128], in0=p.bank(6, cols=128), scalar=0.5,
                         in1=x[:, m, tc:tc + 128], op0=ALU.mult, op1=ALU.add),
                     tag=f"{u}xa{tt}_{mt}_{m}")




def emit_l2attn(p, inp):
    nc, s = p.nc, p.s
    x = p.buf["x"]
    u = "l2a"
    I2 = 2048
    Dh = p.input("l2a_D", np.ascontiguousarray(np.asarray(inp["L2_attn_D"])[None, :]))
    Ds = p.sbuf(u + "_Ds", (1, I2), F32, side="right")
    s.dma("sync", [], [(u, "Ds")], lambda e: e.dma_start(out=Ds[...], in_=Dh[...]),
          tag=u + "ldD")
    win = emit_to_bf16(p, "l2a_win", np.ascontiguousarray(np.asarray(inp["L2_attn_W_in"])))
    wdt_t = np.ascontiguousarray(
        np.asarray(inp["L2_attn_W_dt"]).reshape(16, 128, 16, 128).transpose(2, 1, 0, 3))
    wdt = emit_to_bf16(p, "l2a_wdt", wdt_t, (16, 128, 16, 128))
    wB = emit_to_bf16(p, "l2a_wB", np.ascontiguousarray(np.asarray(inp["L2_attn_W_B"])))
    wC = emit_to_bf16(p, "l2a_wC", np.ascontiguousarray(np.asarray(inp["L2_attn_W_C"])))
    wot = np.ascontiguousarray(
        np.asarray(inp["L2_attn_W_out"]).reshape(16, 128, 8, 128).transpose(2, 1, 0, 3))
    wout = emit_to_bf16(p, "l2a_wout", wot, (8, 128, 16, 128))
    convh = p.input("l2a_convw", np.ascontiguousarray(
        np.asarray(inp["L2_attn_conv_w"])[:, 0, :].reshape(16, 128, 4).transpose(1, 0, 2)))
    convbh = p.input("l2a_convb", np.ascontiguousarray(
        np.asarray(inp["L2_attn_conv_b"]).reshape(16, 128).T))
    bdth = p.input("l2a_bdt", np.ascontiguousarray(
        np.asarray(inp["L2_attn_b_dt"]).reshape(16, 128).T))
    _free_cvt(p, u)
    Drow = p.sbuf(u + "_Drow", (1, I2), BF16, side="right")
    s.op("scalar", [(u, "Ds")], [(u, "Drow")],
         lambda e: e.copy(out=Drow[...], in_=Ds[...]), tag=u + "cvD")
    DBt = p.sbuf(u + "_DBt", (128, I2), BF16)
    for q4 in range(4):
        s.op("tensor", [(u, "Drow")], [("psb", 6)],
             lambda e, q4=q4: e.matmul(p.bank(6), p.buf["onerow_bf"][:, :128],
                                       Drow[:, q4 * 512:(q4 + 1) * 512],
                                       start=True, stop=True),
             tag=f"{u}DB{q4}")
        s.op("vector", [("psb", 6)], [(u, "DBt")],
             lambda e, q4=q4: e.tensor_copy(out=DBt[:, q4 * 512:(q4 + 1) * 512],
                                            in_=p.bank(6)),
             tag=f"{u}DBe{q4}")
    s.barrier()
    p.free_buf(u + "_Drow")
    p.free_buf(u + "_Ds")

    cw = p.sbuf(u + "_cw", (128, 16, 4), F32)
    cb = p.sbuf(u + "_cb", (128, 16), F32)
    bdtc = p.sbuf(u + "_bdtc", (128, 16), F32)
    s.dma("sync", [], [(u, "cw")], lambda e: e.dma_start(out=cw[...], in_=convh[...]),
          tag=u + "ldcw")
    s.dma("sync", [], [(u, "cb")], lambda e: e.dma_start(out=cb[...], in_=convbh[...]),
          tag=u + "ldcb")
    s.dma("sync", [], [(u, "bdtc")], lambda e: e.dma_start(out=bdtc[...], in_=bdth[...]),
          tag=u + "ldbdt")

    tri_bd = np.zeros((128, 128), np.float32)
    tri_bd[:64, :64] = np.triu(np.ones((64, 64), np.float32))
    tri_bd[64:, 64:] = np.triu(np.ones((64, 64), np.float32))
    TriBD = emit_const(p, "l2a_tri", tri_bd, scope="phase")
    ubd = np.zeros((128, 128), BF)
    ubd[:64, :64] = np.triu(np.ones((64, 64), BF))
    ubd[64:, 64:] = np.triu(np.ones((64, 64), BF))
    UBD = emit_const(p, "l2a_ubd", ubd, scope="phase")
    indc = np.zeros((128, 4), np.float32)
    indc[0:64, 0] = 1.0
    indc[64:128, 1] = 1.0
    indc[64:96, 2] = 1.0
    indc[0:32, 3] = 1.0
    IND = emit_const(p, "l2a_ind", indc, scope="phase")
    onerow = p.buf["onerow_bf"]
    ones_col = p.buf["ones_col"]

    xin_h = p.scratch(u + "_xin", (I2, T), BF16)
    xcf_h = p.scratch(u + "_xcf", (I2, T), BF16)
    xct_h = p.scratch(u + "_xct", (T, I2), BF16)
    dtt_h = p.scratch(u + "_dtt", (T, I2), F32)
    zs_h = p.scratch(u + "_zs", (T, I2), BF16)
    btm_h = p.scratch(u + "_btm", (T, 64), BF16)
    ctm_h = p.scratch(u + "_ctm", (T, 64), BF16)
    bfm_h = p.scratch(u + "_bfm", (64, T), BF16)
    cfm_h = p.scratch(u + "_cfm", (64, T), BF16)
    win_r = win.rearrange("(g pp) i -> pp g i", pp=128)

    xnt = p.sbuf(u + "_xnt", (128, NG, TT), BF16)
    wstr = p.sbuf(u + "_wstr", (128, 2, NG, 128), BF16)
    zw = p.sbuf(u + "_zw", (128, NG, TT), BF16)
    stg1 = p.sbuf(u + "_stg1", (128, 2, TT), BF16)
    zstg = p.sbuf(u + "_zstg", (128, 4, TT), BF16)
    for tt in range(NTT):
        t0 = tt * TT
        emit_ln_tile(p, u, tt)
        for g in range(NG):
            emit_ln_apply(p, u, tt, g, xnt[:, g, :], (u, "xnt", g))
        xk = [(u, "xnt", g) for g in range(NG)]
        for m in range(16):
            wb = m % 2
            s.dma("scalar", [(u, "wstr", wb)], [(u, "wstr", wb)],
                  lambda e, m=m, wb=wb: e.dma_start(
                      out=wstr[:, wb], in_=win_r[:, :, m * 128:(m + 1) * 128]),
                  tag=f"{u}lwin{tt}_{m}")
            def mmx(e, m=m, wb=wb):
                last = None
                for kg in range(NG):
                    last = e.matmul(p.bank(m % 4), wstr[:, wb, kg, :], xnt[:, kg, :],
                                    start=(kg == 0), stop=(kg == NG - 1))
                return last
            s.op("tensor", [(u, "wstr", wb)] + xk, [("psb", m % 4)], mmx, tag=f"{u}xi{tt}_{m}")
            sb = m % 2
            s.op("scalar", [("psb", m % 4)], [(u, "stg1", sb)],
                 lambda e, m=m, sb=sb: e.copy(out=stg1[:, sb, :], in_=p.bank(m % 4)),
                 tag=f"{u}xie{tt}_{m}")
            s.dma("gpsimd", [(u, "stg1", sb)], [(u, "xinh", m)],
                  lambda e, m=m, sb=sb, t0=t0: e.dma_start(
                      out=xin_h[m * 128:(m + 1) * 128, t0:t0 + TT], in_=stg1[:, sb, :]),
                  tag=f"{u}xis{tt}_{m}")
        for zs4 in range(4):
            s.dma("scalar", [(u, "zw")], [(u, "zw")],
                  lambda e, zs4=zs4: e.dma_start(
                      out=zw[...], in_=win_r[:, :, I2 + zs4 * 512:I2 + (zs4 + 1) * 512]),
                  tag=f"{u}lwz{tt}_{zs4}")
            def mmz(e):
                last = None
                for mt in range(4):
                    ps = p.bank(4 + mt)
                    for kg in range(NG):
                        last = e.matmul(ps, xnt[:, kg, mt * 128:(mt + 1) * 128],
                                        zw[:, kg, :], start=(kg == 0), stop=(kg == NG - 1),
                                        skip_group_check=True)
                return last
            s.op("tensor", [(u, "zw")] + xk, [("psb", 4 + mt_) for mt_ in range(4)], mmz,
                 tag=f"{u}z{tt}_{zs4}")
            for mt in range(4):
                s.op("scalar", [("psb", 4 + mt)], [(u, "zstg", mt)],
                     lambda e, mt=mt: e.activation(out=zstg[:, mt, :], in_=p.bank(4 + mt),
                                                   func=AF.Silu),
                     tag=f"{u}ze{tt}_{zs4}_{mt}")
                s.dma("gpsimd", [(u, "zstg", mt)], [(u, "zsh", tt, mt, zs4)],
                      lambda e, mt=mt, zs4=zs4, t0=t0: e.dma_start(
                          out=zs_h[t0 + mt * 128:t0 + (mt + 1) * 128, zs4 * 512:(zs4 + 1) * 512],
                          in_=zstg[:, mt, :]),
                      tag=f"{u}zs{tt}_{zs4}_{mt}")
    s.barrier()
    for n in (u + "_zstg", u + "_stg1", u + "_zw", u + "_wstr", u + "_xnt"):
        p.free_buf(n)

    xint = p.sbuf(u + "_xint", (128, TT + 8), BF16)
    cacc = p.sbuf(u + "_cacc", (128, TT), F32)
    xcst = p.sbuf(u + "_xcst", (128, 2, TT), BF16)
    xtst = p.sbuf(u + "_xtst", (128, 2, 128), BF16)
    for ig in range(16):
        for tt in range(NTT):
            t0 = tt * TT
            if tt == 0:
                s.op("vector", [(u, "xint")], [(u, "xint")],
                     lambda e: e.memset(xint[:, 0:3], 0.0), tag=f"{u}cz{ig}")
                s.dma("sync", [(u, "xint")], [(u, "xint")],
                      lambda e, ig=ig: e.dma_start(out=xint[:, 3:3 + TT],
                                                   in_=xin_h[ig * 128:(ig + 1) * 128, 0:TT]),
                      tag=f"{u}cl{ig}_{tt}")
            else:
                s.dma("sync", [(u, "xint")], [(u, "xint")],
                      lambda e, ig=ig, t0=t0: e.dma_start(
                          out=xint[:, 0:3 + TT], in_=xin_h[ig * 128:(ig + 1) * 128, t0 - 3:t0 + TT]),
                      tag=f"{u}cl{ig}_{tt}")
            s.op("vector", [(u, "xint"), (u, "cw")], [(u, "cacc")],
                 lambda e, ig=ig: e.tensor_scalar_mul(cacc[...], xint[:, 0:TT], cw[:, ig, 0:1]),
                 tag=f"{u}c0{ig}_{tt}")
            for j in (1, 2, 3):
                s.op("vector", [(u, "xint"), (u, "cw"), (u, "cacc")], [(u, "cacc")],
                     lambda e, ig=ig, j=j: e.scalar_tensor_tensor(
                         out=cacc[...], in0=xint[:, j:j + TT], scalar=cw[:, ig, j:j + 1],
                         in1=cacc[...], op0=ALU.mult, op1=ALU.add),
                     tag=f"{u}c{j}_{ig}_{tt}")
            sb = tt % 2
            s.op("scalar", [(u, "cacc"), (u, "cb")], [(u, "xcst", sb)],
                 lambda e, ig=ig, sb=sb: e.activation(out=xcst[:, sb, :], in_=cacc[...],
                                                      func=AF.Silu, bias=cb[:, ig:ig + 1]),
                 tag=f"{u}cs{ig}_{tt}")
            s.dma("gpsimd", [(u, "xcst", sb)], [(u, "xcfh", ig, tt)],
                  lambda e, ig=ig, sb=sb, t0=t0: e.dma_start(
                      out=xcf_h[ig * 128:(ig + 1) * 128, t0:t0 + TT], in_=xcst[:, sb, :]),
                  tag=f"{u}cst{ig}_{tt}")
            for j4 in range(4):
                tb = j4 % 2
                s.op("tensor", [(u, "xcst", sb)], [("psb", tb)],
                     lambda e, sb=sb, j4=j4, tb=tb: e.transpose(
                         p.bank(tb).bitcast(BF16)[:, :128], xcst[:, sb, j4 * 128:(j4 + 1) * 128],
                         p.buf["eye_bf"][...]),
                     tag=f"{u}cT{ig}_{tt}_{j4}")
                s.op("vector", [("psb", tb)], [(u, "xtst", tb)],
                     lambda e, tb=tb: e.tensor_copy(out=xtst[:, tb, :],
                                                    in_=p.bank(tb).bitcast(BF16)[:, :128]),
                     tag=f"{u}cTe{ig}_{tt}_{j4}")
                s.dma("gpsimd", [(u, "xtst", tb)], [(u, "xcth", ig, tt, j4)],
                      lambda e, ig=ig, tb=tb, t0=t0, j4=j4: e.dma_start(
                          out=xct_h[t0 + j4 * 128:t0 + (j4 + 1) * 128, ig * 128:(ig + 1) * 128],
                          in_=xtst[:, tb, :]),
                      tag=f"{u}cTs{ig}_{tt}_{j4}")
    s.barrier()
    for n in (u + "_xtst", u + "_xcst", u + "_cacc", u + "_xint"):
        p.free_buf(n)

    xcfb = p.sbuf(u + "_xcfb", (128, 16, TT), BF16)
    wsl = p.sbuf(u + "_wsl", (128, 2, 16, 128), BF16)
    dstg = p.sbuf(u + "_dstg", (128, TT), F32)
    spt = p.sbuf(u + "_spt", (128, 2, TT), F32)
    dts = p.sbuf(u + "_dts", (128, 2, 128), F32)
    wBb = p.sbuf(u + "_wBb", (128, 16, 64), BF16)
    wCb = p.sbuf(u + "_wCb", (128, 16, 64), BF16)
    bcst = p.sbuf(u + "_bcst", (128, 2, 64), BF16)
    bfst = p.sbuf(u + "_bfst", (64, TT), BF16)
    for ki in range(16):
        s.dma("sync", [], [(u, "wBb", ki)],
              lambda e, ki=ki: e.dma_start(out=wBb[:, ki, :], in_=wB[ki * 128:(ki + 1) * 128, :]),
              tag=f"{u}lwB{ki}")
        s.dma("sync", [], [(u, "wCb", ki)],
              lambda e, ki=ki: e.dma_start(out=wCb[:, ki, :], in_=wC[ki * 128:(ki + 1) * 128, :]),
              tag=f"{u}lwC{ki}")
    wk_bc = [(u, "wBb", ki) for ki in range(16)] + [(u, "wCb", ki) for ki in range(16)]
    for tt in range(NTT):
        t0 = tt * TT
        for ki in range(16):
            s.dma("gpsimd", [(u, "xcfb", ki)], [(u, "xcfb", ki)],
                  lambda e, ki=ki, t0=t0: e.dma_start(
                      out=xcfb[:, ki, :], in_=xcf_h[ki * 128:(ki + 1) * 128, t0:t0 + TT]),
                  tag=f"{u}dxc{tt}_{ki}")
        xck = [(u, "xcfb", ki) for ki in range(16)]
        for m2 in range(16):
            wb = m2 % 2
            s.dma("scalar", [(u, "wsl", wb)], [(u, "wsl", wb)],
                  lambda e, m2=m2, wb=wb: e.dma_start(out=wsl[:, wb], in_=wdt[m2]),
                  tag=f"{u}lwdt{tt}_{m2}")
            def mmd(e, m2=m2, wb=wb):
                last = None
                for ki in range(16):
                    last = e.matmul(p.bank(6 + m2 % 2), wsl[:, wb, ki, :], xcfb[:, ki, :],
                                    start=(ki == 0), stop=(ki == 15))
                return last
            s.op("tensor", [(u, "wsl", wb)] + xck, [("psb", 6 + m2 % 2)], mmd,
                 tag=f"{u}dt{tt}_{m2}")
            s.op("scalar", [("psb", 6 + m2 % 2), (u, "bdtc")], [(u, "spt", 0)],
                 lambda e, m2=m2: e.activation(out=spt[:, 0, :], in_=p.bank(6 + m2 % 2),
                                               func=AF.Abs, bias=bdtc[:, m2:m2 + 1]),
                 tag=f"{u}sp1{tt}_{m2}")
            s.op("scalar", [(u, "spt", 0), "fvals"], [(u, "spt", 0)],
                 lambda e: e.activation(out=spt[:, 0, :], in_=spt[:, 0, :], func=AF.Exp,
                                        scale=p.fconst(-1.0)),
                 tag=f"{u}sp2{tt}_{m2}")
            s.op("scalar", [(u, "spt", 0), "fvals"], [(u, "spt", 0)],
                 lambda e: e.activation(out=spt[:, 0, :], in_=spt[:, 0, :], func=AF.Ln,
                                        bias=p.fconst(1.0)),
                 tag=f"{u}sp3{tt}_{m2}")
            s.op("scalar", [("psb", 6 + m2 % 2), (u, "bdtc")], [(u, "spt", 1)],
                 lambda e, m2=m2: e.activation(out=spt[:, 1, :], in_=p.bank(6 + m2 % 2),
                                               func=AF.Relu, bias=bdtc[:, m2:m2 + 1]),
                 tag=f"{u}sp4{tt}_{m2}")
            s.op("vector", [(u, "spt", 0), (u, "spt", 1)], [(u, "dstg")],
                 lambda e: e.tensor_add(dstg[...], spt[:, 0, :], spt[:, 1, :]),
                 tag=f"{u}sp5{tt}_{m2}")
            for j4 in range(4):
                tb = 2 + j4 % 2
                s.op("tensor", [(u, "dstg")], [("psb", tb)],
                     lambda e, j4=j4, tb=tb: e.transpose(
                         p.bank(tb, cols=128), dstg[:, j4 * 128:(j4 + 1) * 128],
                         p.buf["eye_f32"][...]),
                     tag=f"{u}dT{tt}_{m2}_{j4}")
                s.op("vector", [("psb", tb)], [(u, "dts", j4 % 2)],
                     lambda e, tb=tb, j4=j4: e.tensor_copy(out=dts[:, j4 % 2, :],
                                                           in_=p.bank(tb, cols=128)),
                     tag=f"{u}dTe{tt}_{m2}_{j4}")
                s.dma("gpsimd", [(u, "dts", j4 % 2)], [(u, "dtth", tt, m2, j4)],
                      lambda e, j4=j4, t0=t0, m2=m2: e.dma_start(
                          out=dtt_h[t0 + j4 * 128:t0 + (j4 + 1) * 128, m2 * 128:(m2 + 1) * 128],
                          in_=dts[:, j4 % 2, :]),
                      tag=f"{u}dTs{tt}_{m2}_{j4}")
        for nm, w_, dsth, dstf in (("b", wBb, btm_h, bfm_h), ("c", wCb, ctm_h, cfm_h)):
            for mt in range(4):
                sb2 = mt % 2
                def mmb(e, mt=mt, w_=w_):
                    last = None
                    for ki in range(16):
                        last = e.matmul(p.bank(4, cols=64), xcfb[:, ki, mt * 128:(mt + 1) * 128],
                                        w_[:, ki, :], start=(ki == 0), stop=(ki == 15))
                    return last
                s.op("tensor", xck + wk_bc, [("psb", 4)], mmb, tag=f"{u}{nm}tm{tt}_{mt}")
                s.op("vector", [("psb", 4)], [(u, "bcst", sb2)],
                     lambda e, sb2=sb2: e.tensor_copy(out=bcst[:, sb2, :], in_=p.bank(4, cols=64)),
                     tag=f"{u}{nm}te{tt}_{mt}")
                s.dma("sync", [(u, "bcst", sb2)], [(u, nm + "tmh", tt, mt)],
                      lambda e, mt=mt, t0=t0, dsth=dsth, sb2=sb2: e.dma_start(
                          out=dsth[t0 + mt * 128:t0 + (mt + 1) * 128, :], in_=bcst[:, sb2, :]),
                      tag=f"{u}{nm}ts{tt}_{mt}")
            def mmbf(e, w_=w_):
                last = None
                for ki in range(16):
                    last = e.matmul(p.bank(5, rows=64), w_[:, ki, :], xcfb[:, ki, :],
                                    start=(ki == 0), stop=(ki == 15))
                return last
            s.op("tensor", xck + wk_bc, [("psb", 5)], mmbf, tag=f"{u}{nm}fm{tt}")
            s.op("vector", [("psb", 5)], [(u, "bfst")],
                 lambda e: e.tensor_copy(out=bfst[...], in_=p.bank(5, rows=64)),
                 tag=f"{u}{nm}fe{tt}")
            s.dma("sync", [(u, "bfst")], [(u, nm + "fmh", tt)],
                  lambda e, t0=t0, dstf=dstf: e.dma_start(
                      out=dstf[:, t0:t0 + TT], in_=bfst[...]),
                  tag=f"{u}{nm}fs{tt}")
    s.barrier()
    for n in (u + "_bfst", u + "_bcst", u + "_wCb", u + "_wBb", u + "_dts", u + "_spt",
              u + "_dstg", u + "_wsl", u + "_xcfb"):
        p.free_buf(n)

    HI = 1024
    dtc = p.sbuf(u + "_dtc", (128, 1, HI), F32)
    xcc = p.sbuf(u + "_xcc", (128, HI), BF16)
    szc = p.sbuf(u + "_szc", (128, HI), BF16)
    dta = p.sbuf(u + "_dta", (128, HI), BF16)
    r_sb = p.sbuf(u + "_rsb", (128, HI), F32)
    tmpf = p.sbuf(u + "_tmpf", (128, HI), F32)
    e1m = p.sbuf(u + "_e1m", (128, HI), BF16)
    w1m = p.sbuf(u + "_w1m", (128, HI), BF16)
    gtm = p.sbuf(u + "_gtm", (128, 128), BF16)
    hst = p.sbuf(u + "_h", (64, 2, HI), F32)
    htmp = p.sbuf(u + "_htmp", (64, HI), F32)
    hA_bf = p.sbuf(u + "_hAb", (64, HI), BF16)
    hB_bf = p.sbuf(u + "_hBb", (64, HI), BF16)
    fA = p.sbuf(u + "_fA", (1, HI), BF16)
    fB = p.sbuf(u + "_fB", (1, HI), BF16)
    rmA = p.sbuf(u + "_rmA", (1, HI), BF16)
    rmB = p.sbuf(u + "_rmB", (1, HI), BF16)
    bpr = p.sbuf(u + "_bpr", (128, 64), BF16)
    cfpair = p.sbuf(u + "_cfpair", (64, 128), BF16)
    bfpair = p.sbuf(u + "_bfpair", (64, 128), BF16)
    yfm = p.sbuf(u + "_yfm", (128, 2, 8, 128), BF16)
    wos = p.sbuf(u + "_wos", (128, 8, 128), BF16)

    s.op("vector", [], [(u, "h")], lambda e: e.memset(hst[...], 0.0), tag=u + "h0")

    PS = p.psum["PS"]
    for pr in range(32):
        tp0 = pr * 128
        s.dma("sync", [(u, "bpr")], [(u, "bpr")],
              lambda e, tp0=tp0: e.dma_start(out=bpr[...], in_=btm_h[tp0:tp0 + 128, :]),
              tag=f"{u}lb{pr}")
        s.dma("sync", [(u, "cfpair")], [(u, "cfpair")],
              lambda e, tp0=tp0: e.dma_start(out=cfpair[...], in_=cfm_h[:, tp0:tp0 + 128]),
              tag=f"{u}lcf{pr}")
        s.dma("sync", [(u, "bfpair")], [(u, "bfpair")],
              lambda e, tp0=tp0: e.dma_start(out=bfpair[...], in_=bfm_h[:, tp0:tp0 + 128]),
              tag=f"{u}lbf{pr}")
        for hf in range(2):
            i0 = hf * HI
            hk = (u, "hf", pr, hf)
            s.dma("gpsimd", [(u, "dtc", 0)], [(u, "dtc", 0)],
                  lambda e, tp0=tp0, i0=i0: e.dma_start(
                      out=dtc[:, 0, :], in_=dtt_h[tp0:tp0 + 128, i0:i0 + HI]),
                  tag=f"{u}ldt{pr}_{hf}")
            s.dma("gpsimd", [(u, "xcc")], [(u, "xcc")],
                  lambda e, tp0=tp0, i0=i0: e.dma_start(
                      out=xcc[...], in_=xct_h[tp0:tp0 + 128, i0:i0 + HI]),
                  tag=f"{u}lxc{pr}_{hf}")
            s.dma("gpsimd", [(u, "szc")], [(u, "szc")],
                  lambda e, tp0=tp0, i0=i0: e.dma_start(
                      out=szc[...], in_=zs_h[tp0:tp0 + 128, i0:i0 + HI]),
                  tag=f"{u}lsz{pr}_{hf}")
            def mmr(e, hf=hf):
                e.matmul(p.bank(0), TriBD[...], dtc[:, 0, 0:512], start=True, stop=True,
                         skip_group_check=True)
                return e.matmul(p.bank(1), TriBD[...], dtc[:, 0, 512:HI], start=True,
                                stop=True, skip_group_check=True)
            s.op("tensor", [(u, "dtc", 0), "l2a_tri"], [("psb", 0), ("psb", 1)], mmr,
                 tag=f"{u}r{pr}_{hf}")
            s.op("scalar", [("psb", 0)], [(u, "rsb")],
                 lambda e: e.copy(out=r_sb[:, 0:512], in_=p.bank(0)), tag=f"{u}re{pr}_{hf}")
            s.op("scalar", [("psb", 1), (u, "rsb")], [(u, "rsb")],
                 lambda e: e.copy(out=r_sb[:, 512:HI], in_=p.bank(1)), tag=f"{u}re2{pr}_{hf}")
            def mmsums(e, hf=hf):
                last = None
                for q, b0 in ((0, 2), (1, 4), (2, 6), (3, 0)):
                    e.matmul(PS[0:1, b0 * 512:(b0 + 1) * 512], IND[:, q:q + 1],
                             dtc[:, 0, 0:512], start=True, stop=True, skip_group_check=True)
                    last = e.matmul(PS[0:1, (b0 + 1) * 512:(b0 + 2) * 512], IND[:, q:q + 1],
                                    dtc[:, 0, 512:HI], start=True, stop=True,
                                    skip_group_check=True)
                return last
            s.op("tensor", [(u, "dtc", 0), "l2a_ind"],
                 [("psb", b) for b in range(8)], mmsums, tag=f"{u}sums{pr}_{hf}")
            s.op("vector", [("psb", 0), ("psb", 1)], [(u, "rmA")],
                 lambda e: e.tensor_copy(out=rmA[...], in_=PS[0:1, 0:1024]),
                 tag=f"{u}rmA{pr}_{hf}")
            s.op("vector", [(u, "rmA"), ("psb", 2), ("psb", 3)], [(u, "fA")],
                 lambda e: e.tensor_sub(fA[...], rmA[...], PS[0:1, 1024:2048]),
                 tag=f"{u}fA{pr}_{hf}")
            s.op("scalar", [(u, "fA")], [(u, "fA")],
                 lambda e: e.activation(out=fA[...], in_=fA[...], func=AF.Exp),
                 tag=f"{u}fAe{pr}_{hf}")
            s.op("vector", [("psb", 6), ("psb", 7)], [(u, "rmB")],
                 lambda e: e.tensor_copy(out=rmB[...], in_=PS[0:1, 3072:4096]),
                 tag=f"{u}rmB{pr}_{hf}")
            s.op("vector", [(u, "rmB"), ("psb", 4), ("psb", 5)], [(u, "fB")],
                 lambda e: e.tensor_sub(fB[...], rmB[...], PS[0:1, 2048:3072]),
                 tag=f"{u}fB{pr}_{hf}")
            s.op("scalar", [(u, "fB")], [(u, "fB")],
                 lambda e: e.activation(out=fB[...], in_=fB[...], func=AF.Exp),
                 tag=f"{u}fBe{pr}_{hf}")
            def mmshift(e):
                e.matmul(PS[0:64, 1024:1536], p.buf["onerow_bf"][:, :64], rmA[:, 0:512],
                         start=True, stop=True, skip_group_check=True)
                e.matmul(PS[0:64, 1536:2048], p.buf["onerow_bf"][:, :64], rmA[:, 512:HI],
                         start=True, stop=True, skip_group_check=True)
                e.matmul(PS[64:128, 1024:1536], p.buf["onerow_bf"][:, :64], rmB[:, 0:512],
                         start=True, stop=True, skip_group_check=True)
                return e.matmul(PS[64:128, 1536:2048], p.buf["onerow_bf"][:, :64], rmB[:, 512:HI],
                                start=True, stop=True, skip_group_check=True)
            s.op("tensor", [(u, "rmA"), (u, "rmB")], [("psb", 2), ("psb", 3)], mmshift,
                 tag=f"{u}sh{pr}_{hf}")
            s.op("vector", [(u, "rsb"), ("psb", 2), ("psb", 3)], [(u, "tmpf")],
                 lambda e: e.tensor_sub(tmpf[...], r_sb[...], PS[:, 1024:2048]),
                 tag=f"{u}e1s{pr}_{hf}")
            s.op("scalar", [(u, "tmpf")], [(u, "e1m")],
                 lambda e: e.activation(out=e1m[...], in_=tmpf[...], func=AF.Exp),
                 tag=f"{u}e1e{pr}_{hf}")

            s.op("vector", [(u, "dtc", 0), (u, "xcc")], [(u, "dta")],
                 lambda e: e.tensor_mul(dta[...], dtc[:, 0, :], xcc[...]),
                 tag=f"{u}dta{pr}_{hf}")
            s.op("vector", [(u, "dta"), (u, "e1m")], [(u, "w1m")],
                 lambda e: e.tensor_mul(w1m[...], e1m[...], dta[...]), tag=f"{u}w1{pr}_{hf}")
            s.op("tensor", [(u, "bfpair"), (u, "cfpair")], [("psb", 4)],
                 lambda e: e.matmul(p.bank(4, cols=128), bfpair[...], cfpair[...],
                                    start=True, stop=True),
                 tag=f"{u}gt{pr}_{hf}")
            s.op("vector", [("psb", 4), "l2a_ubd"], [(u, "gtm")],
                 lambda e: e.tensor_mul(gtm[...], p.bank(4, cols=128), UBD[...]),
                 tag=f"{u}gtme{pr}_{hf}")
            def mmqA(e):
                e.matmul(PS[0:64, 5 * 512:6 * 512], p.buf["onerow_bf"][:, :64], rmA[:, 0:512],
                         start=True, stop=True, skip_group_check=True)
                return e.matmul(PS[0:64, 6 * 512:7 * 512], p.buf["onerow_bf"][:, :64], rmA[:, 512:HI],
                                start=True, stop=True, skip_group_check=True)
            s.op("tensor", [(u, "rmA")], [("psb", 5), ("psb", 6)], mmqA, tag=f"{u}qA{pr}_{hf}")
            s.op("scalar", [("psb", 5), ("psb", 6)], [("psb", 2), ("psb", 3)],
                 lambda e: e.activation(out=PS[0:64, 1024:2048], in_=PS[0:64, 2560:3584],
                                        func=AF.Exp, scale=p.fconst(-1.0)[0:64, :]),
                 tag=f"{u}qAe{pr}_{hf}")
            s.op("vector", [("psb", 2), ("psb", 3), (u, "h"), "fvals"], [(u, "htmp")],
                 lambda e, hf=hf: e.tensor_mul(htmp[...], hst[:, hf, :], PS[0:64, 1024:2048]),
                 tag=f"{u}hA{pr}_{hf}")
            s.op("vector", [(u, "htmp")], [(u, "hAb")],
                 lambda e: e.tensor_copy(out=hA_bf[...], in_=htmp[...]), tag=f"{u}hAb{pr}_{hf}")
            def mmPA(e):
                e.matmul(PS[0:64, 5 * 512:6 * 512], bpr[0:64, :], w1m[0:64, 0:512],
                         start=True, stop=True, skip_group_check=True)
                return e.matmul(PS[0:64, 6 * 512:7 * 512], bpr[0:64, :], w1m[0:64, 512:HI],
                                start=True, stop=True, skip_group_check=True)
            s.op("tensor", [(u, "bpr"), (u, "w1m")], [("psb", 5), ("psb", 6)], mmPA,
                 tag=f"{u}PA{pr}_{hf}")
            s.op("vector", [("psb", 5), ("psb", 6), (u, "htmp")], [(u, "htmp")],
                 lambda e: e.tensor_add(htmp[...], htmp[...], PS[0:64, 2560:3584]),
                 tag=f"{u}hm1{pr}_{hf}")
            def mmfA(e):
                e.matmul(PS[0:64, 5 * 512:6 * 512], p.buf["onerow_bf"][:, :64], fA[:, 0:512],
                         start=True, stop=True, skip_group_check=True)
                return e.matmul(PS[0:64, 6 * 512:7 * 512], p.buf["onerow_bf"][:, :64], fA[:, 512:HI],
                                start=True, stop=True, skip_group_check=True)
            s.op("tensor", [(u, "fA")], [("psb", 5), ("psb", 6)], mmfA, tag=f"{u}fAb{pr}_{hf}")
            s.op("vector", [("psb", 5), ("psb", 6), (u, "htmp")], [(u, "htmp")],
                 lambda e: e.tensor_mul(htmp[...], htmp[...], PS[0:64, 2560:3584]),
                 tag=f"{u}hm2{pr}_{hf}")
            def mmqB(e):
                e.matmul(PS[0:64, 5 * 512:6 * 512], p.buf["onerow_bf"][:, :64], rmB[:, 0:512],
                         start=True, stop=True, skip_group_check=True)
                return e.matmul(PS[0:64, 6 * 512:7 * 512], p.buf["onerow_bf"][:, :64], rmB[:, 512:HI],
                                start=True, stop=True, skip_group_check=True)
            s.op("tensor", [(u, "rmB")], [("psb", 5), ("psb", 6)], mmqB, tag=f"{u}qB{pr}_{hf}")
            s.op("scalar", [("psb", 5), ("psb", 6)], [("psb", 2), ("psb", 3)],
                 lambda e: e.activation(out=PS[0:64, 1024:2048], in_=PS[0:64, 2560:3584],
                                        func=AF.Exp, scale=p.fconst(-1.0)[0:64, :]),
                 tag=f"{u}qBe{pr}_{hf}")
            s.op("vector", [("psb", 2), ("psb", 3), (u, "htmp")], [(u, "htmp")],
                 lambda e: e.tensor_mul(htmp[...], htmp[...], PS[0:64, 1024:2048]),
                 tag=f"{u}hB{pr}_{hf}")
            s.op("vector", [(u, "htmp")], [(u, "hBb")],
                 lambda e: e.tensor_copy(out=hB_bf[...], in_=htmp[...]), tag=f"{u}hBb{pr}_{hf}")
            def mmPB(e):
                e.matmul(PS[0:64, 5 * 512:6 * 512], bpr[64:128, :], w1m[64:128, 0:512],
                         start=True, stop=True, skip_group_check=True)
                return e.matmul(PS[0:64, 6 * 512:7 * 512], bpr[64:128, :], w1m[64:128, 512:HI],
                                start=True, stop=True, skip_group_check=True)
            s.op("tensor", [(u, "bpr"), (u, "w1m")], [("psb", 5), ("psb", 6)], mmPB,
                 tag=f"{u}PB{pr}_{hf}")
            s.op("vector", [("psb", 5), ("psb", 6), (u, "htmp")], [(u, "htmp")],
                 lambda e: e.tensor_add(htmp[...], htmp[...], PS[0:64, 2560:3584]),
                 tag=f"{u}hn1{pr}_{hf}")
            def mmfB(e):
                e.matmul(PS[0:64, 5 * 512:6 * 512], p.buf["onerow_bf"][:, :64], fB[:, 0:512],
                         start=True, stop=True, skip_group_check=True)
                return e.matmul(PS[0:64, 6 * 512:7 * 512], p.buf["onerow_bf"][:, :64], fB[:, 512:HI],
                                start=True, stop=True, skip_group_check=True)
            s.op("tensor", [(u, "fB")], [("psb", 5), ("psb", 6)], mmfB, tag=f"{u}fBb{pr}_{hf}")
            s.op("vector", [("psb", 5), ("psb", 6), (u, "htmp")], [(u, "h")],
                 lambda e, hf=hf: e.tensor_mul(hst[:, hf, :], htmp[...], PS[0:64, 2560:3584]),
                 tag=f"{u}hn2{pr}_{hf}")
            def mmy(e):
                e.matmul(p.bank(0), gtm[...], w1m[:, 0:512], start=True, stop=False,
                         skip_group_check=True)
                e.matmul(p.bank(1), gtm[...], w1m[:, 512:HI], start=True, stop=False,
                         skip_group_check=True)
                e.matmul(PS[0:64, 0:512], cfpair[:, 0:64], hA_bf[:, 0:512], start=False,
                         stop=False, skip_group_check=True)
                e.matmul(PS[0:64, 512:1024], cfpair[:, 0:64], hA_bf[:, 512:HI], start=False,
                         stop=False, skip_group_check=True)
                e.matmul(PS[64:128, 0:512], cfpair[:, 64:128], hB_bf[:, 0:512], start=False,
                         stop=True, skip_group_check=True)
                return e.matmul(PS[64:128, 512:1024], cfpair[:, 64:128], hB_bf[:, 512:HI],
                                start=False, stop=True, skip_group_check=True)
            s.op("tensor", [(u, "gtm"), (u, "w1m"), (u, "cfpair"), (u, "hAb"), (u, "hBb")],
                 [("psb", 0), ("psb", 1)], mmy, tag=f"{u}y{pr}_{hf}")
            s.op("vector", [(u, "e1m"), (u, "w1m")], [(u, "tmpf")],
                 lambda e: e.reciprocal(tmpf[...], e1m[...]), tag=f"{u}e2{pr}_{hf}")
            s.op("vector", [("psb", 0), ("psb", 1), (u, "tmpf")], [(u, "e1m")],
                 lambda e: e.tensor_mul(e1m[...], PS[:, 0:HI], tmpf[...]),
                 tag=f"{u}ye{pr}_{hf}")
            s.op("vector", [(u, "xcc"), (u, "DBt")], [(u, "dta")],
                 lambda e, i0=i0: e.tensor_mul(dta[...], xcc[...], DBt[:, i0:i0 + HI]),
                 tag=f"{u}yd{pr}_{hf}")
            s.op("vector", [(u, "e1m"), (u, "dta")], [(u, "e1m")],
                 lambda e: e.tensor_add(e1m[...], e1m[...], dta[...]), tag=f"{u}ya{pr}_{hf}")
            s.op("vector", [(u, "e1m"), (u, "szc")], [(u, "e1m")],
                 lambda e: e.tensor_mul(e1m[...], e1m[...], szc[...]), tag=f"{u}yg{pr}_{hf}")
            for j8 in range(8):
                tb = 2 + j8 % 2
                s.op("tensor", [(u, "e1m")], [("psb", tb)],
                     lambda e, j8=j8, tb=tb: e.transpose(
                         p.bank(tb).bitcast(BF16)[:, :128], e1m[:, j8 * 128:(j8 + 1) * 128],
                         p.buf["eye_bf"][...]),
                     tag=f"{u}yT{pr}_{hf}_{j8}")
                s.op("vector", [("psb", tb)], [(u, "yfm", hf)],
                     lambda e, j8=j8, tb=tb, hf=hf: e.tensor_copy(
                         out=yfm[:, hf, j8, :], in_=p.bank(tb).bitcast(BF16)[:, :128]),
                     tag=f"{u}yTe{pr}_{hf}_{j8}")
        for m in range(NG):
            for hf2 in range(2):
                s.dma("scalar", [(u, "wos")], [(u, "wos")],
                      lambda e, m=m, hf2=hf2: e.dma_start(
                          out=wos[...], in_=wout[m, :, hf2 * 8:(hf2 + 1) * 8, :]),
                      tag=f"{u}lwo{pr}_{m}_{hf2}")
                def mmo(e, m=m, hf2=hf2):
                    last = None
                    for j8 in range(8):
                        last = e.matmul(p.bank(7, cols=128), wos[:, j8, :],
                                        yfm[:, hf2, j8, :], start=(hf2 == 0 and j8 == 0),
                                        stop=(hf2 == 1 and j8 == 7))
                    return last
                s.op("tensor", [(u, "wos"), (u, "yfm", hf2)], [("psb", 7)], mmo,
                     tag=f"{u}mo{pr}_{m}_{hf2}")
            s.op("vector", [("psb", 7)], [("x", m)],
                 lambda e, m=m, tp0=tp0: e.tensor_add(
                     x[:, m, tp0:tp0 + 128], x[:, m, tp0:tp0 + 128], p.bank(7, cols=128)),
                 tag=f"{u}xa{pr}_{m}")


def _tile1024(w):
    return np.ascontiguousarray(np.asarray(w).reshape(8, 128, 8, 128).transpose(2, 1, 0, 3))


def emit_l1attn(p, inp):
    nc, s = p.nc, p.s
    x = p.buf["x"]
    u = "l1a"
    N16 = 16
    w1t = emit_to_bf16(p, "l1a_w1", _tile1024(inp["L1_attn_w1_w"]), (8, 128, 8, 128))
    v1t = emit_to_bf16(p, "l1a_v1", _tile1024(inp["L1_attn_v1_w"]), (8, 128, 8, 128))
    wdtt = emit_to_bf16(p, "l1a_wdt", _tile1024(inp["L1_attn_W_dt"]), (8, 128, 8, 128))
    w2t = emit_to_bf16(p, "l1a_w2", _tile1024(inp["L1_attn_w2_w"]), (8, 128, 8, 128))
    wBt = emit_to_bf16(p, "l1a_wB", np.ascontiguousarray(np.asarray(inp["L1_attn_W_B"])))
    wCt = emit_to_bf16(p, "l1a_wC", np.ascontiguousarray(np.asarray(inp["L1_attn_W_C"])))
    def cols(v):
        return np.asarray(v).reshape(8, 128).T
    colh = p.input("l1a_cols", np.ascontiguousarray(np.stack(
        [cols(inp["L1_attn_ln_g"]), cols(inp["L1_attn_ln_b"]), cols(inp["L1_attn_w1_b"]),
         cols(inp["L1_attn_v1_b"]), cols(inp["L1_attn_b_dt"]), cols(inp["L1_attn_w2_b"]),
         cols(inp["L1_attn_D_skip"]),
         cols(np.asarray(inp["L1_attn_conv_w"])[:, 0, 0]),
         cols(np.asarray(inp["L1_attn_conv_w"])[:, 0, 1]),
         cols(np.asarray(inp["L1_attn_conv_w"])[:, 0, 2])], axis=1)))
    alog = np.asarray(inp["L1_attn_A_log"])
    pidx = np.arange(128)
    acl = np.stack([alog[db * 8 + pidx // 16, pidx % 16] for db in range(128)], axis=1)
    aclh = p.input("l1a_acl", np.ascontiguousarray(acl.astype(np.float32)))
    _free_cvt(p, u)
    colb = p.sbuf(u + "_colb", (128, 10, 8), F32)
    acol = p.sbuf(u + "_acol", (128, 128), F32)
    s.dma("sync", [], [(u, "colb")], lambda e: e.dma_start(out=colb[...], in_=colh[...]),
          tag=u + "ldcol")
    s.dma("sync", [], [(u, "acol")], lambda e: e.dma_start(out=acol[...], in_=aclh[...]),
          tag=u + "ldacl")
    s.op("scalar", [(u, "acol")], [(u, "acol")],
         lambda e: e.activation(out=acol[...], in_=acol[...], func=AF.Exp), tag=u + "aex")
    s.op("vector", [(u, "acol")], [(u, "acol")],
         lambda e: e.tensor_scalar_mul(acol[...], acol[...], -1.0), tag=u + "aneg")

    x1_h = p.scratch(u + "_x1", (D, T), BF16)
    u_h = p.scratch(u + "_u", (D, T), BF16)
    v_h = p.scratch(u + "_v", (D, T), BF16)
    dt_h = p.scratch(u + "_dt", (D, T), F32)
    dtu_h = p.scratch(u + "_dtu", (D, T), F32)
    bc_h = p.scratch(u + "_bc", (32, T), BF16)
    y_h = p.scratch(u + "_y", (D, T), BF16)

    xnt = p.sbuf(u + "_xnt", (128, NG, TT), BF16)
    wsl = p.sbuf(u + "_wsl", (128, 2, NG, 128), BF16)
    stg = p.sbuf(u + "_stg", (128, 2, TT), BF16)
    for tt in range(NTT):
        t0 = tt * TT
        emit_ln_tile(p, u, tt)
        for g in range(NG):
            emit_ln_apply(p, u, tt, g, xnt[:, g, :], (u, "xnt0", g))
            s.op("vector", [(u, "xnt0", g), (u, "colb")], [(u, "xnt", g)],
                 lambda e, g=g: e.tensor_scalar(out=xnt[:, g, :], in0=xnt[:, g, :],
                                                scalar1=colb[:, 0, g:g + 1],
                                                scalar2=colb[:, 1, g:g + 1],
                                                op0=ALU.mult, op1=ALU.add),
                 tag=f"{u}gb{tt}_{g}")
        xk = [(u, "xnt", g) for g in range(NG)]
        for wt, nm, dsth, act, bcol in ((w1t, "x1", x1_h, None, 2), (v1t, "v", v_h, AF.Silu, 3)):
            for m in range(NG):
                wb = m % 2
                s.dma("scalar", [(u, "wsl", wb)], [(u, "wsl", wb)],
                      lambda e, m=m, wb=wb, wt=wt: e.dma_start(out=wsl[:, wb], in_=wt[m]),
                      tag=f"{u}lw{nm}{tt}_{m}")
                def mm1(e, m=m, wb=wb):
                    last = None
                    for kg in range(NG):
                        last = e.matmul(p.bank(m % 4), wsl[:, wb, kg, :], xnt[:, kg, :],
                                        start=(kg == 0), stop=(kg == NG - 1))
                    return last
                s.op("tensor", [(u, "wsl", wb)] + xk, [("psb", m % 4)], mm1,
                     tag=f"{u}m{nm}{tt}_{m}")
                sb = m % 2
                if act is None:
                    s.op("vector", [("psb", m % 4), (u, "colb")], [(u, "stg", sb)],
                         lambda e, m=m, sb=sb, bcol=bcol: e.tensor_scalar_add(
                             stg[:, sb, :], p.bank(m % 4), colb[:, bcol, m:m + 1]),
                         tag=f"{u}e{nm}{tt}_{m}")
                else:
                    s.op("scalar", [("psb", m % 4), (u, "colb")], [(u, "stg", sb)],
                         lambda e, m=m, sb=sb, act=act, bcol=bcol: e.activation(
                             out=stg[:, sb, :], in_=p.bank(m % 4),
                             func=act, bias=colb[:, bcol, m:m + 1]),
                         tag=f"{u}e{nm}{tt}_{m}")
                s.dma("gpsimd", [(u, "stg", sb)], [(u, nm + "h", m)],
                      lambda e, m=m, sb=sb, t0=t0, dsth=dsth: e.dma_start(
                          out=dsth[m * 128:(m + 1) * 128, t0:t0 + TT], in_=stg[:, sb, :]),
                      tag=f"{u}s{nm}{tt}_{m}")
    s.barrier()
    for n in (u + "_stg", u + "_wsl", u + "_xnt"):
        p.free_buf(n)

    xint = p.sbuf(u + "_xint", (128, TT + 8), BF16)
    cacc = p.sbuf(u + "_cacc", (128, TT), F32)
    ust = p.sbuf(u + "_ust", (128, 2, TT), BF16)
    for g in range(NG):
        for tt in range(NTT):
            t0 = tt * TT
            if tt == 0:
                s.op("vector", [(u, "xint")], [(u, "xint")],
                     lambda e: e.memset(xint[:, 0:1], 0.0), tag=f"{u}cz{g}")
                s.dma("sync", [(u, "xint")], [(u, "xint")],
                      lambda e, g=g: e.dma_start(out=xint[:, 1:2 + TT],
                                                 in_=x1_h[g * 128:(g + 1) * 128, 0:TT + 1]),
                      tag=f"{u}cl{g}_{tt}")
            elif tt == NTT - 1:
                s.op("vector", [(u, "xint")], [(u, "xint")],
                     lambda e: e.memset(xint[:, 1 + TT:2 + TT], 0.0), tag=f"{u}cz2{g}")
                s.dma("sync", [(u, "xint")], [(u, "xint")],
                      lambda e, g=g, t0=t0: e.dma_start(
                          out=xint[:, 0:1 + TT], in_=x1_h[g * 128:(g + 1) * 128, t0 - 1:t0 + TT]),
                      tag=f"{u}cl{g}_{tt}")
            else:
                s.dma("sync", [(u, "xint")], [(u, "xint")],
                      lambda e, g=g, t0=t0: e.dma_start(
                          out=xint[:, 0:2 + TT], in_=x1_h[g * 128:(g + 1) * 128, t0 - 1:t0 + TT + 1]),
                      tag=f"{u}cl{g}_{tt}")
            s.op("vector", [(u, "xint"), (u, "colb")], [(u, "cacc")],
                 lambda e, g=g: e.tensor_scalar_mul(cacc[...], xint[:, 0:TT], colb[:, 7, g:g + 1]),
                 tag=f"{u}c0{g}_{tt}")
            for j in (1, 2):
                s.op("vector", [(u, "xint"), (u, "colb"), (u, "cacc")], [(u, "cacc")],
                     lambda e, g=g, j=j: e.scalar_tensor_tensor(
                         out=cacc[...], in0=xint[:, j:j + TT], scalar=colb[:, 7 + j, g:g + 1],
                         in1=cacc[...], op0=ALU.mult, op1=ALU.add),
                     tag=f"{u}c{j}_{g}_{tt}")
            sb = tt % 2
            s.op("scalar", [(u, "cacc")], [(u, "ust", sb)],
                 lambda e, sb=sb: e.activation(out=ust[:, sb, :], in_=cacc[...], func=AF.Silu),
                 tag=f"{u}cs{g}_{tt}")
            s.dma("gpsimd", [(u, "ust", sb)], [(u, "uh", g, tt)],
                  lambda e, g=g, sb=sb, t0=t0: e.dma_start(
                      out=u_h[g * 128:(g + 1) * 128, t0:t0 + TT], in_=ust[:, sb, :]),
                  tag=f"{u}cu{g}_{tt}")
    s.barrier()
    for n in (u + "_ust", u + "_cacc", u + "_xint"):
        p.free_buf(n)

    ub = p.sbuf(u + "_ub", (128, NG, TT), BF16)
    wsl2 = p.sbuf(u + "_wsl2", (128, 2, NG, 128), BF16)
    wBs = p.sbuf(u + "_wBs", (128, NG, 32), BF16)
    spt = p.sbuf(u + "_spt", (128, 2, TT), F32)
    dstg = p.sbuf(u + "_dstg", (128, TT), F32)
    dustg = p.sbuf(u + "_dustg", (128, TT), F32)
    bcs = p.sbuf(u + "_bcs", (32, TT), BF16)
    for kg in range(NG):
        s.dma("sync", [], [(u, "wBs", kg)],
              lambda e, kg=kg: e.dma_start(out=wBs[:, kg, 0:16], in_=wBt[kg * 128:(kg + 1) * 128, :]),
              tag=f"{u}lwB{kg}")
        s.dma("sync", [(u, "wBs", kg)], [(u, "wBs", kg)],
              lambda e, kg=kg: e.dma_start(out=wBs[:, kg, 16:32], in_=wCt[kg * 128:(kg + 1) * 128, :]),
              tag=f"{u}lwC{kg}")
    for tt in range(NTT):
        t0 = tt * TT
        for kg in range(NG):
            s.dma("gpsimd", [(u, "ub", kg)], [(u, "ub", kg)],
                  lambda e, kg=kg, t0=t0: e.dma_start(
                      out=ub[:, kg, :], in_=u_h[kg * 128:(kg + 1) * 128, t0:t0 + TT]),
                  tag=f"{u}du{tt}_{kg}")
        uk = [(u, "ub", kg) for kg in range(NG)]
        for m in range(NG):
            wb = m % 2
            s.dma("scalar", [(u, "wsl2", wb)], [(u, "wsl2", wb)],
                  lambda e, m=m, wb=wb: e.dma_start(out=wsl2[:, wb], in_=wdtt[m]),
                  tag=f"{u}lwd{tt}_{m}")
            def mmd(e, m=m, wb=wb):
                last = None
                for kg in range(NG):
                    last = e.matmul(p.bank(6 + m % 2), wsl2[:, wb, kg, :], ub[:, kg, :],
                                    start=(kg == 0), stop=(kg == NG - 1))
                return last
            s.op("tensor", [(u, "wsl2", wb)] + uk, [("psb", 6 + m % 2)], mmd,
                 tag=f"{u}dt{tt}_{m}")
            s.op("scalar", [("psb", 6 + m % 2), (u, "colb")], [(u, "spt", 0)],
                 lambda e, m=m: e.activation(out=spt[:, 0, :], in_=p.bank(6 + m % 2),
                                             func=AF.Abs, bias=colb[:, 4, m:m + 1]),
                 tag=f"{u}sp1{tt}_{m}")
            s.op("scalar", [(u, "spt", 0), "fvals"], [(u, "spt", 0)],
                 lambda e: e.activation(out=spt[:, 0, :], in_=spt[:, 0, :], func=AF.Exp,
                                        scale=p.fconst(-1.0)),
                 tag=f"{u}sp2{tt}_{m}")
            s.op("scalar", [(u, "spt", 0), "fvals"], [(u, "spt", 0)],
                 lambda e: e.activation(out=spt[:, 0, :], in_=spt[:, 0, :], func=AF.Ln,
                                        bias=p.fconst(1.0)),
                 tag=f"{u}sp3{tt}_{m}")
            s.op("scalar", [("psb", 6 + m % 2), (u, "colb")], [(u, "spt", 1)],
                 lambda e, m=m: e.activation(out=spt[:, 1, :], in_=p.bank(6 + m % 2),
                                             func=AF.Relu, bias=colb[:, 4, m:m + 1]),
                 tag=f"{u}sp4{tt}_{m}")
            s.op("vector", [(u, "spt", 0), (u, "spt", 1)], [(u, "dstg")],
                 lambda e: e.tensor_add(dstg[...], spt[:, 0, :], spt[:, 1, :]),
                 tag=f"{u}sp5{tt}_{m}")
            s.dma("gpsimd", [(u, "dstg")], [(u, "dth", tt, m)],
                  lambda e, m=m, t0=t0: e.dma_start(
                      out=dt_h[m * 128:(m + 1) * 128, t0:t0 + TT], in_=dstg[...]),
                  tag=f"{u}dts{tt}_{m}")
            s.op("vector", [(u, "dstg"), (u, "ub", m)], [(u, "dustg")],
                 lambda e, m=m: e.tensor_mul(dustg[...], dstg[...], ub[:, m, :]),
                 tag=f"{u}dtu{tt}_{m}")
            s.dma("sync", [(u, "dustg")], [(u, "dtuh", tt, m)],
                  lambda e, m=m, t0=t0: e.dma_start(
                      out=dtu_h[m * 128:(m + 1) * 128, t0:t0 + TT], in_=dustg[...]),
                  tag=f"{u}dtus{tt}_{m}")
        def mmbc(e):
            last = None
            for kg in range(NG):
                last = e.matmul(p.bank(5, rows=32), wBs[:, kg, :], ub[:, kg, :],
                                start=(kg == 0), stop=(kg == NG - 1))
            return last
        s.op("tensor", uk + [(u, "wBs", kg) for kg in range(NG)], [("psb", 5)], mmbc,
             tag=f"{u}bc{tt}")
        s.op("vector", [("psb", 5)], [(u, "bcs")],
             lambda e: e.tensor_copy(out=bcs[...], in_=p.bank(5, rows=32)), tag=f"{u}bce{tt}")
        s.dma("sync", [(u, "bcs")], [(u, "bch", tt)],
              lambda e, t0=t0: e.dma_start(out=bc_h[:, t0:t0 + TT], in_=bcs[...]),
              tag=f"{u}bcs{tt}")
    s.barrier()
    for n in (u + "_bcs", u + "_dustg", u + "_dstg", u + "_spt", u + "_wBs", u + "_wsl2",
              u + "_ub"):
        p.free_buf(n)

    ind8 = np.zeros((128, 8), BF)
    for dd in range(8):
        ind8[dd * 16:(dd + 1) * 16, dd] = 1.0
    IND8 = emit_const(p, "l1a_ind8", ind8, scope="phase")
    dtr = p.sbuf(u + "_dtr", (128, T), F32)
    dur = p.sbuf(u + "_dur", (128, T), F32)
    bcr = p.sbuf(u + "_bcr", (128, T), BF16)
    ccr = p.sbuf(u + "_ccr", (128, T), BF16)
    hbf = p.sbuf(u + "_hbf", (128, T), BF16)
    yst = p.sbuf(u + "_yst", (8, 4, TT), BF16)
    brep = bc_h[0:16, :][None, :, :].broadcast_to((8, 16, T))
    crep = bc_h[16:32, :][None, :, :].broadcast_to((8, 16, T))
    for db in range(128):
        d0 = db * 8
        s.dma("sync", [(u, "dtr")], [(u, "dtr")],
              lambda e, d0=d0: e.dma_start(
                  out=dtr[...],
                  in_=dt_h[d0:d0 + 8, :][:, None, :].broadcast_to((8, 16, T))),
              tag=f"{u}ldtr{db}")
        s.dma("gpsimd", [(u, "dur")], [(u, "dur")],
              lambda e, d0=d0: e.dma_start(
                  out=dur[...],
                  in_=dtu_h[d0:d0 + 8, :][:, None, :].broadcast_to((8, 16, T))),
              tag=f"{u}ldur{db}")
        if db == 0:
            s.dma("scalar", [(u, "bcr")], [(u, "bcr")],
                  lambda e: e.dma_start(out=bcr[...], in_=brep),
                  tag=f"{u}lbcr{db}")
            s.dma("scalar", [(u, "ccr")], [(u, "ccr")],
                  lambda e: e.dma_start(out=ccr[...], in_=crep),
                  tag=f"{u}lccr{db}")
        s.op("scalar", [(u, "dtr"), (u, "acol")], [(u, "dtr")],
             lambda e, db=db: e.activation(out=dtr[...], in_=dtr[...], func=AF.Exp,
                                           scale=acol[:, db:db + 1]),
             tag=f"{u}a{db}")
        s.op("vector", [(u, "dur"), (u, "bcr")], [(u, "dur")],
             lambda e: e.tensor_mul(dur[...], dur[...], bcr[...]), tag=f"{u}b{db}")
        s.op("vector", [(u, "dtr"), (u, "dur")], [(u, "hbf")],
             lambda e: e.tensor_tensor_scan(out=hbf[...], data0=dtr[...], data1=dur[...],
                                            initial=0.0, op0=ALU.mult, op1=ALU.add),
             tag=f"{u}scan{db}")
        s.op("vector", [(u, "hbf"), (u, "ccr")], [(u, "hbf")],
             lambda e: e.tensor_mul(hbf[...], hbf[...], ccr[...]), tag=f"{u}hc{db}")
        def mmy(e, db=db):
            last = None
            for sp in range(NTT):
                last = e.matmul(p.psum["PS"][0:8, (sp % 8) * 512:(sp % 8) * 512 + 512],
                                IND8[...], hbf[:, sp * 512:(sp + 1) * 512],
                                start=True, stop=True, skip_group_check=True)
            return last
        s.op("tensor", [(u, "hbf"), "l1a_ind8"], [("psb", b) for b in range(8)], mmy,
             tag=f"{u}ymm{db}")
        for hv in range(2):
            s.op("vector", [("psb", b) for b in range(4 * hv, 4 * hv + 4)], [(u, "yst")],
                 lambda e, hv=hv: e.tensor_copy(out=yst.rearrange("a b t -> a (b t)"),
                                                in_=p.psum["PS"][0:8, hv * 2048:(hv + 1) * 2048]),
                 tag=f"{u}ye{db}_{hv}")
            s.dma("gpsimd", [(u, "yst")], [(u, "yh", db, hv)],
                  lambda e, d0=d0, hv=hv: e.dma_start(
                      out=y_h[d0:d0 + 8, hv * 2048:(hv + 1) * 2048].rearrange(
                          "d (b t) -> d b t", t=TT),
                      in_=yst[...]),
                  tag=f"{u}ys{db}_{hv}")
    s.barrier()
    for n in (u + "_yst", u + "_hbf", u + "_ccr", u + "_bcr", u + "_dur",
              u + "_dtr"):
        p.free_buf(n)

    yb = p.sbuf(u + "_yb", (128, NG, TT), BF16)
    ub2 = p.sbuf(u + "_ub2", (128, TT), BF16)
    vb2 = p.sbuf(u + "_vb2", (128, TT), BF16)
    wsl3 = p.sbuf(u + "_wsl3", (128, 2, NG, 128), BF16)
    for tt in range(NTT):
        t0 = tt * TT
        for g in range(NG):
            s.dma("sync", [(u, "yb", g)], [(u, "yb", g)],
                  lambda e, g=g, t0=t0: e.dma_start(
                      out=yb[:, g, :], in_=y_h[g * 128:(g + 1) * 128, t0:t0 + TT]),
                  tag=f"{u}fy{tt}_{g}")
            s.dma("gpsimd", [(u, "ub2")], [(u, "ub2")],
                  lambda e, g=g, t0=t0: e.dma_start(
                      out=ub2[...], in_=u_h[g * 128:(g + 1) * 128, t0:t0 + TT]),
                  tag=f"{u}fu{tt}_{g}")
            s.dma("gpsimd", [(u, "vb2")], [(u, "vb2")],
                  lambda e, g=g, t0=t0: e.dma_start(
                      out=vb2[...], in_=v_h[g * 128:(g + 1) * 128, t0:t0 + TT]),
                  tag=f"{u}fv{tt}_{g}")
            s.op("vector", [(u, "yb", g), (u, "ub2"), (u, "colb")], [(u, "yb", g)],
                 lambda e, g=g: e.scalar_tensor_tensor(
                     out=yb[:, g, :], in0=ub2[...], scalar=colb[:, 6, g:g + 1], in1=yb[:, g, :],
                     op0=ALU.mult, op1=ALU.add),
                 tag=f"{u}fd{tt}_{g}")
            s.op("vector", [(u, "yb", g), (u, "vb2")], [(u, "yb", g)],
                 lambda e, g=g: e.tensor_mul(yb[:, g, :], yb[:, g, :], vb2[...]),
                 tag=f"{u}fv2{tt}_{g}")
        yk = [(u, "yb", g) for g in range(NG)]
        for m in range(NG):
            wb = m % 2
            s.dma("scalar", [(u, "wsl3", wb)], [(u, "wsl3", wb)],
                  lambda e, m=m, wb=wb: e.dma_start(out=wsl3[:, wb], in_=w2t[m]),
                  tag=f"{u}lw2{tt}_{m}")
            def mmo(e, m=m, wb=wb):
                last = None
                for kg in range(NG):
                    last = e.matmul(p.bank(m % 4), wsl3[:, wb, kg, :], yb[:, kg, :],
                                    start=(kg == 0), stop=(kg == NG - 1))
                return last
            s.op("tensor", [(u, "wsl3", wb)] + yk, [("psb", m % 4)], mmo,
                 tag=f"{u}mo{tt}_{m}")
            s.op("vector", [("psb", m % 4), (u, "colb")], [("x", m)],
                 lambda e, m=m, t0=t0: e.scalar_tensor_tensor(
                     out=x[:, m, t0:t0 + TT], in0=p.bank(m % 4), scalar=1.0,
                     in1=x[:, m, t0:t0 + TT], op0=ALU.mult, op1=ALU.add),
                 tag=f"{u}xa{tt}_{m}")
            s.op("vector", [("x", m), (u, "colb")], [("x", m)],
                 lambda e, m=m, t0=t0: e.tensor_scalar_add(
                     x[:, m, t0:t0 + TT], x[:, m, t0:t0 + TT], colb[:, 5, m:m + 1]),
                 tag=f"{u}xb{tt}_{m}")


def _coltile(w):
    w = np.asarray(w)
    K, M = w.shape
    return np.ascontiguousarray(w.reshape(K // 128, 128, M // 128, 128).transpose(2, 1, 0, 3))


def _rotary_tables():
    DIM_HEAD, WINDOW, n = 64, 512, T
    half_idx = np.arange(0, DIM_HEAD, 2, dtype=np.float64)
    inv_freq = 1.0 / 10000.0 ** (half_idx / DIM_HEAD)
    t = np.arange(n, dtype=np.float64)
    freqs = t[:, None] * inv_freq[None, :]
    freqs = np.concatenate([freqs, freqs], axis=-1)
    base = (half_idx + 0.4 * DIM_HEAD) / (1.4 * DIM_HEAD)
    power = (t - n // 2) / float(WINDOW // 2)
    scale = base[None, :] ** power[:, None]
    scale = np.concatenate([scale, scale], axis=-1)
    cos, sin = np.cos(freqs), np.sin(freqs)

    def tile(a):
        a = np.ascontiguousarray(a.T.astype(np.float32))
        return np.ascontiguousarray(np.tile(a, (2, 1)).astype(BF))
    return (tile(cos * scale), tile(sin * scale), tile(cos / scale), tile(sin / scale))


def emit_l1ffn(p, inp, call):
    nc, s = p.nc, p.s
    x = p.buf["x"]
    u = f"l1f{call}"
    HID = 2730
    HIDP = 2816
    wqkv = np.asarray(inp["L1_ffn_wqkv"])
    wo = np.asarray(inp["L1_ffn_wo"])
    w1 = np.asarray(inp["L1_ffn_w1"])
    w2 = np.asarray(inp["L1_ffn_w2"])
    wq_t = np.stack([_coltile(wqkv[l, :, 0:512]) for l in range(2)])
    wk_t = np.stack([_coltile(wqkv[l, :, 512:1024]) for l in range(2)])
    wvn = np.ascontiguousarray(wqkv[:, :, 1024:1536])
    wo_t = np.stack([_coltile(wo[l]) for l in range(2)])
    w1a = np.zeros((2, 1024, HIDP), np.float32)
    w1g = np.zeros((2, 1024, HIDP), np.float32)
    w1a[:, :, :HID] = w1[:, :, 0:HID]
    w1g[:, :, :HID] = w1[:, :, HID:]
    w1a_t = np.stack([_coltile(w1a[l]) for l in range(2)])
    w1g_t = np.stack([_coltile(w1g[l]) for l in range(2)])
    w2p = np.zeros((2, HIDP, 1024), np.float32)
    w2p[:, :HID, :] = w2
    w2_t = np.stack([np.ascontiguousarray(
        w2p[l].reshape(22, 128, 8, 128).transpose(2, 1, 0, 3)) for l in range(2)])
    b2_h = p.input("l1f_b2", np.ascontiguousarray(np.asarray(inp["L1_ffn_b2"]).reshape(1, 2048)))
    b2s = p.sbuf(u + "_b2s", (1, 2048), F32, side="right")
    s.dma("sync", [], [(u, "b2s")], lambda e: e.dma_start(out=b2s[...], in_=b2_h[...]),
          tag=u + "ldb2")
    wqh = emit_to_bf16(p, "l1f_wq", wq_t, (2, 4, 128, 8, 128))
    wkh = emit_to_bf16(p, "l1f_wk", wk_t, (2, 4, 128, 8, 128))
    wvh = emit_to_bf16(p, "l1f_wv", wvn, (2, 1024, 512))
    woh = emit_to_bf16(p, "l1f_wo", wo_t, (2, 8, 128, 4, 128))
    w1ah = emit_to_bf16(p, "l1f_w1a", w1a_t, (2, 22, 128, 8, 128))
    w1gh = emit_to_bf16(p, "l1f_w1g", w1g_t, (2, 22, 128, 8, 128))
    w2h = emit_to_bf16(p, "l1f_w2", w2_t, (2, 8, 128, 22, 128))
    csq, snq, csk, snk = _rotary_tables()
    csqh = p.input("l1f_csq", csq)
    snqh = p.input("l1f_snq", snq)
    cskh = p.input("l1f_csk", csk)
    snkh = p.input("l1f_snk", snk)
    qsc_h = p.input("l1f_qs", np.ascontiguousarray(np.stack(
        [np.tile(np.asarray(inp["L1_ffn_q_scale"])[l], 2) for l in range(2)], 1).astype(np.float32)))
    ksc_h = p.input("l1f_ks", np.ascontiguousarray(np.stack(
        [np.tile(np.asarray(inp["L1_ffn_k_scale"])[l], 2) for l in range(2)], 1).astype(np.float32)))
    lng_h = p.input("l1f_lng", np.ascontiguousarray(
        np.asarray(inp["L1_ffn_ln_g"]).reshape(2, 8, 128).transpose(2, 0, 1)))
    lnb_h = p.input("l1f_lnb", np.ascontiguousarray(
        np.asarray(inp["L1_ffn_ln_b"]).reshape(2, 8, 128).transpose(2, 0, 1)))
    b1 = np.asarray(inp["L1_ffn_b1"])
    b1a_p = np.zeros((2, HIDP), np.float32)
    b1a_p[:, :HID] = b1[:, :HID]
    b1g_p = np.zeros((2, HIDP), np.float32)
    b1g_p[:, :HID] = b1[:, HID:]
    b1a_h = p.input("l1f_b1a", np.ascontiguousarray(b1a_p.reshape(2, 22, 128).transpose(2, 0, 1)))
    b1g_h = p.input("l1f_b1g", np.ascontiguousarray(b1g_p.reshape(2, 22, 128).transpose(2, 0, 1)))
    _free_cvt(p, u)
    b2b = p.sbuf(u + "_b2b", (1, 2048), BF16)
    s.op("scalar", [(u, "b2s")], [(u, "b2b")],
         lambda e: e.copy(out=b2b[...], in_=b2s[...]), tag=u + "cvb2")
    s.barrier()
    p.free_buf(u + "_b2s")

    pcol = p.sbuf(u + "_pcol", (128, 4), F32)
    lngb = p.sbuf(u + "_lngb", (128, 2, 8), F32)
    lnbb = p.sbuf(u + "_lnbb", (128, 2, 8), F32)
    b1ac = p.sbuf(u + "_b1ac", (128, 2, 22), F32)
    b1gc = p.sbuf(u + "_b1gc", (128, 2, 22), F32)
    s.dma("sync", [], [(u, "pcol")],
          lambda e: e.dma_start(out=pcol[:, 0:2], in_=qsc_h[...]), tag=u + "lq")
    s.dma("sync", [(u, "pcol")], [(u, "pcol")],
          lambda e: e.dma_start(out=pcol[:, 2:4], in_=ksc_h[...]), tag=u + "lk")
    s.dma("sync", [], [(u, "lngb")], lambda e: e.dma_start(out=lngb[...], in_=lng_h[...]),
          tag=u + "lg")
    s.dma("sync", [], [(u, "lnbb")], lambda e: e.dma_start(out=lnbb[...], in_=lnb_h[...]),
          tag=u + "lb")
    s.dma("sync", [], [(u, "b1ac")], lambda e: e.dma_start(
        out=b1ac[...], in_=b1a_h.rearrange("p l m -> p l m")), tag=u + "lb1a")
    s.dma("sync", [], [(u, "b1gc")], lambda e: e.dma_start(
        out=b1gc[...], in_=b1g_h.rearrange("p l m -> p l m")), tag=u + "lb1g")
    rotP = np.zeros((128, 128), BF)
    for blk in range(2):
        o = blk * 64
        for i in range(32):
            rotP[o + i + 32, o + i] = -1.0
            rotP[o + i, o + i + 32] = 1.0
    ROTP = emit_const(p, "l1f_rot", rotP, scope="phase")
    trim = emit_const(p, u + "_trim",
                      np.triu(np.full((128, 128), -30000.0, BF), k=1), scope="phase")
    trlo = emit_const(p, u + "_trlo",
                      np.tril(np.full((128, 128), -30000.0, BF), k=-1), scope="phase")
    ind2 = np.zeros((128, 2), BF)
    ind2[0:64, 0] = 1.0
    ind2[64:128, 1] = 1.0
    IND2 = emit_const(p, "l1f_ind2", ind2, scope="phase")
    onerow = p.buf["onerow_bf"]
    onecol_bf = p.buf["onecol_bf"]
    ones_col = p.buf["ones_col"]

    xi_h = p.scratch(u + "_xi", (D, T), F32)
    q_h = p.scratch(u + "_q", (512, T), BF16)
    k_h = p.scratch(u + "_k", (512, T), BF16)
    v_h = p.scratch(u + "_v", (T, 512), BF16)
    q_hr = q_h.rearrange("(a p) t -> p a t", p=128)
    k_hr = k_h.rearrange("(a p) t -> p a t", p=128)
    v_hr = v_h.rearrange("(a p) e -> p a e", p=128)

    xist = p.sbuf(u + "_xist", (128, 2, TT), F32)
    for tt in range(NTT):
        t0 = tt * TT
        emit_ln_tile(p, u, tt)
        for g in range(NG):
            sb = g % 2
            emit_ln_apply(p, u, tt, g, xist[:, sb, :], (u, "xist", sb))
            s.dma("gpsimd", [(u, "xist", sb)], [(u, "xist", sb)],
                  lambda e, g=g, sb=sb, t0=t0: e.dma_start(
                      out=xi_h[g * 128:(g + 1) * 128, t0:t0 + TT], in_=xist[:, sb, :]),
                  tag=f"{u}xis{tt}_{g}")
    s.barrier()
    p.free_buf(u + "_xist")

    for l in range(L1FFN_L):
        ul = f"{u}L{l}"
        xw = p.sbuf(u + "_xw", (128, NG, TT + 8), F32)
        hsh = p.sbuf(u + "_hsh", (128, NG, TT), BF16)
        wsl = p.sbuf(u + "_wsl", (128, 2, NG, 128), BF16)
        qb = p.sbuf(u + "_qb", (128, 4, TT), BF16)
        qsq = p.sbuf(u + "_qsq", (128, 4, TT), BF16)
        r8 = p.sbuf(u + "_r8", (1, 4096), BF16)
        tabs = p.sbuf(u + "_tabs", (128, 2, TT), BF16)
        stg = p.sbuf(u + "_stg", (128, 2, TT), BF16)
        vws = p.sbuf(u + "_vws", (128, NG, 512), BF16)
        for kg in range(NG):
            s.dma("gpsimd", [(u, "vws", kg)], [(u, "vws", kg)],
                  lambda e, kg=kg, l=l: e.dma_start(out=vws[:, kg, :],
                                                    in_=wvh[l, kg * 128:(kg + 1) * 128, :]),
                  tag=f"{ul}lwv{kg}")
        for tt in range(NTT):
            t0 = tt * TT
            for g in range(NG):
                if tt == 0:
                    s.op("vector", [(u, "xw", g)], [(u, "xw", g)],
                         lambda e, g=g: e.memset(xw[:, g, 0:1], 0.0), tag=f"{ul}xz{g}")
                    s.dma("sync", [(u, "xw", g)], [(u, "xw", g)],
                          lambda e, g=g: e.dma_start(out=xw[:, g, 1:1 + TT],
                                                     in_=xi_h[g * 128:(g + 1) * 128, 0:TT]),
                          tag=f"{ul}xl{tt}_{g}")
                else:
                    s.dma("sync", [(u, "xw", g)], [(u, "xw", g)],
                          lambda e, g=g, t0=t0: e.dma_start(
                              out=xw[:, g, 0:1 + TT],
                              in_=xi_h[g * 128:(g + 1) * 128, t0 - 1:t0 + TT]),
                          tag=f"{ul}xl{tt}_{g}")
                off = 1 if g < 4 else 0
                s.op("scalar", [(u, "xw", g)], [(u, "hsh", g)],
                     lambda e, g=g, off=off: e.copy(out=hsh[:, g, :], in_=xw[:, g, off:off + TT]),
                     tag=f"{ul}sh{tt}_{g}")
            hk = [(u, "hsh", g) for g in range(NG)]
            for which, wth, csh, snh, sccol, dsthr in (
                    ("q", wqh, csqh, snqh, 0, q_hr), ("k", wkh, cskh, snkh, 2, k_hr)):
                s.dma("gpsimd", [(u, "tabs")], [(u, "tabs")],
                      lambda e, csh=csh, t0=t0: e.dma_start(out=tabs[:, 0, :],
                                                            in_=csh[:, t0:t0 + TT]),
                      tag=f"{ul}tc{which}{tt}")
                s.dma("gpsimd", [(u, "tabs")], [(u, "tabs")],
                      lambda e, snh=snh, t0=t0: e.dma_start(out=tabs[:, 1, :],
                                                            in_=snh[:, t0:t0 + TT]),
                      tag=f"{ul}ts{which}{tt}")
                for m in range(4):
                    wb = m % 2
                    s.dma("scalar", [(u, "wsl", wb)], [(u, "wsl", wb)],
                          lambda e, m=m, wb=wb, wth=wth, l=l: e.dma_start(
                              out=wsl[:, wb], in_=wth[l, m]),
                          tag=f"{ul}lw{which}{tt}_{m}")
                    def mmq(e, m=m, wb=wb):
                        last = None
                        for kg in range(NG):
                            last = e.matmul(p.bank(m % 4), wsl[:, wb, kg, :], hsh[:, kg, :],
                                            start=(kg == 0), stop=(kg == NG - 1))
                        return last
                    s.op("tensor", [(u, "wsl", wb)] + hk, [("psb", m % 4)], mmq,
                         tag=f"{ul}m{which}{tt}_{m}")
                    s.op("scalar", [("psb", m % 4)], [(u, "qb", m)],
                         lambda e, m=m: e.copy(out=qb[:, m, :], in_=p.bank(m % 4)),
                         tag=f"{ul}e{which}{tt}_{m}")
                qk = [(u, "qb", m_) for m_ in range(4)]
                s.op("scalar", qk, [(u, "qsq")],
                     lambda e: e.square(out=qsq.rearrange("p a t -> p (a t)"),
                                        in_=qb.rearrange("p a t -> p (a t)")),
                     tag=f"{ul}sq{which}{tt}")
                def mmn(e):
                    last = None
                    for h8 in range(8):
                        g2, hi = h8 // 2, h8 % 2
                        last = e.matmul(p.psum["PS"][0:1, h8 * 512:(h8 + 1) * 512],
                                        IND2[:, hi:hi + 1], qsq[:, g2, :],
                                        start=True, stop=True, skip_group_check=True)
                    return last
                s.op("tensor", [(u, "qsq"), "l1f_ind2"], [("psb", b) for b in range(8)],
                     mmn, tag=f"{ul}n{which}{tt}")
                s.op("vector", [("psb", b) for b in range(8)], [(u, "r8")],
                     lambda e: e.tensor_scalar_max(r8[...], p.psum["PS"][0:1, :], 1e-24),
                     tag=f"{ul}rmx{which}{tt}")
                s.op("scalar", [(u, "r8")], [(u, "r8")],
                     lambda e: e.activation(out=r8[...], in_=r8[...], func=AF.Ln),
                     tag=f"{ul}rln{which}{tt}")
                s.op("scalar", [(u, "r8"), "fvals"], [(u, "r8")],
                     lambda e: e.activation(out=r8[...], in_=r8[...], func=AF.Exp,
                                            scale=p.fconst(-0.5)[0:1, :]),
                     tag=f"{ul}rex{which}{tt}")
                for h8 in range(8):
                    g2, hi = h8 // 2, h8 % 2
                    r0 = hi * 64
                    bb = 4 + h8 % 2
                    s.op("tensor", [(u, "r8")], [("psb", bb)],
                         lambda e, h8=h8, bb=bb: e.matmul(
                             p.bank(bb, rows=64, cols=512), onerow[:, :64],
                             r8[:, h8 * 512:(h8 + 1) * 512], start=True, stop=True),
                         tag=f"{ul}rb{which}{tt}_{h8}")
                    s.op("vector", [("psb", bb), (u, "qb", g2)], [(u, "qb", g2)],
                         lambda e, g2=g2, r0=r0, bb=bb: e.tensor_mul(
                             qb[r0:r0 + 64, g2, :], qb[r0:r0 + 64, g2, :],
                             p.bank(bb, rows=64, cols=512)),
                         tag=f"{ul}rm{which}{tt}_{h8}")
                qksc = 8.0 if which == "q" else None
                s.op("vector", qk + [(u, "pcol")], [(u, "qb", 0), (u, "qb", 1), (u, "qb", 2), (u, "qb", 3)],
                     lambda e, sccol=sccol, l=l, qksc=qksc: e.tensor_scalar(
                         out=qb.rearrange("p a t -> p (a t)"),
                         in0=qb.rearrange("p a t -> p (a t)"),
                         scalar1=pcol[:, sccol + l:sccol + l + 1],
                         scalar2=qksc, op0=ALU.mult,
                         op1=(ALU.mult if qksc is not None else ALU.bypass)),
                     tag=f"{ul}sc{which}{tt}")
                for g2 in range(4):
                    bb = 4 + g2 % 2
                    s.op("tensor", [(u, "qb", g2), "l1f_rot"], [("psb", bb)],
                         lambda e, g2=g2, bb=bb: e.matmul(
                             p.bank(bb), ROTP[...], qb[:, g2, :], start=True, stop=True),
                         tag=f"{ul}rt{which}{tt}_{g2}")
                    s.op("vector", [("psb", bb), (u, "tabs")], [(u, "stg", g2 % 2)],
                         lambda e, g2=g2, bb=bb: e.tensor_mul(
                             stg[:, g2 % 2, :], p.bank(bb), tabs[:, 1, :]),
                         tag=f"{ul}rs{which}{tt}_{g2}")
                    s.op("vector", [(u, "qb", g2), (u, "tabs")], [(u, "qb", g2)],
                         lambda e, g2=g2: e.tensor_mul(qb[:, g2, :], qb[:, g2, :], tabs[:, 0, :]),
                         tag=f"{ul}rc{which}{tt}_{g2}")
                    s.op("vector", [(u, "qb", g2), (u, "stg", g2 % 2)], [(u, "qb", g2)],
                         lambda e, g2=g2: e.tensor_add(qb[:, g2, :], qb[:, g2, :],
                                                       stg[:, g2 % 2, :]),
                         tag=f"{ul}ra{which}{tt}_{g2}")
                s.dma("sync", qk, [(u, which + "h", tt)],
                      lambda e, t0=t0, dsthr=dsthr: e.dma_start(
                          out=dsthr[:, :, t0:t0 + TT], in_=qb[...]),
                      tag=f"{ul}s{which}{tt}")
            for vst in range(4):
                bb = 6 + vst % 2
                def mmv(e, vst=vst, bb=bb):
                    last = None
                    for kg in range(NG):
                        last = e.matmul(p.bank(bb), hsh[:, kg, vst * 128:(vst + 1) * 128],
                                        vws[:, kg, :], start=(kg == 0), stop=(kg == NG - 1))
                    return last
                s.op("tensor", hk + [(u, "vws", kg_) for kg_ in range(NG)], [("psb", bb)],
                     mmv, tag=f"{ul}mv{tt}_{vst}")
                s.op("scalar", [("psb", bb)], [(u, "stg", vst % 2)],
                     lambda e, vst=vst, bb=bb: e.copy(out=stg[:, vst % 2, :], in_=p.bank(bb)),
                     tag=f"{ul}ev{tt}_{vst}")
                s.dma("sync", [(u, "stg", vst % 2)], [(u, "vh", tt, vst)],
                      lambda e, t0=t0, vst=vst: e.dma_start(
                          out=v_h[t0 + vst * 128:t0 + (vst + 1) * 128, :],
                          in_=stg[:, vst % 2, :]),
                      tag=f"{ul}sv{tt}_{vst}")
        s.barrier()
        for n in (u + "_vws", u + "_stg", u + "_tabs", u + "_r8", u + "_qsq", u + "_qb",
                  u + "_wsl", u + "_hsh", u + "_xw"):
            p.free_buf(n)

        xwr = p.sbuf(u + "_xwr", (128, NG, TT), F32)
        qw = p.sbuf(u + "_qw", (128, 4, TT), BF16)
        kw = p.sbuf(u + "_kw", (128, 4, 1024), BF16)
        vw = p.sbuf(u + "_vw", (128, 8, 512), BF16)
        pbuf = p.sbuf(u + "_pb2", (128, 1024), BF16)
        pts = p.sbuf(u + "_pts2", (128, 4, 128), BF16)
        ofm = p.sbuf(u + "_ofm", (128, 4, TT), BF16)
        wsl2 = p.sbuf(u + "_wsl2", (128, 2, 4, 128), BF16)
        mcol = p.sbuf(u + "_mcol2", (128, 1), F32)
        scol = p.sbuf(u + "_scol2", (128, 1), F32)
        rcol = p.sbuf(u + "_rcol2", (128, 1), F32)
        rrow = p.sbuf(u + "_rrow2", (1, 128), F32)
        rbs = p.sbuf(u + "_rbs2", (64, 128), F32)
        ost = p.sbuf(u + "_ost2", (128, 2, TT), F32)
        for w in range(8):
            t0 = w * TT
            kv0 = max(0, t0 - 512)
            kvw = t0 + TT - kv0
            nsub = kvw // 128
            s.dma("gpsimd", [(u, "qw")], [(u, "qw")],
                  lambda e, t0=t0: e.dma_start(out=qw[...], in_=q_hr[:, :, t0:t0 + TT]),
                  tag=f"{ul}aq{w}")
            s.dma("gpsimd", [(u, "kw")], [(u, "kw")],
                  lambda e, kv0=kv0, kvw=kvw: e.dma_start(
                      out=kw[:, :, 0:kvw], in_=k_hr[:, :, kv0:kv0 + kvw]),
                  tag=f"{ul}ak{w}")
            s.dma("gpsimd", [(u, "vw")], [(u, "vw")],
                  lambda e, kv0=kv0, nsub=nsub: e.dma_start(
                      out=vw[:, 0:nsub, :], in_=v_hr[:, kv0 // 128:kv0 // 128 + nsub, :]),
                  tag=f"{ul}av{w}")
            for g in range(NG):
                s.dma("sync", [(u, "xwr", g)], [(u, "xwr", g)],
                      lambda e, g=g, t0=t0: e.dma_start(
                          out=xwr[:, g, :], in_=xi_h[g * 128:(g + 1) * 128, t0:t0 + TT]),
                      tag=f"{ul}ax{w}_{g}")
            for h8 in globals().get("L1FFN_HEADS", range(8)):
                g2, hi = h8 // 2, h8 % 2
                r0 = hi * 64
                e0 = h8 * 64
                for qt in range(4):
                    if w == 0:
                        c0 = 0
                        cw = (qt + 1) * 128
                    else:
                        c0 = qt * 128
                        cw = 640
                    nblk = cw // 128
                    def mms(e, g2=g2, r0=r0, qt=qt, c0=c0, cw=cw):
                        last = None
                        for cc in range(0, cw, 512):
                            ww = min(512, cw - cc)
                            last = e.matmul(p.psum["PS"][:, cc:cc + ww],
                                            qw[r0:r0 + 64, g2, qt * 128:(qt + 1) * 128],
                                            kw[r0:r0 + 64, g2, c0 + cc:c0 + cc + ww],
                                            start=True, stop=True, skip_group_check=True)
                        return last
                    s.op("tensor", [(u, "qw"), (u, "kw")], [("psb", 0), ("psb", 1)],
                         mms, tag=f"{ul}S{w}_{h8}_{qt}")
                    s.op("vector", [u + "_trim", ("psb", (cw - 128) // 512)], [("psb", (cw - 128) // 512)],
                         lambda e, cw=cw: e.tensor_add(p.psum["PS"][:, cw - 128:cw],
                                                       p.psum["PS"][:, cw - 128:cw], trim[...]),
                         tag=f"{ul}mk{w}_{h8}_{qt}")
                    if w > 0:
                        s.op("vector", [u + "_trlo", ("psb", 0)], [("psb", 0)],
                             lambda e: e.tensor_add(p.psum["PS"][:, 0:128],
                                                    p.psum["PS"][:, 0:128], trlo[...]),
                             tag=f"{ul}mk2{w}_{h8}_{qt}")
                    s.op("vector", [("psb", 0), ("psb", 1)], [(u, "mcol")],
                         lambda e, cw=cw: e.reduce_max(mcol[:, 0:1], p.psum["PS"][:, 0:cw],
                                                       axis=AX.X, negate=True),
                         tag=f"{ul}mx{w}_{h8}_{qt}")
                    s.op("scalar", [(u, "mcol"), ("psb", 0), ("psb", 1)],
                         [(u, "pb2"), (u, "scol2")],
                         lambda e, cw=cw: e.activation(out=pbuf[:, 0:cw], in_=p.psum["PS"][:, 0:cw],
                                                       func=AF.Exp, bias=mcol[:, 0:1],
                                                       accum_out=scol[:, 0:1]),
                         tag=f"{ul}ex{w}_{h8}_{qt}")
                    def emit_T(c):
                        tb = 2 + c % 4
                        s.op("tensor", [(u, "pb2")], [("psb", tb)],
                             lambda e, c=c, tb=tb: e.transpose(
                                 p.bank(tb).bitcast(BF16)[:, :128], pbuf[:, c * 128:(c + 1) * 128],
                                 p.buf["eye_bf"][...]),
                             tag=f"{ul}T{w}_{h8}_{qt}_{c}")
                        s.op("vector", [("psb", tb)], [(u, "pts2", c % 4)],
                             lambda e, c=c, tb=tb: e.tensor_copy(
                                 out=pts[:, c % 4, :], in_=p.bank(tb).bitcast(BF16)[:, :128]),
                             tag=f"{ul}TE{w}_{h8}_{qt}_{c}")
                    def emit_PV(c, qt=qt, c0=c0, e0=e0, nblk=nblk):
                        def mmpv(e, c=c, qt=qt, c0=c0, e0=e0, nblk=nblk):
                            kb5 = (c0 + c * 128) // 128
                            return e.matmul(p.bank(7, rows=64, cols=128),
                                            vw[:, kb5, e0:e0 + 64], pts[:, c % 4, :],
                                            start=(c == 0), stop=(c == nblk - 1),
                                            skip_group_check=True)
                        s.op("tensor", [(u, "pts2", c % 4), (u, "vw")], [("psb", 7)],
                             mmpv, tag=f"{ul}PV{w}_{h8}_{qt}_{c}")
                    for c in range(nblk):
                        emit_T(c)
                        if c >= 1:
                            emit_PV(c - 1)
                    emit_PV(nblk - 1)
                    s.op("vector", [(u, "scol2")], [(u, "rcol2")],
                         lambda e: e.reciprocal(rcol[:, :], scol[:, :]),
                         tag=f"{ul}ric{w}_{h8}_{qt}")
                    s.op("tensor", [(u, "rcol2")], [("psb", 6)],
                         lambda e: e.transpose(p.bank(6, rows=1, cols=128), rcol[:, :],
                                               p.buf["eye_f32"][...]),
                         tag=f"{ul}rcT{w}_{h8}_{qt}")
                    s.op("scalar", [("psb", 6)], [(u, "rrow2")],
                         lambda e: e.copy(out=rrow[:, :], in_=p.bank(6, rows=1, cols=128)),
                         tag=f"{ul}rce{w}_{h8}_{qt}")
                    s.op("tensor", [(u, "rrow2")], [("psb", 4)],
                         lambda e: e.matmul(p.bank(4, rows=64, cols=128), ones_col[:, :64],
                                            rrow[:, :], start=True, stop=True),
                         tag=f"{ul}rb{w}_{h8}_{qt}")
                    s.op("vector", [("psb", 4)], [(u, "rbs2")],
                         lambda e: e.tensor_copy(out=rbs[:, :], in_=p.bank(4, rows=64, cols=128)),
                         tag=f"{ul}rbe{w}_{h8}_{qt}")
                    s.op("vector", [("psb", 7), (u, "rbs2")], [(u, "ofm", g2)],
                         lambda e, r0=r0, g2=g2, qt=qt: e.tensor_mul(
                             ofm[r0:r0 + 64, g2, qt * 128:(qt + 1) * 128],
                             p.bank(7, rows=64, cols=128), rbs[:, :]),
                         tag=f"{ul}oe{w}_{h8}_{qt}")
            ok = [(u, "ofm", g_) for g_ in range(4)]
            for m in range(NG):
                wb = m % 2
                s.dma("scalar", [(u, "wsl2", wb)], [(u, "wsl2", wb)],
                      lambda e, m=m, wb=wb, l=l: e.dma_start(out=wsl2[:, wb], in_=woh[l, m]),
                      tag=f"{ul}lwo{w}_{m}")
                def mmo(e, m=m, wb=wb):
                    last = None
                    for kg in range(4):
                        last = e.matmul(p.bank(m % 2), wsl2[:, wb, kg, :], ofm[:, kg, :],
                                        start=(kg == 0), stop=(kg == 3))
                    return last
                s.op("tensor", [(u, "wsl2", wb)] + ok, [("psb", m % 2)], mmo,
                     tag=f"{ul}mo{w}_{m}")
                sb = m % 2
                s.op("vector", [("psb", m % 2), (u, "xwr", m)], [(u, "ost2", sb)],
                     lambda e, m=m, sb=sb: e.tensor_add(ost[:, sb, :], p.bank(m % 2),
                                                        xwr[:, m, :]),
                     tag=f"{ul}or{w}_{m}")
                s.dma("sync", [(u, "ost2", sb)], [(u, "xiw", w, m)],
                      lambda e, m=m, sb=sb, t0=t0: e.dma_start(
                          out=xi_h[m * 128:(m + 1) * 128, t0:t0 + TT], in_=ost[:, sb, :]),
                      tag=f"{ul}so{w}_{m}")
        s.barrier()
        for n in (u + "_ost2", u + "_rbs2", u + "_rrow2", u + "_rcol2", u + "_scol2",
                  u + "_mcol2", u + "_wsl2",
                  u + "_ofm", u + "_pts2", u + "_pb2", u + "_vw", u + "_kw", u + "_qw",
                  u + "_xwr"):
            p.free_buf(n)

        if globals().get("L1FFN_SKIP_FFN"):
            continue
        xw2 = p.sbuf(u + "_xw2", (128, NG, TT + 8), F32)
        xnb = p.sbuf(u + "_xnb", (128, NG, TT), BF16)
        habuf = p.sbuf(u + "_hab", (128, 22, TT), BF16)
        wsl3 = p.sbuf(u + "_wsl3", (128, 1, NG, 128), BF16)
        wsl4 = p.sbuf(u + "_wsl4", (128, 1, 22, 128), BF16)
        gstg = p.sbuf(u + "_gstg", (128, TT), BF16)
        ost3 = p.sbuf(u + "_ost3", (128, 1, TT), F32)
        for tt in range(NTT):
            t0 = tt * TT
            for g in range(NG):
                if tt == 0:
                    s.op("vector", [(u, "xw2", g)], [(u, "xw2", g)],
                         lambda e, g=g: e.memset(xw2[:, g, 0:1], 0.0), tag=f"{ul}fz{g}")
                    s.dma("sync", [(u, "xw2", g)], [(u, "xw2", g)],
                          lambda e, g=g: e.dma_start(out=xw2[:, g, 1:1 + TT],
                                                     in_=xi_h[g * 128:(g + 1) * 128, 0:TT]),
                          tag=f"{ul}fl{tt}_{g}")
                else:
                    s.dma("sync", [(u, "xw2", g)], [(u, "xw2", g)],
                          lambda e, g=g, t0=t0: e.dma_start(
                              out=xw2[:, g, 0:1 + TT],
                              in_=xi_h[g * 128:(g + 1) * 128, t0 - 1:t0 + TT]),
                          tag=f"{ul}fl{tt}_{g}")
            offs = [1 if g < 4 else 0 for g in range(NG)]
            emit_ln_tile(p, ul + "f", 0, src=xw2, src_keys=[(u, "xw2", g) for g in range(NG)],
                         src_offs=offs)
            for g in range(NG):
                emit_ln_apply(p, ul + "f", 0, g, xnb[:, g, :], (u, "xnb", g), src=xw2,
                              src_key=(u, "xw2", g), src_off=offs[g])
                s.op("vector", [(u, "xnb", g), (u, "lngb")], [(u, "xnb", g)],
                     lambda e, g=g, l=l: e.tensor_scalar(
                         out=xnb[:, g, :], in0=xnb[:, g, :], scalar1=lngb[:, l, g:g + 1],
                         scalar2=lnbb[:, l, g:g + 1], op0=ALU.mult, op1=ALU.add),
                     tag=f"{ul}fgb{tt}_{g}")
            nk = [(u, "xnb", g) for g in range(NG)]
            if globals().get("L1FFN_F_XNBOUT"):
                for g in range(NG):
                    s.dma("gpsimd", [(u, "xnb", g)], [(u, "xif", tt, g)],
                          lambda e, g=g, t0=t0: e.dma_start(
                              out=xi_h[g * 128:(g + 1) * 128, t0:t0 + TT], in_=xnb[:, g, :]),
                          tag=f"{ul}dbgxnb{tt}_{g}")
                continue
            for ma in range(22):
                wb = 0
                s.dma("scalar", [(u, "wsl3", wb)], [(u, "wsl3", wb)],
                      lambda e, ma=ma, wb=wb, l=l: e.dma_start(out=wsl3[:, wb], in_=w1gh[l, ma]),
                      tag=f"{ul}flg{tt}_{ma}")
                def mmg(e, ma=ma, wb=wb):
                    last = None
                    for kg in range(NG):
                        last = e.matmul(p.bank(4 + ma % 2), wsl3[:, wb, kg, :], xnb[:, kg, :],
                                        start=(kg == 0), stop=(kg == NG - 1))
                    return last
                s.op("tensor", [(u, "wsl3", wb)] + nk, [("psb", 4 + ma % 2)], mmg,
                     tag=f"{ul}fg{tt}_{ma}")
                s.op("scalar", [("psb", 4 + ma % 2), (u, "b1gc")], [(u, "gstg")],
                     lambda e, ma=ma, l=l: e.activation(out=gstg[...], in_=p.bank(4 + ma % 2),
                                                        func=AF.Gelu, bias=b1gc[:, l, ma:ma + 1]),
                     tag=f"{ul}fge{tt}_{ma}")
                s.dma("gpsimd", [(u, "wsl3", wb)], [(u, "wsl3", wb)],
                      lambda e, ma=ma, wb=wb, l=l: e.dma_start(out=wsl3[:, wb], in_=w1ah[l, ma]),
                      tag=f"{ul}fla{tt}_{ma}")
                def mma(e, ma=ma, wb=wb):
                    last = None
                    for kg in range(NG):
                        last = e.matmul(p.bank(6 + ma % 2), wsl3[:, wb, kg, :], xnb[:, kg, :],
                                        start=(kg == 0), stop=(kg == NG - 1))
                    return last
                s.op("tensor", [(u, "wsl3", wb)] + nk, [("psb", 6 + ma % 2)], mma,
                     tag=f"{ul}fa{tt}_{ma}")
                s.op("vector", [("psb", 6 + ma % 2), (u, "gstg"), (u, "b1ac")], [(u, "hab", ma)],
                     lambda e, ma=ma, l=l: e.scalar_tensor_tensor(
                         out=habuf[:, ma, :], in0=p.bank(6 + ma % 2),
                         scalar=b1ac[:, l, ma:ma + 1], in1=gstg[...],
                         op0=ALU.add, op1=ALU.mult),
                     tag=f"{ul}fh{tt}_{ma}")
            hkk = [(u, "hab", ma) for ma in range(22)]
            if globals().get("L1FFN_F_HOUT"):
                for g in range(NG):
                    s.dma("gpsimd", [(u, "hab", g)], [(u, "xif", tt, g)],
                          lambda e, g=g, t0=t0: e.dma_start(
                              out=xi_h[g * 128:(g + 1) * 128, t0:t0 + TT], in_=habuf[:, g, :]),
                          tag=f"{ul}dbgh{tt}_{g}")
                continue
            for m in range(NG):
                wb = 0
                s.dma("scalar", [(u, "wsl4", wb)], [(u, "wsl4", wb)],
                      lambda e, m=m, wb=wb, l=l: e.dma_start(out=wsl4[:, wb], in_=w2h[l, m]),
                      tag=f"{ul}flw2{tt}_{m}")
                def mm2(e, m=m, wb=wb, l=l):
                    last = None
                    for kk in range(22):
                        last = e.matmul(p.bank(m % 4), wsl4[:, wb, kk, :], habuf[:, kk, :],
                                        start=(kk == 0), stop=False, skip_group_check=True)
                    return e.matmul(p.bank(m % 4), b2b[:, l * 1024 + m * 128:l * 1024 + (m + 1) * 128],
                                    onerow[:, :], start=False, stop=True, skip_group_check=True)
                s.op("tensor", [(u, "wsl4", wb), (u, "b2b")] + hkk, [("psb", m % 4)], mm2,
                     tag=f"{ul}f2{tt}_{m}")
                sb = 0
                s.op("vector", [("psb", m % 4), (u, "xw2", m)], [(u, "ost3", sb)],
                     lambda e, m=m, sb=sb: e.tensor_add(ost3[:, sb, :], p.bank(m % 4),
                                                        xw2[:, m, 1:1 + TT]),
                     tag=f"{ul}fr{tt}_{m}")
                s.dma("sync", [(u, "ost3", sb)], [(u, "xif", tt, m)],
                      lambda e, m=m, sb=sb, t0=t0: e.dma_start(
                          out=xi_h[m * 128:(m + 1) * 128, t0:t0 + TT], in_=ost3[:, sb, :]),
                      tag=f"{ul}fs{tt}_{m}")
        s.barrier()
        for n in (u + "_ost3", u + "_gstg", u + "_wsl4", u + "_wsl3", u + "_hab",
                  u + "_xnb", u + "_xw2"):
            p.free_buf(n)

    xfin = p.sbuf(u + "_xfin", (128, 2, TT), F32)
    for tt in range(NTT):
        t0 = tt * TT
        for g in range(NG):
            sb = g % 2
            s.dma("sync", [(u, "xfin", sb)], [(u, "xfin", sb)],
                  lambda e, g=g, sb=sb, t0=t0: e.dma_start(
                      out=xfin[:, sb, :], in_=xi_h[g * 128:(g + 1) * 128, t0:t0 + TT]),
                  tag=f"{u}ff{tt}_{g}")
            s.op("vector", [(u, "xfin", sb)], [("x", g)],
                 lambda e, g=g, sb=sb, t0=t0: e.scalar_tensor_tensor(
                     out=x[:, g, t0:t0 + TT], in0=xfin[:, sb, :], scalar=0.5,
                     in1=x[:, g, t0:t0 + TT], op0=ALU.mult, op1=ALU.add),
                 tag=f"{u}fx{tt}_{g}")



STEP_FNS = {}


def build_program(inp, x_percore, steps):
    import contextlib
    p = Prog()
    p._stack = contextlib.ExitStack()
    with p._stack:
        emit_load_x_any(p, x_percore)
        _consts(p)
        p.s.barrier()
        for st in steps:
            p.close_phase()
            STEP_FNS[st[0]](p, inp, *st[1:])
            p.s.barrier()
        p.close_phase()
        emit_store_x(p)
        p.s.finalize()
        p.s.emit_all(p._stack)
    return p


def emit_load_x_any(p, x_percore):
    nc, s = p.nc, p.s
    a = x_percore[0]
    xh = p.input("x_in", x_percore)
    x = p.sbuf("x", (128, NG, T), F32, scope="global")
    if a.dtype == np.float32:
        for g in range(NG):
            s.dma("sync", [], [("x", g)],
                  lambda e, g=g: e.dma_start(out=x[:, g, :], in_=xh[g * 128:(g + 1) * 128, :]),
                  tag=f"ldx{g}")
    else:
        stg = p.sbuf("xstg", (128, 2, T), F16, side="right")
        for g in range(NG):
            s.dma("sync", [("xstg", g % 2)], [("xstg", g % 2)],
                  lambda e, g=g: e.dma_start(out=stg[:, g % 2, :], in_=xh[g * 128:(g + 1) * 128, :]),
                  tag=f"ldx{g}")
            s.op("scalar", [("xstg", g % 2)], [("x", g)],
                 lambda e, g=g: e.copy(out=x[:, g, :], in_=stg[:, g % 2, :]), tag=f"cvx{g}")
        p.s.barrier()
        p.free_buf("xstg")


STEP_FNS["l3ffn"] = emit_l3ffn
STEP_FNS["l0attn"] = emit_l0attn
STEP_FNS["l3attn"] = emit_l3attn
STEP_FNS["l0ffn"] = emit_l0ffn
STEP_FNS["l2ffn"] = emit_l2ffn
STEP_FNS["l2attn"] = emit_l2attn
STEP_FNS["l1attn"] = emit_l1attn
STEP_FNS["l1ffn"] = emit_l1ffn

ALL_STEPS = [
    ("l0ffn", 0), ("l0attn",), ("l0ffn", 1),
    ("l1ffn", 0), ("l1attn",), ("l1ffn", 1),
    ("l2ffn", 0), ("l2attn",), ("l2ffn", 1),
    ("l3ffn", 0), ("l3attn",), ("l3ffn", 1),
]


def run_steps(inp, x_percore, steps):
    p = build_program(inp, x_percore, steps)
    in_maps = [{k: v[c] for k, v in p.inputs.items()} for c in range(8)]
    res = run_bass_kernel_spmd(p.nc, in_maps, core_ids=list(range(8)))
    return [res.results[c]["y"] for c in range(8)]


def kernel(**inputs):
    x = inputs["L0_attn_hidden_states"]
    x_percore = [np.ascontiguousarray(np.asarray(x[b]).T) for b in range(8)]
    ys = run_steps(inputs, x_percore, ALL_STEPS)
    out = np.stack([y.T for y in ys], axis=0)
    return out.astype(np.float32)


# version 15 (final)
# speedup vs baseline: 1.3619x; 1.3619x over previous
import math
import numpy as np
import ml_dtypes
import concourse.bass as bass
import concourse.mybir as mybir
from concourse.bass_utils import run_bass_kernel_spmd

F32 = mybir.dt.float32
BF16 = mybir.dt.bfloat16
F16 = mybir.dt.float16
AF = mybir.ActivationFunctionType
ALU = mybir.AluOpType
AX = mybir.AxisListType

BF = np.dtype(ml_dtypes.bfloat16)

D = 1024
T = 4096
NG = 8
TT = 512
L1FFN_L = 2
NTT = T // TT


ENGINES = ("sync", "tensor", "vector", "scalar", "gpsimd")


class _Op:
    __slots__ = ("engine", "kind", "emit", "deps", "idx", "sem_val", "forced", "tag",
                 "pool_idx", "pool_use")

    def __init__(self, engine, kind, emit, tag=""):
        self.engine = engine
        self.kind = kind
        self.emit = emit
        self.deps = set()
        self.idx = 0
        self.sem_val = None
        self.forced = False
        self.tag = tag
        self.pool_idx = 0
        self.pool_use = 0


class Sched:
    def __init__(self, nc):
        self.nc = nc
        self.ops = {e: [] for e in ENGINES}
        self.all_ops = []
        self.key_w = {}
        self.key_r = {}
        self.barrier_tails = None
        self.barrier_pending = set()
        self.dma_counts = {}
        self.last_dma = {}

    def _dep(self, op, d):
        if d is None or d is op:
            return
        op.deps.add(d)

    def op(self, engine, reads, writes, emit, kind="c", tag=""):
        o = _Op(engine, kind, emit, tag)
        o.idx = len(self.all_ops)
        if self.barrier_tails is not None and engine in self.barrier_pending:
            for t in self.barrier_tails:
                self._dep(o, t)
            self.barrier_pending.discard(engine)
        for k in reads:
            self._dep(o, self.key_w.get(k))
        for k in writes:
            self._dep(o, self.key_w.get(k))
            for r in self.key_r.get(k, ()):
                self._dep(o, r)
        for k in reads:
            self.key_r.setdefault(k, []).append(o)
        for k in writes:
            self.key_w[k] = o
            self.key_r[k] = []
        self.ops[engine].append(o)
        self.all_ops.append(o)
        return o

    DMA_POOL = 6

    def dma(self, queue, reads, writes, emit, tag=""):
        o = self.op(queue, reads, writes, emit, kind="d", tag=tag)
        o.forced = True
        n = self.dma_counts.get(queue, 0)
        self.dma_counts[queue] = n + 1
        o.pool_idx = n % self.DMA_POOL
        o.pool_use = n // self.DMA_POOL
        o.sem_val = 16 * (o.pool_use + 1)
        self.last_dma[(queue, o.pool_idx)] = o
        return o

    def barrier(self):
        tails = []
        for e in ENGINES:
            for o in reversed(self.ops[e]):
                if o.kind == "c":
                    o.forced = True
                    tails.append(o)
                    break
        tails.extend(self.last_dma.values())
        self.barrier_tails = tails
        self.barrier_pending = set(ENGINES)
        self.key_w.clear()
        self.key_r.clear()

    STRICT_SAME_ENGINE = True

    def finalize(self):
        for o in self.all_ops:
            if o.kind == "c":
                o.forced = True
        for e in ENGINES:
            cnt_c = 0
            for o in self.ops[e]:
                if o.kind == "c" and o.forced:
                    cnt_c += 1
                    o.sem_val = cnt_c

    def _semkey(self, o):
        if o.kind == "c":
            return ("c", o.engine)
        return ("d", o.engine, o.pool_idx)

    def emit_all(self, stack):
        nc = self.nc
        sems = {}
        for e in ENGINES:
            if any(o.kind == "c" and o.forced for o in self.ops[e]):
                sems[("c", e)] = stack.enter_context(nc.semaphore(f"s_{e}"))
            nq = min(self.dma_counts.get(e, 0), self.DMA_POOL)
            for i in range(nq):
                sems[("d", e, i)] = stack.enter_context(nc.semaphore(f"q_{e}{i}"))
        self.sems = sems
        block = stack.enter_context(nc.Block())

        def mk(e):
            ops = self.ops[e]

            def body(eng):
                waited = {}

                def wait(sk, v):
                    if waited.get(sk, 0) >= v:
                        return
                    eng.wait_ge(sems[sk], v)
                    waited[sk] = v

                for o in ops:
                    wants = {}
                    if o.kind == "d" and o.pool_use > 0:
                        sk = self._semkey(o)
                        wants[sk] = 16 * o.pool_use
                    for d in o.deps:
                        if (not self.STRICT_SAME_ENGINE) and d.kind == "c" and d.engine == o.engine:
                            continue
                        sk = self._semkey(d)
                        v = d.sem_val
                        if v is None:
                            raise RuntimeError(f"dep without sem: {d.tag}")
                        if wants.get(sk, 0) < v:
                            wants[sk] = v
                    for sk, v in sorted(wants.items(), key=lambda kv: str(kv[0])):
                        wait(sk, v)
                    inst = o.emit(eng)
                    if o.forced:
                        if inst is None:
                            raise RuntimeError(f"op {o.tag} forced but emitted nothing")
                        inst.then_inc(sems[self._semkey(o)], 16 if o.kind == "d" else 1)

            return body

        for e in ENGINES:
            if self.ops[e]:
                getattr(block, e)(mk(e))




class Prog:

    def __init__(self):
        self.nc = bass.Bass("TRN2", target_bir_lowering=False, debug=False)
        self.s = Sched(self.nc)
        self.dram = {}
        self.inputs = {}
        self.outputs = []
        self._stack = None
        self.phase_stack = None
        self.phase_names = []
        self._phase_cms = {}
        self._global_cms = []
        self.buf = {}
        self.psum = {}
        self._uid = 0

    def uid(self):
        self._uid += 1
        return self._uid

    def input(self, name, percore):
        if name in self.inputs:
            return self.dram[name]
        if not isinstance(percore, list):
            percore = [percore] * 8
        a = percore[0]
        t = self.nc.dram_tensor(name, list(a.shape), mybir.dt.from_np(a.dtype), kind="ExternalInput").ap()
        self.inputs[name] = percore
        self.dram[name] = t
        return t

    def scratch(self, name, shape, dtype):
        if name in self.dram:
            return self.dram[name]
        t = self.nc.dram_tensor(name, list(shape), dtype, kind="Internal").ap()
        self.dram[name] = t
        return t

    def output(self, name, shape, dtype):
        if name in self.dram:
            return self.dram[name]
        t = self.nc.dram_tensor(name, list(shape), dtype, kind="ExternalOutput").ap()
        self.outputs.append(name)
        self.dram[name] = t
        return t

    def sbuf(self, name, shape, dtype, scope="phase", side="left"):
        if name in self.buf:
            return self.buf[name]
        if not hasattr(self, "_name_ver"):
            self._name_ver = {}
        v = self._name_ver.get(name, 0)
        self._name_ver[name] = v + 1
        real = name if v == 0 else f"{name}__{v}"
        cm = self.nc.sbuf_tensor(real, list(shape), dtype, side=side)
        ap = cm.__enter__().ap()
        self.buf[name] = ap
        if (scope == "global" or self.phase_stack is None) and side == "left":
            self._global_cms.append(cm)
        else:
            self._phase_cms[name] = cm
        return ap

    def free_buf(self, name):
        cm = self._phase_cms.pop(name, None)
        if cm is not None:
            cm.__exit__(None, None, None)
        self.buf.pop(name, None)

    def close_phase(self):
        for n in reversed(list(self._phase_cms)):
            self.free_buf(n)
        self.phase_stack = True

    def psum_tile(self, name, shape=(128, 512), dtype=F32):
        if name in self.psum:
            return self.psum[name]
        ap = self._stack.enter_context(self.nc.psum_tensor(name, list(shape), dtype)).ap()
        self.psum[name] = ap
        return ap

    def bank(self, i, rows=128, cols=512):
        if "PS" not in self.psum:
            self.psum["PS"] = self._stack.enter_context(
                self.nc.psum_tensor("PS", [128, 4096], F32)).ap()
        return self.psum["PS"][0:rows, i * 512:i * 512 + cols]

    def banks(self, i0, n):
        if "PS" not in self.psum:
            self.bank(0)
        return self.psum["PS"][:, i0 * 512:(i0 + n) * 512]

    def fconst(self, val):
        val = float(val)
        if not hasattr(self, "_fc"):
            self._fc = {}
            self._fc_arr = np.zeros((128, 64), np.float32)
            self.input("c_fvals", self._fc_arr)
            t = self.sbuf("fvals", (128, 64), F32, scope="global")
            self.s.dma("sync", [], ["fvals"],
                       lambda e: e.dma_start(out=t[...], in_=self.dram["c_fvals"][...]),
                       tag="ld_fvals")
        if val not in self._fc:
            idx = len(self._fc)
            assert idx < 64
            self._fc[val] = idx
            self._fc_arr[:, idx] = val
        return self.buf["fvals"][:, self._fc[val]:self._fc[val] + 1]


def to_np(dt):
    return mybir.dt.np(dt)




def emit_store_x(p):
    nc, s = p.nc, p.s
    x = p.buf["x"]
    yh = p.output("y", (D, T), F32)
    for g in range(NG):
        s.dma("sync", [("x", g)], [("out", g)],
              lambda e, g=g: e.dma_start(out=yh[g * 128:(g + 1) * 128, :], in_=x[:, g, :]),
              tag=f"stx{g}")
    s.op("gpsimd", [("out", g_) for g_ in range(NG)], [],
         lambda e: e.nop(), tag="final_wait")


def emit_const(p, name, arr, scope="global"):
    if name in p.buf:
        return p.buf[name]
    h = p.input("c_" + name, np.ascontiguousarray(arr))
    t = p.sbuf(name, arr.shape, mybir.dt.from_np(arr.dtype), scope=scope)
    p.s.dma("sync", [], [name], lambda e: e.dma_start(out=t[...], in_=h[...]), tag="ld_" + name)
    return t


def _consts(p):
    emit_const(p, "ones_col", np.ones((1, 128), np.float32))
    emit_const(p, "ones_k", np.ones((128, 1), np.float32))
    emit_const(p, "eye_bf", np.eye(128, dtype=BF))
    emit_const(p, "eye_f32", np.eye(128, dtype=np.float32))
    emit_const(p, "onerow_bf", np.ones((1, TT), BF))
    emit_const(p, "onecol_bf", np.ones((128, 1), BF))
    emit_const(p, "ones65", np.ones((65, 128), np.float32))
    p.fconst(0.0)
    p.sbuf("ln_muB", (128, 1, TT), F32, scope="global")
    p.sbuf("ln_rB", (128, 1, TT), F32, scope="global")
    p.sbuf("ln_rows", (1, 3, TT), F32, scope="global")
    p.sbuf("ln_sq", (128, 1, TT), F32, scope="global")
    p.sbuf("lntmp", (128, 1, TT), F32, scope="global")


def emit_ln_tile(p, u, tt, src=None, src_keys=None, eps=1e-5, nt=TT, src_offs=None):
    nc, s = p.nc, p.s
    if src is None:
        src = p.buf["x"]
        src_keys = [("x", g) for g in range(NG)]
    if src_offs is None:
        src_offs = [0] * NG
    b = 0
    t0 = tt * nt
    muB = p.sbuf("ln_muB", (128, 1, TT), F32, scope="global")
    rB = p.sbuf("ln_rB", (128, 1, TT), F32, scope="global")
    rows = p.sbuf("ln_rows", (1, 3, TT), F32, scope="global")
    sq = p.sbuf("ln_sq", (128, 1, TT), F32, scope="global")
    ps1 = p.bank(5, rows=1, cols=TT)
    ps2 = p.bank(6, rows=1, cols=TT)
    psb = p.bank(4, cols=TT)
    ones_k = p.buf["ones_k"]
    ones_col = p.buf["ones_col"]

    def mm_s1(e):
        last = None
        for g in range(NG):
            o_ = src_offs[g]
            last = e.matmul(ps1[:, :nt], ones_k[:, :], src[:, g, t0 + o_:t0 + o_ + nt],
                            start=(g == 0), stop=(g == NG - 1))
        return last
    s.op("tensor", src_keys, [("psb", 5)], mm_s1, tag=f"lns1_{u}{tt}")
    for g in range(NG):
        s.op("scalar", [src_keys[g]], [("ln_sq", 0)],
             lambda e, g=g, o_=src_offs[g]: e.square(out=sq[:, 0, :nt],
                                                     in_=src[:, g, t0 + o_:t0 + o_ + nt]),
             tag=f"lnsq{u}{tt}_{g}")
        s.op("tensor", [("ln_sq", 0)], [("psb", 6)],
             lambda e, g=g: e.matmul(ps2[:, :nt], ones_k[:, :], sq[:, 0, :nt],
                                     start=(g == 0), stop=(g == NG - 1)),
             tag=f"lns2_{u}{tt}_{g}")
    s.op("vector", [("psb", 5)], [("ln_mu_row",)],
         lambda e: e.tensor_scalar_mul(rows[:, 0, :nt], ps1[:, :nt], 1.0 / D), tag=f"lnmu{u}{tt}")
    s.op("vector", [("ln_mu_row",)], [("ln_var_row",)],
         lambda e: e.tensor_mul(rows[:, 1, :nt], rows[:, 0, :nt], rows[:, 0, :nt]),
         tag=f"lnv0{u}{tt}")
    s.op("vector", [("psb", 6), ("ln_var_row",)], [("ln_var_row",)],
         lambda e: e.scalar_tensor_tensor(out=rows[:, 1, :nt], in0=ps2[:, :nt], scalar=1.0 / D,
                                          in1=rows[:, 1, :nt], op0=ALU.mult, op1=ALU.subtract),
         tag=f"lnv1{u}{tt}")
    c_eps = p.fconst(eps)
    c_nh = p.fconst(-0.5)
    s.op("scalar", [("ln_var_row",), "fvals"], [("ln_r_row",)],
         lambda e: e.activation(out=rows[:, 2, :nt], in_=rows[:, 1, :nt], func=AF.Ln,
                                bias=c_eps[0:1, :]),
         tag=f"lnln{u}{tt}")
    s.op("scalar", [("ln_r_row",), "fvals"], [("ln_r_row2",)],
         lambda e: e.activation(out=rows[:, 1, :nt], in_=rows[:, 2, :nt], func=AF.Exp,
                                scale=c_nh[0:1, :]),
         tag=f"lnexp{u}{tt}")
    s.op("tensor", [("ln_mu_row",)], [("psb", 4)],
         lambda e: e.matmul(psb[:, :nt], ones_col[:, :], rows[:, 0, :nt], start=True, stop=True),
         tag=f"lnbm{u}{tt}")
    s.op("vector", [("psb", 4)], [("lnmuB", b)],
         lambda e: e.tensor_copy(out=muB[:, b, :nt], in_=psb[:, :nt]), tag=f"lnbe{u}{tt}")
    s.op("tensor", [("ln_r_row2",), ("lnmuB", b)], [("psb", 4)],
         lambda e: e.matmul(psb[:, :nt], ones_col[:, :], rows[:, 1, :nt], start=True, stop=True),
         tag=f"lnbm2{u}{tt}")
    s.op("vector", [("psb", 4)], [("lnB", b)],
         lambda e: e.tensor_copy(out=rB[:, b, :nt], in_=psb[:, :nt]), tag=f"lnbe2{u}{tt}")


def emit_ln_apply(p, u, tt, g, dst_sl, dkey, src=None, src_key=None, nt=TT,
                  tmp_name="lntmp", sub=None, src_off=0):
    nc, s = p.nc, p.s
    if src is None:
        src = p.buf["x"]
        src_key = ("x", g)
    b = 0
    t0 = tt * nt
    off, w = sub if sub is not None else (0, nt)
    muB = p.buf["ln_muB"]
    rB = p.buf["ln_rB"]
    veng = "vector" if g % 2 == 0 else "gpsimd"
    if veng == "vector":
        tmp = p.sbuf(tmp_name, (128, 1, TT), F32, scope="global")
    else:
        tmp_name = "ln_sq"
        tmp = p.sbuf("ln_sq", (128, 1, TT), F32, scope="global")
    tb = 0
    s.op(veng, [src_key, ("lnmuB", b), ("lnB", b)], [(tmp_name, tb)],
         lambda e: e.tensor_sub(tmp[:, tb, :w], src[:, g, t0 + off + src_off:t0 + off + src_off + w],
                                muB[:, b, off:off + w]),
         tag=f"lna{u}{tt}_{g}_{off}")
    s.op(veng, [(tmp_name, tb), ("lnB", b)], [dkey],
         lambda e: e.tensor_mul(dst_sl, tmp[:, tb, :w], rB[:, b, off:off + w]),
         tag=f"lnb{u}{tt}_{g}_{off}")




def emit_l3ffn(p, inp, call):
    nc, s = p.nc, p.s
    x = p.buf["x"]
    u = f"l3f{call}"
    wT = p.input("l3f_wT", np.ascontiguousarray(inp["L3_ffn_weight"].T))
    swT = p.sbuf("l3f_swT", (128, NG, D), BF16)
    wstg = p.sbuf("l3f_wstg", (128, 2, D), F32)
    if True:
        for kg in range(NG):
            s.dma("gpsimd", [], [("l3f_wstg", kg % 2)],
                  lambda e, kg=kg: e.dma_start(out=wstg[:, kg % 2, :], in_=wT[kg * 128:(kg + 1) * 128, :]),
                  tag=f"{u}ldw{kg}")
            s.op("scalar", [("l3f_wstg", kg % 2)], [("l3f_swT", kg)],
                 lambda e, kg=kg: e.sign(out=swT[:, kg, :], in_=wstg[:, kg % 2, :]),
                 tag=f"{u}sgn{kg}")
    xn = p.sbuf("l3f_xn", (128, NG, TT), BF16)
    for tt in range(NTT):
        t0 = tt * TT
        emit_ln_tile(p, u, tt)
        for g in range(NG):
            emit_ln_apply(p, u, tt, g, xn[:, g, :], ("l3f_xn", g, u))
        for m in range(NG):
            ps = p.bank(m % 4)
            def mm(e, m=m, t0=t0, ps=ps):
                last = None
                for kg in range(NG):
                    last = e.matmul(ps[:, :], swT[:, kg, m * 128:(m + 1) * 128], xn[:, kg, :],
                                    start=(kg == 0), stop=(kg == NG - 1))
                return last
            s.op("tensor",
                 [("l3f_xn", g, u) for g in range(NG)] + [("l3f_swT", kg) for kg in range(NG)],
                 [("pmm", m % 4)], mm, tag=f"{u}mm{tt}_{m}")
            s.op("vector", [("pmm", m % 4)], [("x", m)],
                 lambda e, m=m, t0=t0, ps=ps: e.scalar_tensor_tensor(
                     out=x[:, m, t0:t0 + TT], in0=ps[:, :], scalar=0.5, in1=x[:, m, t0:t0 + TT],
                     op0=ALU.mult, op1=ALU.add),
                 tag=f"{u}acc{tt}_{m}")


def emit_to_bf16(p, name, host_arr, logical_shape=None):
    nc, s = p.nc, p.s
    key = name + "_bf"
    if key in p.dram:
        return p.dram[key]
    n = host_arr.size
    C = 4096
    while n % (128 * C) != 0:
        C //= 2
    R = n // C
    h2 = np.ascontiguousarray(host_arr).reshape(R, C)
    h = p.input(name, h2)
    dst = p.scratch(key, list(logical_shape or host_arr.shape), BF16)
    dst2 = dst.rearrange(" ".join(f"a{i}" for i in range(len(dst.shape))) + " -> ("
                         + " ".join(f"a{i}" for i in range(len(dst.shape))) + ")").rearrange(
        "(r c) -> r c", c=C)
    srcdt = mybir.dt.from_np(host_arr.dtype)
    sn = "w_cvt_stg_" + str(srcdt)
    src_t = p.sbuf(sn, (128, 2, 4096), srcdt, side="right")
    stgb = p.sbuf("w_cvt_stgb", (128, 2, 4096), BF16, side="right")
    for r in range(R // 128):
        b = r % 2
        s.dma("sync", [("wcvt", b)], [("wcvt_in", b)],
              lambda e, r=r, b=b: e.dma_start(
                  out=src_t[:, b, :C], in_=h[r * 128:(r + 1) * 128, :]),
              tag=f"cv_ld_{name}_{r}")
        s.op("scalar", [("wcvt_in", b)], [("wcvt", b)],
             lambda e, r=r, b=b: e.copy(out=stgb[:, b, :C], in_=src_t[:, b, :C]),
             tag=f"cv_{name}_{r}")
        s.dma("gpsimd", [("wcvt", b)], [("wcvt_out", b)],
              lambda e, r=r, b=b: e.dma_start(out=dst2[r * 128:(r + 1) * 128, :],
                                              in_=stgb[:, b, :C]),
              tag=f"cv_st_{name}_{r}")
    return dst


def _free_cvt(p, u):
    p.s.barrier()
    p.free_buf(u + "_brst")
    p.free_buf("w_cvt_stgb")
    p.free_buf("w_cvt_stg_" + str(F32))
    p.free_buf("w_cvt_stg_" + str(F16))




def emit_l0ffn(p, inp, call):
    nc, s = p.nc, p.s
    x = p.buf["x"]
    u = f"l0f{call}"
    w1 = np.asarray(inp["L0_ffn_w1"])
    w2 = np.asarray(inp["L0_ffn_w2"])
    w1t = np.ascontiguousarray(w1.reshape(8, 8, 128, 32, 128).transpose(0, 3, 2, 1, 4))
    w1bf = emit_to_bf16(p, "l0f_w1", w1t, (8, 32, 128, 8, 128))
    w2bf = emit_to_bf16(p, "l0f_w2", w2, (8, 32, 128, 1024))
    gwbf = emit_to_bf16(p, "l0f_gw", np.ascontiguousarray(np.asarray(inp["L0_ffn_gate_w"])))
    b2bf = emit_to_bf16(p, "l0f_b2", np.ascontiguousarray(np.asarray(inp["L0_ffn_b2"])))
    b1h = p.input("l0f_b1", np.ascontiguousarray(
        np.asarray(inp["L0_ffn_b1"]).reshape(8, 32, 128).transpose(0, 2, 1)))
    _free_cvt(p, u)

    gwb = p.sbuf(u + "_gwb", (128, NG, 8), BF16)
    b1c = p.sbuf(u + "_b1c", (128, 8, 32), F32)
    b2b = p.sbuf(u + "_b2b", (8, D), BF16)
    s.dma("sync", [], [(u, "gwb")],
          lambda e: e.dma_start(out=gwb[...], in_=gwbf.rearrange("(g p) e -> p g e", p=128)),
          tag=u + "ldgw")
    s.dma("sync", [], [(u, "b1c")], lambda e: e.dma_start(out=b1c[...], in_=b1h[...]),
          tag=u + "ldb1")
    s.dma("sync", [], [(u, "b2b")], lambda e: e.dma_start(out=b2b[...], in_=b2bf[...]),
          tag=u + "ldb2")

    xnt = p.sbuf(u + "_xnt", (128, NG, TT), BF16)
    glog = p.sbuf(u + "_gl", (128, 4, 8), F32)
    gm = p.sbuf(u + "_gm", (128, 4, 5), F32)
    geq = p.sbuf(u + "_geq", (128, 3, 4, 8), F32)
    wT = p.sbuf(u + "_wT", (8, TT), BF16)
    h_sb = p.sbuf(u + "_h", (128, 32, TT), BF16)
    y_acc = p.sbuf(u + "_y", (128, 4, D), BF16)
    w1s = p.sbuf(u + "_w1s", (128, 2, NG, 128), BF16)
    w2s = p.sbuf(u + "_w2s", (128, 2, D), BF16)

    for tt in range(NTT):
        t0 = tt * TT
        emit_ln_tile(p, u, tt)
        for g in range(NG):
            emit_ln_apply(p, u, tt, g, xnt[:, g, :], (u, "xnt", g))
        xk = [(u, "xnt", g) for g in range(NG)]
        def mmg(e):
            last = None
            for mt in range(4):
                for kg in range(NG):
                    last = e.matmul(p.bank(4)[:, mt * 8:mt * 8 + 8],
                                    xnt[:, kg, mt * 128:(mt + 1) * 128], gwb[:, kg, :],
                                    start=(kg == 0), stop=(kg == NG - 1),
                                    skip_group_check=True)
            return last
        s.op("tensor", xk + [(u, "gwb")], [("psb", 4)], mmg, tag=f"{u}gmm{tt}")
        s.op("vector", [("psb", 4)], [(u, "gl")],
             lambda e: e.tensor_copy(out=glog.rearrange("p a b -> p (a b)"),
                                     in_=p.bank(4)[:, 0:32]), tag=f"{u}gev{tt}")
        for mt in range(4):
            gl = glog[:, mt, :]
            ge = "vector"
            s.op("vector", [(u, "gl")], [(u, "gm", mt, 0)],
                 lambda e, gl=gl, mt=mt: e.reduce_max(gm[:, mt, 0:1], gl, axis=AX.X),
                 tag=f"{u}m1_{tt}{mt}")
            s.op("vector", [(u, "gl"), (u, "gm", mt, 0)], [(u, "geq", 1, mt)],
                 lambda e, gl=gl, mt=mt: e.tensor_scalar(out=geq[:, 1, mt, :], in0=gl,
                                                         scalar1=gm[:, mt, 0:1], scalar2=None,
                                                         op0=ALU.is_equal),
                 tag=f"{u}eq1_{tt}{mt}")
            s.op(ge, [(u, "geq", 1, mt), (u, "gl")], [(u, "geq", 2, mt)],
                 lambda e, gl=gl, mt=mt: e.scalar_tensor_tensor(
                     out=geq[:, 2, mt, :], in0=geq[:, 1, mt, :], scalar=-1e30, in1=gl,
                     op0=ALU.mult, op1=ALU.add),
                 tag=f"{u}msk_{tt}{mt}")
            s.op("vector", [(u, "geq", 2, mt)], [(u, "gm", mt, 1)],
                 lambda e, mt=mt: e.reduce_max(gm[:, mt, 1:2], geq[:, 2, mt, :], axis=AX.X),
                 tag=f"{u}m2_{tt}{mt}")
            s.op(ge, [(u, "gm", mt, 0), (u, "gm", mt, 1)], [(u, "gm", mt, 4)],
                 lambda e, mt=mt: e.tensor_sub(gm[:, mt, 4:5], gm[:, mt, 0:1], gm[:, mt, 1:2]),
                 tag=f"{u}d_{tt}{mt}")
            s.op("scalar", [(u, "gm", mt, 4)], [(u, "gm", mt, 2)],
                 lambda e, mt=mt: e.activation(out=gm[:, mt, 2:3], in_=gm[:, mt, 4:5],
                                               func=AF.Sigmoid),
                 tag=f"{u}s1_{tt}{mt}")
            s.op(ge, [(u, "gm", mt, 2)], [(u, "gm", mt, 3)],
                 lambda e, mt=mt: e.tensor_scalar(out=gm[:, mt, 3:4], in0=gm[:, mt, 2:3],
                                                  scalar1=-1.0, scalar2=1.0,
                                                  op0=ALU.mult, op1=ALU.add),
                 tag=f"{u}s2_{tt}{mt}")
            s.op("vector", [(u, "geq", 1, mt), (u, "gl"), (u, "gm", mt, 1)], [(u, "geq", 0, mt)],
                 lambda e, gl=gl, mt=mt: e.tensor_scalar(out=geq[:, 0, mt, :], in0=gl,
                                                         scalar1=gm[:, mt, 1:2], scalar2=None,
                                                         op0=ALU.is_equal),
                 tag=f"{u}eq2_{tt}{mt}")
            s.op(ge, [(u, "geq", 0, mt), (u, "geq", 1, mt)], [(u, "geq", 2, mt)],
                 lambda e, mt=mt: e.tensor_mul(geq[:, 2, mt, :], geq[:, 0, mt, :], geq[:, 1, mt, :]),
                 tag=f"{u}ee_{tt}{mt}")
            s.op(ge, [(u, "geq", 0, mt), (u, "geq", 2, mt)], [(u, "geq", 0, mt)],
                 lambda e, mt=mt: e.tensor_sub(geq[:, 0, mt, :], geq[:, 0, mt, :], geq[:, 2, mt, :]),
                 tag=f"{u}ep_{tt}{mt}")
            s.op("vector", [(u, "geq", 0, mt), (u, "gm", mt, 3)], [(u, "geq", 0, mt)],
                 lambda e, mt=mt: e.tensor_scalar_mul(geq[:, 0, mt, :], geq[:, 0, mt, :],
                                                      gm[:, mt, 3:4]),
                 tag=f"{u}w2_{tt}{mt}")
            s.op("vector", [(u, "geq", 1, mt), (u, "gm", mt, 2), (u, "geq", 0, mt)], [(u, "gw_w", mt)],
                 lambda e, mt=mt: e.scalar_tensor_tensor(
                     out=geq[:, 1, mt, :], in0=geq[:, 1, mt, :], scalar=gm[:, mt, 2:3],
                     in1=geq[:, 0, mt, :], op0=ALU.mult, op1=ALU.add),
                 tag=f"{u}w_{tt}{mt}")
            wbk = 5 + mt % 2
            s.op("tensor", [(u, "gw_w", mt)], [("psb", wbk)],
                 lambda e, mt=mt, wbk=wbk: e.transpose(p.bank(wbk, rows=8, cols=128),
                                                       geq[:, 1, mt, :],
                                                       p.buf["eye_f32"][...]),
                 tag=f"{u}wT_{tt}{mt}")
            s.op("vector" if mt % 2 == 0 else "scalar", [("psb", wbk)], [(u, "wT")],
                 lambda e, mt=mt, wbk=wbk: (e.tensor_copy if mt % 2 == 0 else e.copy)(
                     out=wT[:, mt * 128:(mt + 1) * 128],
                     in_=p.bank(wbk, rows=8, cols=128)),
                 tag=f"{u}wTe_{tt}{mt}")
        for mt in range(4):
            for ds in range(2):
                bk = mt * 2 + ds
                s.op("tensor", [(u, "wT"), (u, "b2b")], [("psb", bk)],
                     lambda e, mt=mt, ds=ds, bk=bk: e.matmul(
                         p.bank(bk), wT[:, mt * 128:(mt + 1) * 128],
                         b2b[:, ds * 512:(ds + 1) * 512], start=True, stop=True),
                     tag=f"{u}b2m{tt}_{mt}{ds}")
                s.op("vector" if ds == 0 else "scalar", [("psb", bk)], [(u, "y", mt, ds)],
                     lambda e, mt=mt, ds=ds, bk=bk: (e.tensor_copy if ds == 0 else e.copy)(
                         out=y_acc[:, mt, ds * 512:(ds + 1) * 512], in_=p.bank(bk)),
                     tag=f"{u}b2e{tt}_{mt}{ds}")
        for ex in range(8):
            for fs in range(32):
                wb = fs % 2
                s.dma("sync", [(u, "w1s", wb)], [(u, "w1s", wb)],
                      lambda e, ex=ex, fs=fs, wb=wb: e.dma_start(
                          out=w1s[:, wb], in_=w1bf[ex, fs]),
                      tag=f"{u}lw1_{tt}_{ex}_{fs}")
                bk = 4 + fs % 2
                def mmh(e, fs=fs, wb=wb, bk=bk):
                    last = None
                    for kg in range(NG):
                        last = e.matmul(p.bank(bk), w1s[:, wb, kg, :], xnt[:, kg, :],
                                        start=(kg == 0), stop=(kg == NG - 1))
                    return last
                s.op("tensor", [(u, "w1s", wb)] + xk, [("psb", bk)], mmh,
                     tag=f"{u}h{tt}_{ex}_{fs}")
                s.op("scalar", [("psb", bk), (u, "b1c")], [(u, "h", fs)],
                     lambda e, ex=ex, fs=fs, bk=bk: e.activation(
                         out=h_sb[:, fs, :], in_=p.bank(bk), func=AF.Gelu,
                         bias=b1c[:, ex, fs:fs + 1]),
                     tag=f"{u}g{tt}_{ex}_{fs}")
            for fs in range(32):
                wb = fs % 2
                s.dma("gpsimd" if fs % 2 == 0 else "scalar",
                      [(u, "w2s", wb)], [(u, "w2s", wb)],
                      lambda e, ex=ex, fs=fs, wb=wb: e.dma_start(
                          out=w2s[:, wb, :], in_=w2bf[ex, fs]),
                      tag=f"{u}lw2_{tt}_{ex}_{fs}")
                def mmy(e, fs=fs, wb=wb):
                    last = None
                    for mt in range(4):
                        for ds in range(2):
                            last = e.matmul(p.bank(mt * 2 + ds),
                                            h_sb[:, fs, mt * 128:(mt + 1) * 128],
                                            w2s[:, wb, ds * 512:(ds + 1) * 512],
                                            start=(fs == 0), stop=(fs == 31),
                                            skip_group_check=True)
                    return last
                s.op("tensor", [(u, "w2s", wb), (u, "h", fs)],
                     [("psb", b) for b in range(8)], mmy, tag=f"{u}y{tt}_{ex}_{fs}")
            for mt, ds in ((2, 0), (2, 1), (0, 0), (0, 1), (1, 0), (1, 1), (3, 0), (3, 1)):
                    bk = mt * 2 + ds
                    s.op("vector",
                         [("psb", bk), (u, "gw_w", mt), (u, "y", mt, ds)],
                         [(u, "y", mt, ds)],
                         lambda e, ex=ex, mt=mt, ds=ds, bk=bk: e.scalar_tensor_tensor(
                             out=y_acc[:, mt, ds * 512:(ds + 1) * 512], in0=p.bank(bk),
                             scalar=geq[:, 1, mt, ex:ex + 1],
                             in1=y_acc[:, mt, ds * 512:(ds + 1) * 512],
                             op0=ALU.mult, op1=ALU.add),
                         tag=f"{u}ya{tt}_{ex}_{mt}{ds}")
        for mt in range(4):
            for g in range(NG):
                bk = g % 2
                s.op("tensor", [(u, "y", mt, g // 4)], [("psb", bk)],
                     lambda e, mt=mt, g=g, bk=bk: e.transpose(
                         p.bank(bk).bitcast(BF16)[:, :128], y_acc[:, mt, g * 128:(g + 1) * 128],
                         p.buf["eye_bf"][...]),
                     tag=f"{u}yT{tt}_{mt}{g}")
                s.op("vector", [("psb", bk)], [("x", g)],
                     lambda e, mt=mt, g=g, bk=bk, t0=t0: e.scalar_tensor_tensor(
                         out=x[:, g, t0 + mt * 128:t0 + (mt + 1) * 128],
                         in0=p.bank(bk).bitcast(BF16)[:, :128], scalar=0.5,
                         in1=x[:, g, t0 + mt * 128:t0 + (mt + 1) * 128],
                         op0=ALU.mult, op1=ALU.add),
                     tag=f"{u}xa{tt}_{mt}{g}")




def emit_mha(p, u, cfg):
    nc, s = p.nc, p.s
    x = p.buf["x"]
    wq_bf = emit_to_bf16(p, u + "_wq", cfg["wq"])
    wk_bf = emit_to_bf16(p, u + "_wk", cfg["wk"])
    wv_bf = emit_to_bf16(p, u + "_wv", cfg["wv"])
    wo_bf = emit_to_bf16(p, u + "_wo", cfg["wo"])
    has_bias = cfg.get("bias_row") is not None
    if has_bias:
        brh = p.input(u + "_br", np.ascontiguousarray(cfg["bias_row"]))
        brst = p.sbuf(u + "_brst", (1, 4 * D), mybir.dt.from_np(cfg["bias_row"].dtype), side="right")
        brow = p.sbuf(u + "_brow", (1, 4 * D), BF16)
        s.dma("sync", [], [(u, "brst")], lambda e: e.dma_start(out=brst[...], in_=brh[...]),
              tag=u + "ldb")
        s.op("scalar", [(u, "brst")], [(u, "brow")],
             lambda e: e.copy(out=brow[...], in_=brst[...]), tag=u + "cvb")
    s.barrier()
    p.free_buf(u + "_brst")
    p.free_buf("w_cvt_stgb")
    p.free_buf("w_cvt_stg_" + str(F32))
    p.free_buf("w_cvt_stg_" + str(F16))
    onerow = p.buf["onerow_bf"]
    onecol_bf = p.buf["onecol_bf"]
    G = np.where((np.arange(896)[None, :] - np.arange(128)[:, None]) >= 384,
                 np.float32(0), np.float32(cfg["mask_val"])).astype(BF)
    mkG = emit_const(p, u + "_mkG", G, scope="phase")
    mks = [mkG[:, 384 - 128 * jj:896 - 128 * jj] for jj in range(4)]
    xnh = p.scratch(u + "_xnh", (D, T), BF16)
    oh = p.scratch(u + "_oh", (D, T), BF16)
    xnh_r = xnh.rearrange("(g p) t -> p g t", p=128)
    oh_r = oh.rearrange("(g p) t -> p g t", p=128)
    qsc = p.fconst(cfg["q_scale"])

    xs = p.sbuf(u + "_xs", (128, 1, NG, TT), BF16)
    qb = p.sbuf(u + "_qb", (128, T), BF16)
    kb = p.sbuf(u + "_kb", (128, T), BF16)
    vb = p.sbuf(u + "_vb", (128, 32, 130), BF16)
    pts = p.sbuf(u + "_pts", (128, 7, TT), BF16)
    ost = p.sbuf(u + "_ost", (64, 4, TT), BF16)
    rrowS = p.sbuf(u + "_rrowS", (65, TT), F32)
    rbsS = p.sbuf(u + "_rbsS", (64, TT), BF16)
    wsb = p.sbuf(u + "_wsb", (128, NG, 3, 128), BF16)
    for oc in (64, 129):
        s.op("vector", [(u, "vb")], [(u, "vb")],
             lambda e, oc=oc: e.memset(vb[:, :, oc:oc + 1], 1.0), tag=f"{u}vone{oc}")

    for tt in range(NTT):
        emit_ln_tile(p, u, tt)
        for g in range(NG):
            emit_ln_apply(p, u, tt, g, xs[:, 0, g, :], (u, "xs", 0))
        s.dma("sync", [(u, "xs", 0)], [(u, "xnh", tt)],
              lambda e, tt=tt: e.dma_start(out=xnh_r[:, :, tt * TT:(tt + 1) * TT],
                                           in_=xs[:, 0]),
              tag=f"{u}stxn{tt}")

    for hp in range(8):
        c0 = hp * 128
        for kg in range(NG):
            for wi, wsrc in enumerate((wq_bf, wk_bf, wv_bf)):
                s.dma("scalar", [(u, "wsb", kg, wi)], [(u, "wsb", kg, wi)],
                      lambda e, kg=kg, wi=wi, wsrc=wsrc, c0=c0: e.dma_start(
                          out=wsb[:, kg, wi, :], in_=wsrc[kg * 128:(kg + 1) * 128, c0:c0 + 128]),
                      tag=f"{u}ldw{hp}_{kg}_{wi}")
        for tt in range(NTT):
            bsl = 0
            t0 = tt * TT
            s.dma("gpsimd", [(u, "xnh", tt), (u, "xs", bsl)], [(u, "xs", bsl)],
                  lambda e, tt=tt, bsl=bsl: e.dma_start(
                      out=xs[:, bsl], in_=xnh_r[:, :, tt * TT:(tt + 1) * TT]),
                  tag=f"{u}ldxn{hp}_{tt}")
            wk_keys = [(u, "wsb", kg, wi) for kg in range(NG) for wi in range(3)]
            for wi, dstb, bkr in ((0, qb, 0), (1, kb, 1)):
                def mmqk(e, wi=wi, bkr=bkr, bsl=bsl, c0=c0):
                    ps = p.bank(bkr)
                    last = None
                    for kg in range(NG):
                        last = e.matmul(ps, wsb[:, kg, wi, :], xs[:, bsl, kg, :],
                                        start=(kg == 0), stop=(not has_bias and kg == NG - 1))
                    if has_bias:
                        last = e.matmul(ps, brow[0:1, wi * D + c0:wi * D + c0 + 128],
                                        onerow[:, :], start=False, stop=True)
                    return last
                s.op("tensor", wk_keys + [(u, "xs", bsl), (u, "brow")], [("psb", bkr)],
                     mmqk, tag=f"{u}qk{hp}_{tt}_{wi}")
                sc = qsc if wi == 0 else None
                def evqk(e, wi=wi, bkr=bkr, dstb=dstb, t0=t0, sc=sc):
                    if sc is not None:
                        return e.activation(out=dstb[:, t0:t0 + TT], in_=p.bank(bkr),
                                            func=AF.Copy, scale=sc)
                    return e.copy(out=dstb[:, t0:t0 + TT], in_=p.bank(bkr))
                s.op("scalar", [("psb", bkr), "fvals"], [(u, "qb" if wi == 0 else "kb")],
                     evqk, tag=f"{u}eqk{hp}_{tt}_{wi}")
            for j in range(4):
                bkr = 2 + j % 2
                def mmv(e, j=j, bkr=bkr, bsl=bsl, c0=c0):
                    ps = p.bank(bkr, cols=128)
                    last = None
                    for kg in range(NG):
                        last = e.matmul(ps, xs[:, bsl, kg, j * 128:(j + 1) * 128], wsb[:, kg, 2, :],
                                        start=(kg == 0), stop=(not has_bias and kg == NG - 1))
                    if has_bias:
                        last = e.matmul(ps, onerow[:, :128], brow[0:1, 2 * D + c0:2 * D + c0 + 128],
                                        start=False, stop=True)
                    return last
                s.op("tensor", wk_keys + [(u, "xs", bsl), (u, "brow")], [("psb", bkr)],
                     mmv, tag=f"{u}v{hp}_{tt}_{j}")
                def evv(e, j=j, tt=tt, bkr=bkr):
                    e.copy(out=vb[:, tt * 4 + j, 0:64], in_=p.bank(bkr, cols=128)[:, 0:64])
                    return e.copy(out=vb[:, tt * 4 + j, 65:129],
                                  in_=p.bank(bkr, cols=128)[:, 64:128])
                s.op("scalar", [("psb", bkr)], [(u, "vb")], evv, tag=f"{u}ev{hp}_{tt}_{j}")
        for h in (0, 1):
            r0 = h * 64
            vc0 = 65 * h
            for sp in range(8):
                q0 = sp * TT
                nb = 4 * (sp + 1)
                PS = p.psum["PS"]
                chunks = []
                blk = 0
                ci = 0
                while blk < nb:
                    cn = 4 if ci % 2 == 0 else 3
                    cn = min(cn, nb - blk)
                    b0 = 0 if ci % 2 == 0 else 4
                    chunks.append((blk, cn, b0))
                    blk += cn
                    ci += 1
                def emit_S(blk, cn, b0, r0=r0, q0=q0, nb=nb):
                    def mmst(e, blk=blk, cn=cn, b0=b0, r0=r0, q0=q0):
                        last = None
                        for i in range(cn):
                            last = e.matmul(p.bank(b0 + i),
                                            kb[r0:r0 + 64, (blk + i) * 128:(blk + i + 1) * 128],
                                            qb[r0:r0 + 64, q0:q0 + TT],
                                            start=True, stop=True)
                        return last
                    s.op("tensor", [(u, "qb"), (u, "kb")],
                         [("psb", b0 + i) for i in range(cn)], mmst,
                         tag=f"{u}S{hp}_{h}_{sp}_{blk}")
                    for i in range(cn):
                        jj = (blk + i) - (nb - 4)
                        if jj >= 0:
                            s.op("vector", [u + "_mkG", ("psb", b0 + i)], [("psb", b0 + i)],
                                 lambda e, i=i, b0=b0, jj=jj: e.tensor_add(
                                     p.bank(b0 + i), p.bank(b0 + i), mks[jj]),
                                 tag=f"{u}mk{hp}_{h}_{sp}_{blk + i}")
                    s.op("scalar", [("psb", b0 + i) for i in range(cn)], [(u, "pts", b0)],
                         lambda e, cn=cn, b0=b0: e.activation(
                             out=pts[:, b0:b0 + cn, :], in_=PS[:, b0 * 512:(b0 + cn) * 512],
                             func=AF.Exp),
                         tag=f"{u}exp{hp}_{h}_{sp}_{blk}")
                def emit_PV(blk, cn, b0, vc0=vc0, nb=nb):
                    def mmpv(e, blk=blk, cn=cn, b0=b0, vc0=vc0, nb=nb):
                        last = None
                        for i in range(cn):
                            last = e.matmul(PS[0:65, 7 * 512:8 * 512],
                                            vb[:, blk + i, vc0:vc0 + 65],
                                            pts[:, b0 + i, :],
                                            start=(blk + i == 0), stop=(blk + i == nb - 1),
                                            skip_group_check=True)
                        return last
                    s.op("tensor", [(u, "pts", b0), (u, "vb")], [("psb", 7)], mmpv,
                         tag=f"{u}PV{hp}_{h}_{sp}_{blk}")
                for i, (blk, cn, b0) in enumerate(chunks):
                    emit_S(blk, cn, b0)
                    if i >= 1:
                        emit_PV(*chunks[i - 1])
                emit_PV(*chunks[-1])
                s.op("vector", [("psb", 7)], [(u, "rrowS")],
                     lambda e: e.reciprocal(rrowS[64:65, :], PS[64:65, 7 * 512:8 * 512]),
                     tag=f"{u}ri{hp}_{h}_{sp}")
                s.op("tensor", [(u, "rrowS"), "ones65"], [("psb", 0)],
                     lambda e: e.matmul(p.bank(0, rows=64, cols=TT),
                                        p.buf["ones65"][64:65, 0:64],
                                        rrowS[64:65, :], start=True, stop=True),
                     tag=f"{u}rb{hp}_{h}_{sp}")
                ob2 = (2 * sp + h) % 4
                s.op("scalar", [("psb", 0)], [(u, "rbsS")],
                     lambda e: e.copy(out=rbsS[0:64, :], in_=p.bank(0, rows=64, cols=TT)),
                     tag=f"{u}rbe{hp}_{h}_{sp}")
                s.op("vector", [("psb", 7), (u, "rbsS"), (u, "ost", h, ob2)], [(u, "ost", h, ob2)],
                     lambda e, ob2=ob2: e.tensor_mul(ost[0:64, ob2, :],
                                                     PS[0:64, 7 * 512:8 * 512],
                                                     rbsS[0:64, :]),
                     tag=f"{u}oe{hp}_{h}_{sp}")
                s.dma("sync", [(u, "ost", h, ob2)], [(u, "oh", hp)],
                      lambda e, hp=hp, r0=r0, sp=sp, ob2=ob2: e.dma_start(
                          out=oh[hp * 128 + r0:hp * 128 + r0 + 64, sp * TT:(sp + 1) * TT],
                          in_=ost[0:64, ob2, :]),
                      tag=f"{u}sto{hp}_{h}_{sp}")

    if cfg.get("skip_out"):
        return
    for tt in range(NTT):
        bsl = 0
        t0 = tt * TT
        s.dma("gpsimd", [(u, "oh", hp_) for hp_ in range(8)] + [(u, "xs", bsl)], [(u, "xs", bsl)],
              lambda e, tt=tt, bsl=bsl: e.dma_start(
                  out=xs[:, bsl], in_=oh_r[:, :, tt * TT:(tt + 1) * TT]),
              tag=f"{u}ldo{tt}")
        for m in range(NG):
            wb2 = m % 2
            s.dma("scalar", [(u, "wos", wb2)], [(u, "wos", wb2)],
                  lambda e, m=m, wb2=wb2: e.dma_start(
                      out=wsb[:, :, wb2, :], in_=wo_bf.rearrange("(g p) d -> p g d", p=128)[:, :, m * 128:(m + 1) * 128]),
                  tag=f"{u}ldwo{tt}_{m}")
            def mmo(e, m=m, wb2=wb2, bsl=bsl):
                ps = p.bank(m % 4)
                last = None
                for kg in range(NG):
                    last = e.matmul(ps, wsb[:, kg, wb2, :], xs[:, bsl, kg, :],
                                    start=(kg == 0), stop=(not has_bias and kg == NG - 1))
                if has_bias:
                    last = e.matmul(ps, brow[0:1, 3 * D + m * 128:3 * D + (m + 1) * 128],
                                    onerow[:, :], start=False, stop=True)
                return last
            s.op("tensor", [(u, "wos", wb2), (u, "xs", bsl), (u, "brow")], [("psb", m % 4)],
                 mmo, tag=f"{u}o{tt}_{m}")
            s.op("vector", [("psb", m % 4)], [("x", m)],
                 lambda e, m=m, t0=t0: e.tensor_add(x[:, m, t0:t0 + TT], x[:, m, t0:t0 + TT],
                                                    p.bank(m % 4)),
                 tag=f"{u}acc{tt}_{m}")


def emit_l0attn(p, inp):
    f = lambda k: np.asarray(inp["L0_attn_" + k])
    bias_row = np.concatenate([f("bq"), f("bk"), f("bv"), f("bo")])[None, :]
    cfg = dict(wq=f("wq").T, wk=f("wk").T, wv=f("wv").T, wo=f("wo").T,
               bias_row=np.ascontiguousarray(bias_row), q_scale=float(np.float16(0.125)),
               mask_val=-10000.0)
    emit_mha(p, "l0a", cfg)


def emit_l3attn(p, inp):
    wqkv = np.asarray(inp["L3_attn_w_qkv"])
    wq = np.ascontiguousarray(wqkv.reshape(1024, 16, 3, 64)[:, :, 0, :].reshape(1024, 1024))
    wk = np.ascontiguousarray(wqkv.reshape(1024, 16, 3, 64)[:, :, 1, :].reshape(1024, 1024))
    wv = np.ascontiguousarray(wqkv.reshape(1024, 16, 3, 64)[:, :, 2, :].reshape(1024, 1024))
    cfg = dict(wq=wq, wk=wk, wv=wv, wo=np.asarray(inp["L3_attn_w_o"]),
               bias_row=None, q_scale=1.0 / 32.0, mask_val=-1e30)
    emit_mha(p, "l3a", cfg)




def emit_l2ffn(p, inp, call):
    nc, s = p.nc, p.s
    x = p.buf["x"]
    u = f"l2f{call}"
    W = 2040
    binh = p.input("l2f_bin", np.ascontiguousarray(np.asarray(inp["L2_ffn_b_in"])[None, :]))
    bins = p.sbuf(u + "_bins", (1, 2040), F32, side="right")
    s.dma("sync", [], [(u, "bins")], lambda e: e.dma_start(out=bins[...], in_=binh[...]),
          tag=u + "ldbin")
    winT = emit_to_bf16(p, "l2f_winT", np.ascontiguousarray(np.asarray(inp["L2_ffn_W_in"]).T))
    wpad = np.zeros((2048, 1024), np.float32)
    wpad[:W] = np.asarray(inp["L2_ffn_W_out"]).T
    wot = np.ascontiguousarray(wpad.reshape(16, 128, 8, 128).transpose(2, 1, 0, 3))
    woutT = emit_to_bf16(p, "l2f_woutT", wot, (8, 128, 16, 128))
    _free_cvt(p, u)
    binb = p.sbuf(u + "_binb", (1, W), BF16)
    s.op("scalar", [(u, "bins")], [(u, "binb")],
         lambda e: e.copy(out=binb[...], in_=bins[...]), tag=u + "cvbin")
    p.s.barrier()
    p.free_buf(u + "_bins")
    onerow = p.buf["onerow_bf"]

    xnt = p.sbuf(u + "_xnt", (128, NG, 128), BF16)
    acts = p.sbuf(u + "_acts", (128, 2048), BF16)
    dec = p.sbuf(u + "_dec", (128, W), BF16)
    vis = p.sbuf(u + "_vis", (128, W), BF16)
    mT = p.sbuf(u + "_mT", (128, 16, 128), BF16)
    wsl = p.sbuf(u + "_wsl", (128, 2, 16, 128), BF16)
    win_s = p.sbuf(u + "_win", (128, NG, W), BF16)
    NSL = (512, 512, 512, 504)
    for kg in range(NG):
        s.dma("gpsimd", [], [(u, "win", kg)],
              lambda e, kg=kg: e.dma_start(out=win_s[:, kg, :],
                                           in_=winT[kg * 128:(kg + 1) * 128, :]),
              tag=f"{u}ldwin{kg}")
    s.op("vector", [], [(u, "acts")],
         lambda e: e.memset(acts[:, W:], 0.0), tag=u + "mz")
    for tt in range(NTT):
        t0 = tt * TT
        emit_ln_tile(p, u, tt)
        for mt in range(4):
            tc = t0 + mt * 128
            for g in range(NG):
                emit_ln_apply(p, u, tt, g, xnt[:, g, :], (u, "xnt", g), sub=(mt * 128, 128))
            xk = [(u, "xnt", g) for g in range(NG)]
            def mml(e, mt=mt):
                last = None
                for ns in range(4):
                    n0 = 512 * ns
                    nw = NSL[ns]
                    ps = p.psum["PS"][:, ns * 512:ns * 512 + nw]
                    for kg in range(NG):
                        e.matmul(ps, xnt[:, kg, :], win_s[:, kg, n0:n0 + nw],
                                 start=(kg == 0), stop=False, skip_group_check=True)
                    last = e.matmul(ps, onerow[:, :128], binb[:, n0:n0 + nw],
                                    start=False, stop=True, skip_group_check=True)
                return last
            s.op("tensor", xk + [(u, "win", kg) for kg in range(NG)] + [(u, "binb")],
                 [("psb", b) for b in range(4)], mml, tag=f"{u}mml{tt}_{mt}")
            if globals().get("L2F_MM_ONLY"):
                s.op("vector", [("psb", b) for b in range(4)], [(u, "acts")],
                     lambda e: e.tensor_copy(out=acts[:, 0:W], in_=p.psum["PS"][:, 0:W]),
                     tag=f"{u}dbgev{tt}_{mt}")
                continue
            PSW = p.psum["PS"][:, 0:W]
            s.op("scalar", [("psb", b) for b in range(4)], [(u, "acts")],
                 lambda e: e.activation(out=acts[:, 0:W], in_=PSW, func=AF.Silu),
                 tag=f"{u}silu{tt}_{mt}")
            s.op("scalar", [("psb", b) for b in range(4)], [(u, "dec")],
                 lambda e: e.sign(out=dec[...], in_=PSW), tag=f"{u}dec{tt}_{mt}")
            s.op("vector", [(u, "dec")], [(u, "dec")],
                 lambda e: e.tensor_relu(out=dec[...], in_=dec[...]), tag=f"{u}dec2{tt}_{mt}")
            v3 = vis.rearrange("p (tr n) -> p tr n", n=255)
            d3 = dec.rearrange("p (tr n) -> p tr n", n=255)
            if globals().get("L2F_NOTREE"):
                s.op("vector", [], [(u, "vis")],
                     lambda e: e.memset(vis[...], 1.0), tag=f"{u}vr{tt}_{mt}")
            else:
                s.op("vector", [], [(u, "vis")],
                     lambda e: e.memset(v3[:, :, 0:1], 1.0), tag=f"{u}vr{tt}_{mt}")
            for dl in ([] if globals().get("L2F_NOTREE") else range(7)):
                cp = (1 << dl) - 1
                npl = (1 << (dl + 1)) - 1
                n = 1 << dl
                s.op("vector", [(u, "vis"), (u, "dec")], [(u, "vis")],
                     lambda e, cp=cp, npl=npl, n=n: e.tensor_mul(
                         v3[:, :, npl + 1:npl + 2 * n:2], v3[:, :, cp:cp + n], d3[:, :, cp:cp + n]),
                     tag=f"{u}vo{tt}_{mt}_{dl}")
                s.op("vector", [(u, "vis")], [(u, "vis")],
                     lambda e, cp=cp, npl=npl, n=n: e.tensor_sub(
                         v3[:, :, npl:npl + 2 * n:2], v3[:, :, cp:cp + n],
                         v3[:, :, npl + 1:npl + 2 * n:2]),
                     tag=f"{u}ve{tt}_{mt}_{dl}")
            s.op("vector", [(u, "acts"), (u, "vis")], [(u, "acts")],
                 lambda e: e.tensor_mul(acts[:, 0:W], acts[:, 0:W], vis[...]), tag=f"{u}mm{tt}_{mt}")
            if globals().get("L2F_SKIP_OUT"):
                continue
            for jp in range(0, 16, 2):
                def mmt(e, jp=jp):
                    last = None
                    for i in range(2):
                        j = jp + i
                        bv = p.bank(4 + i).bitcast(BF16)
                        last = e.transpose(bv[:, :128], acts[:, j * 128:(j + 1) * 128],
                                           p.buf["eye_bf"][...])
                    return last
                s.op("tensor", [(u, "acts")], [("psb", 4), ("psb", 5)], mmt,
                     tag=f"{u}T{tt}_{mt}_{jp}")
                def evt(e, jp=jp):
                    last = None
                    for i in range(2):
                        j = jp + i
                        bv = p.bank(4 + i).bitcast(BF16)
                        last = e.tensor_copy(out=mT[:, j, :], in_=bv[:, :128])
                    return last
                s.op("vector", [("psb", 4), ("psb", 5)], [(u, "mT", jp)], evt,
                     tag=f"{u}TE{tt}_{mt}_{jp}")
            for m in range(NG):
                wb = m % 2
                s.dma("sync", [(u, "wsl", wb)], [(u, "wsl", wb)],
                      lambda e, m=m, wb=wb: e.dma_start(out=wsl[:, wb], in_=woutT[m]),
                      tag=f"{u}lwo{tt}_{mt}_{m}")
                def mmo(e, m=m, wb=wb):
                    last = None
                    for j in range(16):
                        last = e.matmul(p.bank(6, cols=128), wsl[:, wb, j, :],
                                        mT[:, j, :], start=(j == 0), stop=(j == 15))
                    return last
                s.op("tensor", [(u, "wsl", wb)] + [(u, "mT", jp) for jp in range(0, 16, 2)],
                     [("psb", 6)], mmo, tag=f"{u}mo{tt}_{mt}_{m}")
                s.op("vector", [("psb", 6)], [("x", m)],
                     lambda e, m=m, tc=tc: e.scalar_tensor_tensor(
                         out=x[:, m, tc:tc + 128], in0=p.bank(6, cols=128), scalar=0.5,
                         in1=x[:, m, tc:tc + 128], op0=ALU.mult, op1=ALU.add),
                     tag=f"{u}xa{tt}_{mt}_{m}")




def emit_l2attn(p, inp):
    nc, s = p.nc, p.s
    x = p.buf["x"]
    u = "l2a"
    I2 = 2048
    Dh = p.input("l2a_D", np.ascontiguousarray(np.asarray(inp["L2_attn_D"])[None, :]))
    Ds = p.sbuf(u + "_Ds", (1, I2), F32, side="right")
    s.dma("sync", [], [(u, "Ds")], lambda e: e.dma_start(out=Ds[...], in_=Dh[...]),
          tag=u + "ldD")
    win = emit_to_bf16(p, "l2a_win", np.ascontiguousarray(np.asarray(inp["L2_attn_W_in"])))
    wdt_t = np.ascontiguousarray(
        np.asarray(inp["L2_attn_W_dt"]).reshape(16, 128, 16, 128).transpose(2, 1, 0, 3))
    wdt = emit_to_bf16(p, "l2a_wdt", wdt_t, (16, 128, 16, 128))
    wB = emit_to_bf16(p, "l2a_wB", np.ascontiguousarray(np.asarray(inp["L2_attn_W_B"])))
    wC = emit_to_bf16(p, "l2a_wC", np.ascontiguousarray(np.asarray(inp["L2_attn_W_C"])))
    wot = np.ascontiguousarray(
        np.asarray(inp["L2_attn_W_out"]).reshape(16, 128, 8, 128).transpose(2, 1, 0, 3))
    wout = emit_to_bf16(p, "l2a_wout", wot, (8, 128, 16, 128))
    convh = p.input("l2a_convw", np.ascontiguousarray(
        np.asarray(inp["L2_attn_conv_w"])[:, 0, :].reshape(16, 128, 4).transpose(1, 0, 2)))
    convbh = p.input("l2a_convb", np.ascontiguousarray(
        np.asarray(inp["L2_attn_conv_b"]).reshape(16, 128).T))
    bdth = p.input("l2a_bdt", np.ascontiguousarray(
        np.asarray(inp["L2_attn_b_dt"]).reshape(16, 128).T))
    _free_cvt(p, u)
    Drow = p.sbuf(u + "_Drow", (1, I2), BF16, side="right")
    s.op("scalar", [(u, "Ds")], [(u, "Drow")],
         lambda e: e.copy(out=Drow[...], in_=Ds[...]), tag=u + "cvD")
    DBt = p.sbuf(u + "_DBt", (128, I2), BF16)
    for q4 in range(4):
        s.op("tensor", [(u, "Drow")], [("psb", 6)],
             lambda e, q4=q4: e.matmul(p.bank(6), p.buf["onerow_bf"][:, :128],
                                       Drow[:, q4 * 512:(q4 + 1) * 512],
                                       start=True, stop=True),
             tag=f"{u}DB{q4}")
        s.op("vector", [("psb", 6)], [(u, "DBt")],
             lambda e, q4=q4: e.tensor_copy(out=DBt[:, q4 * 512:(q4 + 1) * 512],
                                            in_=p.bank(6)),
             tag=f"{u}DBe{q4}")
    s.barrier()
    p.free_buf(u + "_Drow")
    p.free_buf(u + "_Ds")

    cw = p.sbuf(u + "_cw", (128, 16, 4), F32)
    cb = p.sbuf(u + "_cb", (128, 16), F32)
    bdtc = p.sbuf(u + "_bdtc", (128, 16), F32)
    s.dma("sync", [], [(u, "cw")], lambda e: e.dma_start(out=cw[...], in_=convh[...]),
          tag=u + "ldcw")
    s.dma("sync", [], [(u, "cb")], lambda e: e.dma_start(out=cb[...], in_=convbh[...]),
          tag=u + "ldcb")
    s.dma("sync", [], [(u, "bdtc")], lambda e: e.dma_start(out=bdtc[...], in_=bdth[...]),
          tag=u + "ldbdt")

    tri_bd = np.zeros((128, 128), np.float32)
    tri_bd[:64, :64] = np.triu(np.ones((64, 64), np.float32))
    tri_bd[64:, 64:] = np.triu(np.ones((64, 64), np.float32))
    TriBD = emit_const(p, "l2a_tri", tri_bd, scope="phase")
    ubd = np.zeros((128, 128), BF)
    ubd[:64, :64] = np.triu(np.ones((64, 64), BF))
    ubd[64:, 64:] = np.triu(np.ones((64, 64), BF))
    UBD = emit_const(p, "l2a_ubd", ubd, scope="phase")
    indc = np.zeros((128, 4), np.float32)
    indc[0:64, 0] = 1.0
    indc[64:128, 1] = 1.0
    indc[64:96, 2] = 1.0
    indc[0:32, 3] = 1.0
    IND = emit_const(p, "l2a_ind", indc, scope="phase")
    onerow = p.buf["onerow_bf"]
    ones_col = p.buf["ones_col"]

    xin_h = p.scratch(u + "_xin", (I2, T), BF16)
    xcf_h = p.scratch(u + "_xcf", (I2, T), BF16)
    xct_h = p.scratch(u + "_xct", (T, I2), BF16)
    dtt_h = p.scratch(u + "_dtt", (T, I2), F32)
    zs_h = p.scratch(u + "_zs", (T, I2), BF16)
    btm_h = p.scratch(u + "_btm", (T, 64), BF16)
    ctm_h = p.scratch(u + "_ctm", (T, 64), BF16)
    bfm_h = p.scratch(u + "_bfm", (64, T), BF16)
    cfm_h = p.scratch(u + "_cfm", (64, T), BF16)
    win_r = win.rearrange("(g pp) i -> pp g i", pp=128)

    xnt = p.sbuf(u + "_xnt", (128, NG, TT), BF16)
    wstr = p.sbuf(u + "_wstr", (128, 2, NG, 128), BF16)
    zw = p.sbuf(u + "_zw", (128, NG, TT), BF16)
    stg1 = p.sbuf(u + "_stg1", (128, 2, TT), BF16)
    zstg = p.sbuf(u + "_zstg", (128, 4, TT), BF16)
    for tt in range(NTT):
        t0 = tt * TT
        emit_ln_tile(p, u, tt)
        for g in range(NG):
            emit_ln_apply(p, u, tt, g, xnt[:, g, :], (u, "xnt", g))
        xk = [(u, "xnt", g) for g in range(NG)]
        for m in range(16):
            wb = m % 2
            s.dma("scalar", [(u, "wstr", wb)], [(u, "wstr", wb)],
                  lambda e, m=m, wb=wb: e.dma_start(
                      out=wstr[:, wb], in_=win_r[:, :, m * 128:(m + 1) * 128]),
                  tag=f"{u}lwin{tt}_{m}")
            def mmx(e, m=m, wb=wb):
                last = None
                for kg in range(NG):
                    last = e.matmul(p.bank(m % 4), wstr[:, wb, kg, :], xnt[:, kg, :],
                                    start=(kg == 0), stop=(kg == NG - 1))
                return last
            s.op("tensor", [(u, "wstr", wb)] + xk, [("psb", m % 4)], mmx, tag=f"{u}xi{tt}_{m}")
            sb = m % 2
            s.op("scalar", [("psb", m % 4)], [(u, "stg1", sb)],
                 lambda e, m=m, sb=sb: e.copy(out=stg1[:, sb, :], in_=p.bank(m % 4)),
                 tag=f"{u}xie{tt}_{m}")
            s.dma("gpsimd", [(u, "stg1", sb)], [(u, "xinh", m)],
                  lambda e, m=m, sb=sb, t0=t0: e.dma_start(
                      out=xin_h[m * 128:(m + 1) * 128, t0:t0 + TT], in_=stg1[:, sb, :]),
                  tag=f"{u}xis{tt}_{m}")
        for zs4 in range(4):
            s.dma("scalar", [(u, "zw")], [(u, "zw")],
                  lambda e, zs4=zs4: e.dma_start(
                      out=zw[...], in_=win_r[:, :, I2 + zs4 * 512:I2 + (zs4 + 1) * 512]),
                  tag=f"{u}lwz{tt}_{zs4}")
            def mmz(e):
                last = None
                for mt in range(4):
                    ps = p.bank(4 + mt)
                    for kg in range(NG):
                        last = e.matmul(ps, xnt[:, kg, mt * 128:(mt + 1) * 128],
                                        zw[:, kg, :], start=(kg == 0), stop=(kg == NG - 1),
                                        skip_group_check=True)
                return last
            s.op("tensor", [(u, "zw")] + xk, [("psb", 4 + mt_) for mt_ in range(4)], mmz,
                 tag=f"{u}z{tt}_{zs4}")
            for mt in range(4):
                s.op("scalar", [("psb", 4 + mt)], [(u, "zstg", mt)],
                     lambda e, mt=mt: e.activation(out=zstg[:, mt, :], in_=p.bank(4 + mt),
                                                   func=AF.Silu),
                     tag=f"{u}ze{tt}_{zs4}_{mt}")
                s.dma("gpsimd", [(u, "zstg", mt)], [(u, "zsh", tt, mt, zs4)],
                      lambda e, mt=mt, zs4=zs4, t0=t0: e.dma_start(
                          out=zs_h[t0 + mt * 128:t0 + (mt + 1) * 128, zs4 * 512:(zs4 + 1) * 512],
                          in_=zstg[:, mt, :]),
                      tag=f"{u}zs{tt}_{zs4}_{mt}")
    s.barrier()
    for n in (u + "_zstg", u + "_stg1", u + "_zw", u + "_wstr", u + "_xnt"):
        p.free_buf(n)

    xint = p.sbuf(u + "_xint", (128, TT + 8), BF16)
    cacc = p.sbuf(u + "_cacc", (128, TT), F32)
    xcst = p.sbuf(u + "_xcst", (128, 2, TT), BF16)
    xtst = p.sbuf(u + "_xtst", (128, 2, 128), BF16)
    for ig in range(16):
        for tt in range(NTT):
            t0 = tt * TT
            if tt == 0:
                s.op("vector", [(u, "xint")], [(u, "xint")],
                     lambda e: e.memset(xint[:, 0:3], 0.0), tag=f"{u}cz{ig}")
                s.dma("sync", [(u, "xint")], [(u, "xint")],
                      lambda e, ig=ig: e.dma_start(out=xint[:, 3:3 + TT],
                                                   in_=xin_h[ig * 128:(ig + 1) * 128, 0:TT]),
                      tag=f"{u}cl{ig}_{tt}")
            else:
                s.dma("sync", [(u, "xint")], [(u, "xint")],
                      lambda e, ig=ig, t0=t0: e.dma_start(
                          out=xint[:, 0:3 + TT], in_=xin_h[ig * 128:(ig + 1) * 128, t0 - 3:t0 + TT]),
                      tag=f"{u}cl{ig}_{tt}")
            s.op("vector", [(u, "xint"), (u, "cw")], [(u, "cacc")],
                 lambda e, ig=ig: e.tensor_scalar_mul(cacc[...], xint[:, 0:TT], cw[:, ig, 0:1]),
                 tag=f"{u}c0{ig}_{tt}")
            for j in (1, 2, 3):
                s.op("vector", [(u, "xint"), (u, "cw"), (u, "cacc")], [(u, "cacc")],
                     lambda e, ig=ig, j=j: e.scalar_tensor_tensor(
                         out=cacc[...], in0=xint[:, j:j + TT], scalar=cw[:, ig, j:j + 1],
                         in1=cacc[...], op0=ALU.mult, op1=ALU.add),
                     tag=f"{u}c{j}_{ig}_{tt}")
            sb = tt % 2
            s.op("scalar", [(u, "cacc"), (u, "cb")], [(u, "xcst", sb)],
                 lambda e, ig=ig, sb=sb: e.activation(out=xcst[:, sb, :], in_=cacc[...],
                                                      func=AF.Silu, bias=cb[:, ig:ig + 1]),
                 tag=f"{u}cs{ig}_{tt}")
            s.dma("gpsimd", [(u, "xcst", sb)], [(u, "xcfh", ig, tt)],
                  lambda e, ig=ig, sb=sb, t0=t0: e.dma_start(
                      out=xcf_h[ig * 128:(ig + 1) * 128, t0:t0 + TT], in_=xcst[:, sb, :]),
                  tag=f"{u}cst{ig}_{tt}")
            for j4 in range(4):
                tb = j4 % 2
                s.op("tensor", [(u, "xcst", sb)], [("psb", tb)],
                     lambda e, sb=sb, j4=j4, tb=tb: e.transpose(
                         p.bank(tb).bitcast(BF16)[:, :128], xcst[:, sb, j4 * 128:(j4 + 1) * 128],
                         p.buf["eye_bf"][...]),
                     tag=f"{u}cT{ig}_{tt}_{j4}")
                s.op("vector", [("psb", tb)], [(u, "xtst", tb)],
                     lambda e, tb=tb: e.tensor_copy(out=xtst[:, tb, :],
                                                    in_=p.bank(tb).bitcast(BF16)[:, :128]),
                     tag=f"{u}cTe{ig}_{tt}_{j4}")
                s.dma("gpsimd", [(u, "xtst", tb)], [(u, "xcth", ig, tt, j4)],
                      lambda e, ig=ig, tb=tb, t0=t0, j4=j4: e.dma_start(
                          out=xct_h[t0 + j4 * 128:t0 + (j4 + 1) * 128, ig * 128:(ig + 1) * 128],
                          in_=xtst[:, tb, :]),
                      tag=f"{u}cTs{ig}_{tt}_{j4}")
    s.barrier()
    for n in (u + "_xtst", u + "_xcst", u + "_cacc", u + "_xint"):
        p.free_buf(n)

    xcfb = p.sbuf(u + "_xcfb", (128, 16, TT), BF16)
    wsl = p.sbuf(u + "_wsl", (128, 2, 16, 128), BF16)
    dstg = p.sbuf(u + "_dstg", (128, TT), F32)
    spt = p.sbuf(u + "_spt", (128, 2, TT), F32)
    dts = p.sbuf(u + "_dts", (128, 2, 128), F32)
    wBb = p.sbuf(u + "_wBb", (128, 16, 64), BF16)
    wCb = p.sbuf(u + "_wCb", (128, 16, 64), BF16)
    bcst = p.sbuf(u + "_bcst", (128, 2, 64), BF16)
    bfst = p.sbuf(u + "_bfst", (64, TT), BF16)
    for ki in range(16):
        s.dma("sync", [], [(u, "wBb", ki)],
              lambda e, ki=ki: e.dma_start(out=wBb[:, ki, :], in_=wB[ki * 128:(ki + 1) * 128, :]),
              tag=f"{u}lwB{ki}")
        s.dma("sync", [], [(u, "wCb", ki)],
              lambda e, ki=ki: e.dma_start(out=wCb[:, ki, :], in_=wC[ki * 128:(ki + 1) * 128, :]),
              tag=f"{u}lwC{ki}")
    wk_bc = [(u, "wBb", ki) for ki in range(16)] + [(u, "wCb", ki) for ki in range(16)]
    for tt in range(NTT):
        t0 = tt * TT
        for ki in range(16):
            s.dma("gpsimd", [(u, "xcfb", ki)], [(u, "xcfb", ki)],
                  lambda e, ki=ki, t0=t0: e.dma_start(
                      out=xcfb[:, ki, :], in_=xcf_h[ki * 128:(ki + 1) * 128, t0:t0 + TT]),
                  tag=f"{u}dxc{tt}_{ki}")
        xck = [(u, "xcfb", ki) for ki in range(16)]
        for m2 in range(16):
            wb = m2 % 2
            s.dma("scalar", [(u, "wsl", wb)], [(u, "wsl", wb)],
                  lambda e, m2=m2, wb=wb: e.dma_start(out=wsl[:, wb], in_=wdt[m2]),
                  tag=f"{u}lwdt{tt}_{m2}")
            def mmd(e, m2=m2, wb=wb):
                last = None
                for ki in range(16):
                    last = e.matmul(p.bank(6 + m2 % 2), wsl[:, wb, ki, :], xcfb[:, ki, :],
                                    start=(ki == 0), stop=(ki == 15))
                return last
            s.op("tensor", [(u, "wsl", wb)] + xck, [("psb", 6 + m2 % 2)], mmd,
                 tag=f"{u}dt{tt}_{m2}")
            s.op("scalar", [("psb", 6 + m2 % 2), (u, "bdtc")], [(u, "spt", 0)],
                 lambda e, m2=m2: e.activation(out=spt[:, 0, :], in_=p.bank(6 + m2 % 2),
                                               func=AF.Abs, bias=bdtc[:, m2:m2 + 1]),
                 tag=f"{u}sp1{tt}_{m2}")
            s.op("scalar", [(u, "spt", 0), "fvals"], [(u, "spt", 0)],
                 lambda e: e.activation(out=spt[:, 0, :], in_=spt[:, 0, :], func=AF.Exp,
                                        scale=p.fconst(-1.0)),
                 tag=f"{u}sp2{tt}_{m2}")
            s.op("scalar", [(u, "spt", 0), "fvals"], [(u, "spt", 0)],
                 lambda e: e.activation(out=spt[:, 0, :], in_=spt[:, 0, :], func=AF.Ln,
                                        bias=p.fconst(1.0)),
                 tag=f"{u}sp3{tt}_{m2}")
            s.op("scalar", [("psb", 6 + m2 % 2), (u, "bdtc")], [(u, "spt", 1)],
                 lambda e, m2=m2: e.activation(out=spt[:, 1, :], in_=p.bank(6 + m2 % 2),
                                               func=AF.Relu, bias=bdtc[:, m2:m2 + 1]),
                 tag=f"{u}sp4{tt}_{m2}")
            s.op("vector", [(u, "spt", 0), (u, "spt", 1)], [(u, "dstg")],
                 lambda e: e.tensor_add(dstg[...], spt[:, 0, :], spt[:, 1, :]),
                 tag=f"{u}sp5{tt}_{m2}")
            for j4 in range(4):
                tb = 2 + j4 % 2
                s.op("tensor", [(u, "dstg")], [("psb", tb)],
                     lambda e, j4=j4, tb=tb: e.transpose(
                         p.bank(tb, cols=128), dstg[:, j4 * 128:(j4 + 1) * 128],
                         p.buf["eye_f32"][...]),
                     tag=f"{u}dT{tt}_{m2}_{j4}")
                s.op("vector", [("psb", tb)], [(u, "dts", j4 % 2)],
                     lambda e, tb=tb, j4=j4: e.tensor_copy(out=dts[:, j4 % 2, :],
                                                           in_=p.bank(tb, cols=128)),
                     tag=f"{u}dTe{tt}_{m2}_{j4}")
                s.dma("gpsimd", [(u, "dts", j4 % 2)], [(u, "dtth", tt, m2, j4)],
                      lambda e, j4=j4, t0=t0, m2=m2: e.dma_start(
                          out=dtt_h[t0 + j4 * 128:t0 + (j4 + 1) * 128, m2 * 128:(m2 + 1) * 128],
                          in_=dts[:, j4 % 2, :]),
                      tag=f"{u}dTs{tt}_{m2}_{j4}")
        for nm, w_, dsth, dstf in (("b", wBb, btm_h, bfm_h), ("c", wCb, ctm_h, cfm_h)):
            for mt in range(4):
                sb2 = mt % 2
                def mmb(e, mt=mt, w_=w_):
                    last = None
                    for ki in range(16):
                        last = e.matmul(p.bank(4, cols=64), xcfb[:, ki, mt * 128:(mt + 1) * 128],
                                        w_[:, ki, :], start=(ki == 0), stop=(ki == 15))
                    return last
                s.op("tensor", xck + wk_bc, [("psb", 4)], mmb, tag=f"{u}{nm}tm{tt}_{mt}")
                s.op("vector", [("psb", 4)], [(u, "bcst", sb2)],
                     lambda e, sb2=sb2: e.tensor_copy(out=bcst[:, sb2, :], in_=p.bank(4, cols=64)),
                     tag=f"{u}{nm}te{tt}_{mt}")
                s.dma("sync", [(u, "bcst", sb2)], [(u, nm + "tmh", tt, mt)],
                      lambda e, mt=mt, t0=t0, dsth=dsth, sb2=sb2: e.dma_start(
                          out=dsth[t0 + mt * 128:t0 + (mt + 1) * 128, :], in_=bcst[:, sb2, :]),
                      tag=f"{u}{nm}ts{tt}_{mt}")
            def mmbf(e, w_=w_):
                last = None
                for ki in range(16):
                    last = e.matmul(p.bank(5, rows=64), w_[:, ki, :], xcfb[:, ki, :],
                                    start=(ki == 0), stop=(ki == 15))
                return last
            s.op("tensor", xck + wk_bc, [("psb", 5)], mmbf, tag=f"{u}{nm}fm{tt}")
            s.op("vector", [("psb", 5)], [(u, "bfst")],
                 lambda e: e.tensor_copy(out=bfst[...], in_=p.bank(5, rows=64)),
                 tag=f"{u}{nm}fe{tt}")
            s.dma("sync", [(u, "bfst")], [(u, nm + "fmh", tt)],
                  lambda e, t0=t0, dstf=dstf: e.dma_start(
                      out=dstf[:, t0:t0 + TT], in_=bfst[...]),
                  tag=f"{u}{nm}fs{tt}")
    s.barrier()
    for n in (u + "_bfst", u + "_bcst", u + "_wCb", u + "_wBb", u + "_dts", u + "_spt",
              u + "_dstg", u + "_wsl", u + "_xcfb"):
        p.free_buf(n)

    HI = 1024
    dtc = p.sbuf(u + "_dtc", (128, 1, HI), F32)
    xcc = p.sbuf(u + "_xcc", (128, HI), BF16)
    szc = p.sbuf(u + "_szc", (128, HI), BF16)
    dta = p.sbuf(u + "_dta", (128, HI), BF16)
    r_sb = p.sbuf(u + "_rsb", (128, HI), F32)
    tmpf = p.sbuf(u + "_tmpf", (128, HI), F32)
    e1m = p.sbuf(u + "_e1m", (128, HI), BF16)
    w1m = p.sbuf(u + "_w1m", (128, HI), BF16)
    gtm = p.sbuf(u + "_gtm", (128, 128), BF16)
    hst = p.sbuf(u + "_h", (64, 2, HI), F32)
    htmp = p.sbuf(u + "_htmp", (64, HI), F32)
    hA_bf = p.sbuf(u + "_hAb", (64, HI), BF16)
    hB_bf = p.sbuf(u + "_hBb", (64, HI), BF16)
    fA = p.sbuf(u + "_fA", (1, HI), BF16)
    fB = p.sbuf(u + "_fB", (1, HI), BF16)
    rmA = p.sbuf(u + "_rmA", (1, HI), BF16)
    rmB = p.sbuf(u + "_rmB", (1, HI), BF16)
    bpr = p.sbuf(u + "_bpr", (128, 64), BF16)
    cfpair = p.sbuf(u + "_cfpair", (64, 128), BF16)
    bfpair = p.sbuf(u + "_bfpair", (64, 128), BF16)
    yfm = p.sbuf(u + "_yfm", (128, 2, 8, 128), BF16)
    wos = p.sbuf(u + "_wos", (128, 8, 128), BF16)

    s.op("vector", [], [(u, "h")], lambda e: e.memset(hst[...], 0.0), tag=u + "h0")

    PS = p.psum["PS"]
    for pr in range(32):
        tp0 = pr * 128
        s.dma("sync", [(u, "bpr")], [(u, "bpr")],
              lambda e, tp0=tp0: e.dma_start(out=bpr[...], in_=btm_h[tp0:tp0 + 128, :]),
              tag=f"{u}lb{pr}")
        s.dma("sync", [(u, "cfpair")], [(u, "cfpair")],
              lambda e, tp0=tp0: e.dma_start(out=cfpair[...], in_=cfm_h[:, tp0:tp0 + 128]),
              tag=f"{u}lcf{pr}")
        s.dma("sync", [(u, "bfpair")], [(u, "bfpair")],
              lambda e, tp0=tp0: e.dma_start(out=bfpair[...], in_=bfm_h[:, tp0:tp0 + 128]),
              tag=f"{u}lbf{pr}")
        for hf in range(2):
            i0 = hf * HI
            hk = (u, "hf", pr, hf)
            s.dma("gpsimd", [(u, "dtc", 0)], [(u, "dtc", 0)],
                  lambda e, tp0=tp0, i0=i0: e.dma_start(
                      out=dtc[:, 0, :], in_=dtt_h[tp0:tp0 + 128, i0:i0 + HI]),
                  tag=f"{u}ldt{pr}_{hf}")
            s.dma("gpsimd", [(u, "xcc")], [(u, "xcc")],
                  lambda e, tp0=tp0, i0=i0: e.dma_start(
                      out=xcc[...], in_=xct_h[tp0:tp0 + 128, i0:i0 + HI]),
                  tag=f"{u}lxc{pr}_{hf}")
            s.dma("gpsimd", [(u, "szc")], [(u, "szc")],
                  lambda e, tp0=tp0, i0=i0: e.dma_start(
                      out=szc[...], in_=zs_h[tp0:tp0 + 128, i0:i0 + HI]),
                  tag=f"{u}lsz{pr}_{hf}")
            def mmr(e, hf=hf):
                e.matmul(p.bank(0), TriBD[...], dtc[:, 0, 0:512], start=True, stop=True,
                         skip_group_check=True)
                return e.matmul(p.bank(1), TriBD[...], dtc[:, 0, 512:HI], start=True,
                                stop=True, skip_group_check=True)
            s.op("tensor", [(u, "dtc", 0), "l2a_tri"], [("psb", 0), ("psb", 1)], mmr,
                 tag=f"{u}r{pr}_{hf}")
            s.op("scalar", [("psb", 0)], [(u, "rsb")],
                 lambda e: e.copy(out=r_sb[:, 0:512], in_=p.bank(0)), tag=f"{u}re{pr}_{hf}")
            s.op("scalar", [("psb", 1), (u, "rsb")], [(u, "rsb")],
                 lambda e: e.copy(out=r_sb[:, 512:HI], in_=p.bank(1)), tag=f"{u}re2{pr}_{hf}")
            def mmsums(e, hf=hf):
                last = None
                for q, b0 in ((0, 2), (1, 4), (2, 6), (3, 0)):
                    e.matmul(PS[0:1, b0 * 512:(b0 + 1) * 512], IND[:, q:q + 1],
                             dtc[:, 0, 0:512], start=True, stop=True, skip_group_check=True)
                    last = e.matmul(PS[0:1, (b0 + 1) * 512:(b0 + 2) * 512], IND[:, q:q + 1],
                                    dtc[:, 0, 512:HI], start=True, stop=True,
                                    skip_group_check=True)
                return last
            s.op("tensor", [(u, "dtc", 0), "l2a_ind"],
                 [("psb", b) for b in range(8)], mmsums, tag=f"{u}sums{pr}_{hf}")
            s.op("vector", [("psb", 0), ("psb", 1)], [(u, "rmA")],
                 lambda e: e.tensor_copy(out=rmA[...], in_=PS[0:1, 0:1024]),
                 tag=f"{u}rmA{pr}_{hf}")
            s.op("vector", [(u, "rmA"), ("psb", 2), ("psb", 3)], [(u, "fA")],
                 lambda e: e.tensor_sub(fA[...], rmA[...], PS[0:1, 1024:2048]),
                 tag=f"{u}fA{pr}_{hf}")
            s.op("scalar", [(u, "fA")], [(u, "fA")],
                 lambda e: e.activation(out=fA[...], in_=fA[...], func=AF.Exp),
                 tag=f"{u}fAe{pr}_{hf}")
            s.op("vector", [("psb", 6), ("psb", 7)], [(u, "rmB")],
                 lambda e: e.tensor_copy(out=rmB[...], in_=PS[0:1, 3072:4096]),
                 tag=f"{u}rmB{pr}_{hf}")
            s.op("vector", [(u, "rmB"), ("psb", 4), ("psb", 5)], [(u, "fB")],
                 lambda e: e.tensor_sub(fB[...], rmB[...], PS[0:1, 2048:3072]),
                 tag=f"{u}fB{pr}_{hf}")
            s.op("scalar", [(u, "fB")], [(u, "fB")],
                 lambda e: e.activation(out=fB[...], in_=fB[...], func=AF.Exp),
                 tag=f"{u}fBe{pr}_{hf}")
            def mmshift(e):
                e.matmul(PS[0:64, 1024:1536], p.buf["onerow_bf"][:, :64], rmA[:, 0:512],
                         start=True, stop=True, skip_group_check=True)
                e.matmul(PS[0:64, 1536:2048], p.buf["onerow_bf"][:, :64], rmA[:, 512:HI],
                         start=True, stop=True, skip_group_check=True)
                e.matmul(PS[64:128, 1024:1536], p.buf["onerow_bf"][:, :64], rmB[:, 0:512],
                         start=True, stop=True, skip_group_check=True)
                return e.matmul(PS[64:128, 1536:2048], p.buf["onerow_bf"][:, :64], rmB[:, 512:HI],
                                start=True, stop=True, skip_group_check=True)
            s.op("tensor", [(u, "rmA"), (u, "rmB")], [("psb", 2), ("psb", 3)], mmshift,
                 tag=f"{u}sh{pr}_{hf}")
            s.op("vector", [(u, "rsb"), ("psb", 2), ("psb", 3)], [(u, "tmpf")],
                 lambda e: e.tensor_sub(tmpf[...], r_sb[...], PS[:, 1024:2048]),
                 tag=f"{u}e1s{pr}_{hf}")
            s.op("scalar", [(u, "tmpf")], [(u, "e1m")],
                 lambda e: e.activation(out=e1m[...], in_=tmpf[...], func=AF.Exp),
                 tag=f"{u}e1e{pr}_{hf}")

            s.op("vector", [(u, "dtc", 0), (u, "xcc")], [(u, "dta")],
                 lambda e: e.tensor_mul(dta[...], dtc[:, 0, :], xcc[...]),
                 tag=f"{u}dta{pr}_{hf}")
            s.op("vector", [(u, "dta"), (u, "e1m")], [(u, "w1m")],
                 lambda e: e.tensor_mul(w1m[...], e1m[...], dta[...]), tag=f"{u}w1{pr}_{hf}")
            s.op("tensor", [(u, "bfpair"), (u, "cfpair")], [("psb", 4)],
                 lambda e: e.matmul(p.bank(4, cols=128), bfpair[...], cfpair[...],
                                    start=True, stop=True),
                 tag=f"{u}gt{pr}_{hf}")
            s.op("vector", [("psb", 4), "l2a_ubd"], [(u, "gtm")],
                 lambda e: e.tensor_mul(gtm[...], p.bank(4, cols=128), UBD[...]),
                 tag=f"{u}gtme{pr}_{hf}")
            def mmqA(e):
                e.matmul(PS[0:64, 5 * 512:6 * 512], p.buf["onerow_bf"][:, :64], rmA[:, 0:512],
                         start=True, stop=True, skip_group_check=True)
                return e.matmul(PS[0:64, 6 * 512:7 * 512], p.buf["onerow_bf"][:, :64], rmA[:, 512:HI],
                                start=True, stop=True, skip_group_check=True)
            s.op("tensor", [(u, "rmA")], [("psb", 5), ("psb", 6)], mmqA, tag=f"{u}qA{pr}_{hf}")
            s.op("scalar", [("psb", 5), ("psb", 6)], [("psb", 2), ("psb", 3)],
                 lambda e: e.activation(out=PS[0:64, 1024:2048], in_=PS[0:64, 2560:3584],
                                        func=AF.Exp, scale=p.fconst(-1.0)[0:64, :]),
                 tag=f"{u}qAe{pr}_{hf}")
            s.op("vector", [("psb", 2), ("psb", 3), (u, "h"), "fvals"], [(u, "htmp")],
                 lambda e, hf=hf: e.tensor_mul(htmp[...], hst[:, hf, :], PS[0:64, 1024:2048]),
                 tag=f"{u}hA{pr}_{hf}")
            s.op("vector", [(u, "htmp")], [(u, "hAb")],
                 lambda e: e.tensor_copy(out=hA_bf[...], in_=htmp[...]), tag=f"{u}hAb{pr}_{hf}")
            def mmPA(e):
                e.matmul(PS[0:64, 5 * 512:6 * 512], bpr[0:64, :], w1m[0:64, 0:512],
                         start=True, stop=True, skip_group_check=True)
                return e.matmul(PS[0:64, 6 * 512:7 * 512], bpr[0:64, :], w1m[0:64, 512:HI],
                                start=True, stop=True, skip_group_check=True)
            s.op("tensor", [(u, "bpr"), (u, "w1m")], [("psb", 5), ("psb", 6)], mmPA,
                 tag=f"{u}PA{pr}_{hf}")
            s.op("vector", [("psb", 5), ("psb", 6), (u, "htmp")], [(u, "htmp")],
                 lambda e: e.tensor_add(htmp[...], htmp[...], PS[0:64, 2560:3584]),
                 tag=f"{u}hm1{pr}_{hf}")
            def mmfA(e):
                e.matmul(PS[0:64, 5 * 512:6 * 512], p.buf["onerow_bf"][:, :64], fA[:, 0:512],
                         start=True, stop=True, skip_group_check=True)
                return e.matmul(PS[0:64, 6 * 512:7 * 512], p.buf["onerow_bf"][:, :64], fA[:, 512:HI],
                                start=True, stop=True, skip_group_check=True)
            s.op("tensor", [(u, "fA")], [("psb", 5), ("psb", 6)], mmfA, tag=f"{u}fAb{pr}_{hf}")
            s.op("vector", [("psb", 5), ("psb", 6), (u, "htmp")], [(u, "htmp")],
                 lambda e: e.tensor_mul(htmp[...], htmp[...], PS[0:64, 2560:3584]),
                 tag=f"{u}hm2{pr}_{hf}")
            def mmqB(e):
                e.matmul(PS[0:64, 5 * 512:6 * 512], p.buf["onerow_bf"][:, :64], rmB[:, 0:512],
                         start=True, stop=True, skip_group_check=True)
                return e.matmul(PS[0:64, 6 * 512:7 * 512], p.buf["onerow_bf"][:, :64], rmB[:, 512:HI],
                                start=True, stop=True, skip_group_check=True)
            s.op("tensor", [(u, "rmB")], [("psb", 5), ("psb", 6)], mmqB, tag=f"{u}qB{pr}_{hf}")
            s.op("scalar", [("psb", 5), ("psb", 6)], [("psb", 2), ("psb", 3)],
                 lambda e: e.activation(out=PS[0:64, 1024:2048], in_=PS[0:64, 2560:3584],
                                        func=AF.Exp, scale=p.fconst(-1.0)[0:64, :]),
                 tag=f"{u}qBe{pr}_{hf}")
            s.op("vector", [("psb", 2), ("psb", 3), (u, "htmp")], [(u, "htmp")],
                 lambda e: e.tensor_mul(htmp[...], htmp[...], PS[0:64, 1024:2048]),
                 tag=f"{u}hB{pr}_{hf}")
            s.op("vector", [(u, "htmp")], [(u, "hBb")],
                 lambda e: e.tensor_copy(out=hB_bf[...], in_=htmp[...]), tag=f"{u}hBb{pr}_{hf}")
            def mmPB(e):
                e.matmul(PS[0:64, 5 * 512:6 * 512], bpr[64:128, :], w1m[64:128, 0:512],
                         start=True, stop=True, skip_group_check=True)
                return e.matmul(PS[0:64, 6 * 512:7 * 512], bpr[64:128, :], w1m[64:128, 512:HI],
                                start=True, stop=True, skip_group_check=True)
            s.op("tensor", [(u, "bpr"), (u, "w1m")], [("psb", 5), ("psb", 6)], mmPB,
                 tag=f"{u}PB{pr}_{hf}")
            s.op("vector", [("psb", 5), ("psb", 6), (u, "htmp")], [(u, "htmp")],
                 lambda e: e.tensor_add(htmp[...], htmp[...], PS[0:64, 2560:3584]),
                 tag=f"{u}hn1{pr}_{hf}")
            def mmfB(e):
                e.matmul(PS[0:64, 5 * 512:6 * 512], p.buf["onerow_bf"][:, :64], fB[:, 0:512],
                         start=True, stop=True, skip_group_check=True)
                return e.matmul(PS[0:64, 6 * 512:7 * 512], p.buf["onerow_bf"][:, :64], fB[:, 512:HI],
                                start=True, stop=True, skip_group_check=True)
            s.op("tensor", [(u, "fB")], [("psb", 5), ("psb", 6)], mmfB, tag=f"{u}fBb{pr}_{hf}")
            s.op("vector", [("psb", 5), ("psb", 6), (u, "htmp")], [(u, "h")],
                 lambda e, hf=hf: e.tensor_mul(hst[:, hf, :], htmp[...], PS[0:64, 2560:3584]),
                 tag=f"{u}hn2{pr}_{hf}")
            def mmy(e):
                e.matmul(p.bank(0), gtm[...], w1m[:, 0:512], start=True, stop=False,
                         skip_group_check=True)
                e.matmul(p.bank(1), gtm[...], w1m[:, 512:HI], start=True, stop=False,
                         skip_group_check=True)
                e.matmul(PS[0:64, 0:512], cfpair[:, 0:64], hA_bf[:, 0:512], start=False,
                         stop=False, skip_group_check=True)
                e.matmul(PS[0:64, 512:1024], cfpair[:, 0:64], hA_bf[:, 512:HI], start=False,
                         stop=False, skip_group_check=True)
                e.matmul(PS[64:128, 0:512], cfpair[:, 64:128], hB_bf[:, 0:512], start=False,
                         stop=True, skip_group_check=True)
                return e.matmul(PS[64:128, 512:1024], cfpair[:, 64:128], hB_bf[:, 512:HI],
                                start=False, stop=True, skip_group_check=True)
            s.op("tensor", [(u, "gtm"), (u, "w1m"), (u, "cfpair"), (u, "hAb"), (u, "hBb")],
                 [("psb", 0), ("psb", 1)], mmy, tag=f"{u}y{pr}_{hf}")
            s.op("vector", [(u, "e1m"), (u, "w1m")], [(u, "tmpf")],
                 lambda e: e.reciprocal(tmpf[...], e1m[...]), tag=f"{u}e2{pr}_{hf}")
            s.op("vector", [("psb", 0), ("psb", 1), (u, "tmpf")], [(u, "e1m")],
                 lambda e: e.tensor_mul(e1m[...], PS[:, 0:HI], tmpf[...]),
                 tag=f"{u}ye{pr}_{hf}")
            s.op("vector", [(u, "xcc"), (u, "DBt")], [(u, "dta")],
                 lambda e, i0=i0: e.tensor_mul(dta[...], xcc[...], DBt[:, i0:i0 + HI]),
                 tag=f"{u}yd{pr}_{hf}")
            s.op("vector", [(u, "e1m"), (u, "dta")], [(u, "e1m")],
                 lambda e: e.tensor_add(e1m[...], e1m[...], dta[...]), tag=f"{u}ya{pr}_{hf}")
            s.op("vector", [(u, "e1m"), (u, "szc")], [(u, "e1m")],
                 lambda e: e.tensor_mul(e1m[...], e1m[...], szc[...]), tag=f"{u}yg{pr}_{hf}")
            for j8 in range(8):
                tb = 2 + j8 % 2
                s.op("tensor", [(u, "e1m")], [("psb", tb)],
                     lambda e, j8=j8, tb=tb: e.transpose(
                         p.bank(tb).bitcast(BF16)[:, :128], e1m[:, j8 * 128:(j8 + 1) * 128],
                         p.buf["eye_bf"][...]),
                     tag=f"{u}yT{pr}_{hf}_{j8}")
                s.op("vector", [("psb", tb)], [(u, "yfm", hf)],
                     lambda e, j8=j8, tb=tb, hf=hf: e.tensor_copy(
                         out=yfm[:, hf, j8, :], in_=p.bank(tb).bitcast(BF16)[:, :128]),
                     tag=f"{u}yTe{pr}_{hf}_{j8}")
        for m in range(NG):
            for hf2 in range(2):
                s.dma("scalar", [(u, "wos")], [(u, "wos")],
                      lambda e, m=m, hf2=hf2: e.dma_start(
                          out=wos[...], in_=wout[m, :, hf2 * 8:(hf2 + 1) * 8, :]),
                      tag=f"{u}lwo{pr}_{m}_{hf2}")
                def mmo(e, m=m, hf2=hf2):
                    last = None
                    for j8 in range(8):
                        last = e.matmul(p.bank(7, cols=128), wos[:, j8, :],
                                        yfm[:, hf2, j8, :], start=(hf2 == 0 and j8 == 0),
                                        stop=(hf2 == 1 and j8 == 7))
                    return last
                s.op("tensor", [(u, "wos"), (u, "yfm", hf2)], [("psb", 7)], mmo,
                     tag=f"{u}mo{pr}_{m}_{hf2}")
            s.op("vector", [("psb", 7)], [("x", m)],
                 lambda e, m=m, tp0=tp0: e.tensor_add(
                     x[:, m, tp0:tp0 + 128], x[:, m, tp0:tp0 + 128], p.bank(7, cols=128)),
                 tag=f"{u}xa{pr}_{m}")


def _tile1024(w):
    return np.ascontiguousarray(np.asarray(w).reshape(8, 128, 8, 128).transpose(2, 1, 0, 3))


def emit_l1attn(p, inp):
    nc, s = p.nc, p.s
    x = p.buf["x"]
    u = "l1a"
    N16 = 16
    w1t = emit_to_bf16(p, "l1a_w1", _tile1024(inp["L1_attn_w1_w"]), (8, 128, 8, 128))
    v1t = emit_to_bf16(p, "l1a_v1", _tile1024(inp["L1_attn_v1_w"]), (8, 128, 8, 128))
    wdtt = emit_to_bf16(p, "l1a_wdt", _tile1024(inp["L1_attn_W_dt"]), (8, 128, 8, 128))
    w2t = emit_to_bf16(p, "l1a_w2", _tile1024(inp["L1_attn_w2_w"]), (8, 128, 8, 128))
    wBt = emit_to_bf16(p, "l1a_wB", np.ascontiguousarray(np.asarray(inp["L1_attn_W_B"])))
    wCt = emit_to_bf16(p, "l1a_wC", np.ascontiguousarray(np.asarray(inp["L1_attn_W_C"])))
    def cols(v):
        return np.asarray(v).reshape(8, 128).T
    colh = p.input("l1a_cols", np.ascontiguousarray(np.stack(
        [cols(inp["L1_attn_ln_g"]), cols(inp["L1_attn_ln_b"]), cols(inp["L1_attn_w1_b"]),
         cols(inp["L1_attn_v1_b"]), cols(inp["L1_attn_b_dt"]), cols(inp["L1_attn_w2_b"]),
         cols(inp["L1_attn_D_skip"]),
         cols(np.asarray(inp["L1_attn_conv_w"])[:, 0, 0]),
         cols(np.asarray(inp["L1_attn_conv_w"])[:, 0, 1]),
         cols(np.asarray(inp["L1_attn_conv_w"])[:, 0, 2])], axis=1)))
    alog = np.asarray(inp["L1_attn_A_log"])
    pidx = np.arange(128)
    acl = np.stack([alog[db * 8 + pidx // 16, pidx % 16] for db in range(128)], axis=1)
    aclh = p.input("l1a_acl", np.ascontiguousarray(acl.astype(np.float32)))
    _free_cvt(p, u)
    colb = p.sbuf(u + "_colb", (128, 10, 8), F32)
    acol = p.sbuf(u + "_acol", (128, 128), F32)
    s.dma("sync", [], [(u, "colb")], lambda e: e.dma_start(out=colb[...], in_=colh[...]),
          tag=u + "ldcol")
    s.dma("sync", [], [(u, "acol")], lambda e: e.dma_start(out=acol[...], in_=aclh[...]),
          tag=u + "ldacl")
    s.op("scalar", [(u, "acol")], [(u, "acol")],
         lambda e: e.activation(out=acol[...], in_=acol[...], func=AF.Exp), tag=u + "aex")
    s.op("vector", [(u, "acol")], [(u, "acol")],
         lambda e: e.tensor_scalar_mul(acol[...], acol[...], -1.0), tag=u + "aneg")

    x1_h = p.scratch(u + "_x1", (D, T), BF16)
    u_h = p.scratch(u + "_u", (D, T), BF16)
    v_h = p.scratch(u + "_v", (D, T), BF16)
    dt_h = p.scratch(u + "_dt", (D, T), F32)
    dtu_h = p.scratch(u + "_dtu", (D, T), F32)
    bc_h = p.scratch(u + "_bc", (32, T), BF16)
    y_h = p.scratch(u + "_y", (D, T), BF16)

    xnt = p.sbuf(u + "_xnt", (128, NG, TT), BF16)
    wsl = p.sbuf(u + "_wsl", (128, 2, NG, 128), BF16)
    stg = p.sbuf(u + "_stg", (128, 2, TT), BF16)
    for tt in range(NTT):
        t0 = tt * TT
        emit_ln_tile(p, u, tt)
        for g in range(NG):
            emit_ln_apply(p, u, tt, g, xnt[:, g, :], (u, "xnt0", g))
            s.op("vector", [(u, "xnt0", g), (u, "colb")], [(u, "xnt", g)],
                 lambda e, g=g: e.tensor_scalar(out=xnt[:, g, :], in0=xnt[:, g, :],
                                                scalar1=colb[:, 0, g:g + 1],
                                                scalar2=colb[:, 1, g:g + 1],
                                                op0=ALU.mult, op1=ALU.add),
                 tag=f"{u}gb{tt}_{g}")
        xk = [(u, "xnt", g) for g in range(NG)]
        for wt, nm, dsth, act, bcol in ((w1t, "x1", x1_h, None, 2), (v1t, "v", v_h, AF.Silu, 3)):
            for m in range(NG):
                wb = m % 2
                s.dma("scalar", [(u, "wsl", wb)], [(u, "wsl", wb)],
                      lambda e, m=m, wb=wb, wt=wt: e.dma_start(out=wsl[:, wb], in_=wt[m]),
                      tag=f"{u}lw{nm}{tt}_{m}")
                def mm1(e, m=m, wb=wb):
                    last = None
                    for kg in range(NG):
                        last = e.matmul(p.bank(m % 4), wsl[:, wb, kg, :], xnt[:, kg, :],
                                        start=(kg == 0), stop=(kg == NG - 1))
                    return last
                s.op("tensor", [(u, "wsl", wb)] + xk, [("psb", m % 4)], mm1,
                     tag=f"{u}m{nm}{tt}_{m}")
                sb = m % 2
                if act is None:
                    s.op("vector", [("psb", m % 4), (u, "colb")], [(u, "stg", sb)],
                         lambda e, m=m, sb=sb, bcol=bcol: e.tensor_scalar_add(
                             stg[:, sb, :], p.bank(m % 4), colb[:, bcol, m:m + 1]),
                         tag=f"{u}e{nm}{tt}_{m}")
                else:
                    s.op("scalar", [("psb", m % 4), (u, "colb")], [(u, "stg", sb)],
                         lambda e, m=m, sb=sb, act=act, bcol=bcol: e.activation(
                             out=stg[:, sb, :], in_=p.bank(m % 4),
                             func=act, bias=colb[:, bcol, m:m + 1]),
                         tag=f"{u}e{nm}{tt}_{m}")
                s.dma("gpsimd", [(u, "stg", sb)], [(u, nm + "h", m)],
                      lambda e, m=m, sb=sb, t0=t0, dsth=dsth: e.dma_start(
                          out=dsth[m * 128:(m + 1) * 128, t0:t0 + TT], in_=stg[:, sb, :]),
                      tag=f"{u}s{nm}{tt}_{m}")
    s.barrier()
    for n in (u + "_stg", u + "_wsl", u + "_xnt"):
        p.free_buf(n)

    xint = p.sbuf(u + "_xint", (128, TT + 8), BF16)
    cacc = p.sbuf(u + "_cacc", (128, TT), F32)
    ust = p.sbuf(u + "_ust", (128, 2, TT), BF16)
    for g in range(NG):
        for tt in range(NTT):
            t0 = tt * TT
            if tt == 0:
                s.op("vector", [(u, "xint")], [(u, "xint")],
                     lambda e: e.memset(xint[:, 0:1], 0.0), tag=f"{u}cz{g}")
                s.dma("sync", [(u, "xint")], [(u, "xint")],
                      lambda e, g=g: e.dma_start(out=xint[:, 1:2 + TT],
                                                 in_=x1_h[g * 128:(g + 1) * 128, 0:TT + 1]),
                      tag=f"{u}cl{g}_{tt}")
            elif tt == NTT - 1:
                s.op("vector", [(u, "xint")], [(u, "xint")],
                     lambda e: e.memset(xint[:, 1 + TT:2 + TT], 0.0), tag=f"{u}cz2{g}")
                s.dma("sync", [(u, "xint")], [(u, "xint")],
                      lambda e, g=g, t0=t0: e.dma_start(
                          out=xint[:, 0:1 + TT], in_=x1_h[g * 128:(g + 1) * 128, t0 - 1:t0 + TT]),
                      tag=f"{u}cl{g}_{tt}")
            else:
                s.dma("sync", [(u, "xint")], [(u, "xint")],
                      lambda e, g=g, t0=t0: e.dma_start(
                          out=xint[:, 0:2 + TT], in_=x1_h[g * 128:(g + 1) * 128, t0 - 1:t0 + TT + 1]),
                      tag=f"{u}cl{g}_{tt}")
            s.op("vector", [(u, "xint"), (u, "colb")], [(u, "cacc")],
                 lambda e, g=g: e.tensor_scalar_mul(cacc[...], xint[:, 0:TT], colb[:, 7, g:g + 1]),
                 tag=f"{u}c0{g}_{tt}")
            for j in (1, 2):
                s.op("vector", [(u, "xint"), (u, "colb"), (u, "cacc")], [(u, "cacc")],
                     lambda e, g=g, j=j: e.scalar_tensor_tensor(
                         out=cacc[...], in0=xint[:, j:j + TT], scalar=colb[:, 7 + j, g:g + 1],
                         in1=cacc[...], op0=ALU.mult, op1=ALU.add),
                     tag=f"{u}c{j}_{g}_{tt}")
            sb = tt % 2
            s.op("scalar", [(u, "cacc")], [(u, "ust", sb)],
                 lambda e, sb=sb: e.activation(out=ust[:, sb, :], in_=cacc[...], func=AF.Silu),
                 tag=f"{u}cs{g}_{tt}")
            s.dma("gpsimd", [(u, "ust", sb)], [(u, "uh", g, tt)],
                  lambda e, g=g, sb=sb, t0=t0: e.dma_start(
                      out=u_h[g * 128:(g + 1) * 128, t0:t0 + TT], in_=ust[:, sb, :]),
                  tag=f"{u}cu{g}_{tt}")
    s.barrier()
    for n in (u + "_ust", u + "_cacc", u + "_xint"):
        p.free_buf(n)

    ub = p.sbuf(u + "_ub", (128, NG, TT), BF16)
    wsl2 = p.sbuf(u + "_wsl2", (128, 2, NG, 128), BF16)
    wBs = p.sbuf(u + "_wBs", (128, NG, 32), BF16)
    spt = p.sbuf(u + "_spt", (128, 2, TT), F32)
    dstg = p.sbuf(u + "_dstg", (128, TT), F32)
    dustg = p.sbuf(u + "_dustg", (128, TT), F32)
    bcs = p.sbuf(u + "_bcs", (32, TT), BF16)
    for kg in range(NG):
        s.dma("sync", [], [(u, "wBs", kg)],
              lambda e, kg=kg: e.dma_start(out=wBs[:, kg, 0:16], in_=wBt[kg * 128:(kg + 1) * 128, :]),
              tag=f"{u}lwB{kg}")
        s.dma("sync", [(u, "wBs", kg)], [(u, "wBs", kg)],
              lambda e, kg=kg: e.dma_start(out=wBs[:, kg, 16:32], in_=wCt[kg * 128:(kg + 1) * 128, :]),
              tag=f"{u}lwC{kg}")
    for tt in range(NTT):
        t0 = tt * TT
        for kg in range(NG):
            s.dma("gpsimd", [(u, "ub", kg)], [(u, "ub", kg)],
                  lambda e, kg=kg, t0=t0: e.dma_start(
                      out=ub[:, kg, :], in_=u_h[kg * 128:(kg + 1) * 128, t0:t0 + TT]),
                  tag=f"{u}du{tt}_{kg}")
        uk = [(u, "ub", kg) for kg in range(NG)]
        for m in range(NG):
            wb = m % 2
            s.dma("scalar", [(u, "wsl2", wb)], [(u, "wsl2", wb)],
                  lambda e, m=m, wb=wb: e.dma_start(out=wsl2[:, wb], in_=wdtt[m]),
                  tag=f"{u}lwd{tt}_{m}")
            def mmd(e, m=m, wb=wb):
                last = None
                for kg in range(NG):
                    last = e.matmul(p.bank(6 + m % 2), wsl2[:, wb, kg, :], ub[:, kg, :],
                                    start=(kg == 0), stop=(kg == NG - 1))
                return last
            s.op("tensor", [(u, "wsl2", wb)] + uk, [("psb", 6 + m % 2)], mmd,
                 tag=f"{u}dt{tt}_{m}")
            s.op("scalar", [("psb", 6 + m % 2), (u, "colb")], [(u, "spt", 0)],
                 lambda e, m=m: e.activation(out=spt[:, 0, :], in_=p.bank(6 + m % 2),
                                             func=AF.Abs, bias=colb[:, 4, m:m + 1]),
                 tag=f"{u}sp1{tt}_{m}")
            s.op("scalar", [(u, "spt", 0), "fvals"], [(u, "spt", 0)],
                 lambda e: e.activation(out=spt[:, 0, :], in_=spt[:, 0, :], func=AF.Exp,
                                        scale=p.fconst(-1.0)),
                 tag=f"{u}sp2{tt}_{m}")
            s.op("scalar", [(u, "spt", 0), "fvals"], [(u, "spt", 0)],
                 lambda e: e.activation(out=spt[:, 0, :], in_=spt[:, 0, :], func=AF.Ln,
                                        bias=p.fconst(1.0)),
                 tag=f"{u}sp3{tt}_{m}")
            s.op("scalar", [("psb", 6 + m % 2), (u, "colb")], [(u, "spt", 1)],
                 lambda e, m=m: e.activation(out=spt[:, 1, :], in_=p.bank(6 + m % 2),
                                             func=AF.Relu, bias=colb[:, 4, m:m + 1]),
                 tag=f"{u}sp4{tt}_{m}")
            s.op("vector", [(u, "spt", 0), (u, "spt", 1)], [(u, "dstg")],
                 lambda e: e.tensor_add(dstg[...], spt[:, 0, :], spt[:, 1, :]),
                 tag=f"{u}sp5{tt}_{m}")
            s.dma("gpsimd", [(u, "dstg")], [(u, "dth", tt, m)],
                  lambda e, m=m, t0=t0: e.dma_start(
                      out=dt_h[m * 128:(m + 1) * 128, t0:t0 + TT], in_=dstg[...]),
                  tag=f"{u}dts{tt}_{m}")
            s.op("vector", [(u, "dstg"), (u, "ub", m)], [(u, "dustg")],
                 lambda e, m=m: e.tensor_mul(dustg[...], dstg[...], ub[:, m, :]),
                 tag=f"{u}dtu{tt}_{m}")
            s.dma("sync", [(u, "dustg")], [(u, "dtuh", tt, m)],
                  lambda e, m=m, t0=t0: e.dma_start(
                      out=dtu_h[m * 128:(m + 1) * 128, t0:t0 + TT], in_=dustg[...]),
                  tag=f"{u}dtus{tt}_{m}")
        def mmbc(e):
            last = None
            for kg in range(NG):
                last = e.matmul(p.bank(5, rows=32), wBs[:, kg, :], ub[:, kg, :],
                                start=(kg == 0), stop=(kg == NG - 1))
            return last
        s.op("tensor", uk + [(u, "wBs", kg) for kg in range(NG)], [("psb", 5)], mmbc,
             tag=f"{u}bc{tt}")
        s.op("vector", [("psb", 5)], [(u, "bcs")],
             lambda e: e.tensor_copy(out=bcs[...], in_=p.bank(5, rows=32)), tag=f"{u}bce{tt}")
        s.dma("sync", [(u, "bcs")], [(u, "bch", tt)],
              lambda e, t0=t0: e.dma_start(out=bc_h[:, t0:t0 + TT], in_=bcs[...]),
              tag=f"{u}bcs{tt}")
    s.barrier()
    for n in (u + "_bcs", u + "_dustg", u + "_dstg", u + "_spt", u + "_wBs", u + "_wsl2",
              u + "_ub"):
        p.free_buf(n)

    ind8 = np.zeros((128, 8), BF)
    for dd in range(8):
        ind8[dd * 16:(dd + 1) * 16, dd] = 1.0
    IND8 = emit_const(p, "l1a_ind8", ind8, scope="phase")
    dtr = p.sbuf(u + "_dtr", (128, T), F32)
    dur = p.sbuf(u + "_dur", (128, T), F32)
    bcr = p.sbuf(u + "_bcr", (128, T), BF16)
    ccr = p.sbuf(u + "_ccr", (128, T), BF16)
    hbf = p.sbuf(u + "_hbf", (128, T), BF16)
    yst = p.sbuf(u + "_yst", (8, 4, TT), BF16)
    brep = bc_h[0:16, :][None, :, :].broadcast_to((8, 16, T))
    crep = bc_h[16:32, :][None, :, :].broadcast_to((8, 16, T))
    for db in range(128):
        d0 = db * 8
        s.dma("sync", [(u, "dtr")], [(u, "dtr")],
              lambda e, d0=d0: e.dma_start(
                  out=dtr[...],
                  in_=dt_h[d0:d0 + 8, :][:, None, :].broadcast_to((8, 16, T))),
              tag=f"{u}ldtr{db}")
        s.dma("gpsimd", [(u, "dur")], [(u, "dur")],
              lambda e, d0=d0: e.dma_start(
                  out=dur[...],
                  in_=dtu_h[d0:d0 + 8, :][:, None, :].broadcast_to((8, 16, T))),
              tag=f"{u}ldur{db}")
        if db == 0:
            s.dma("scalar", [(u, "bcr")], [(u, "bcr")],
                  lambda e: e.dma_start(out=bcr[...], in_=brep),
                  tag=f"{u}lbcr{db}")
            s.dma("scalar", [(u, "ccr")], [(u, "ccr")],
                  lambda e: e.dma_start(out=ccr[...], in_=crep),
                  tag=f"{u}lccr{db}")
        s.op("scalar", [(u, "dtr"), (u, "acol")], [(u, "dtr")],
             lambda e, db=db: e.activation(out=dtr[...], in_=dtr[...], func=AF.Exp,
                                           scale=acol[:, db:db + 1]),
             tag=f"{u}a{db}")
        s.op("vector", [(u, "dur"), (u, "bcr")], [(u, "dur")],
             lambda e: e.tensor_mul(dur[...], dur[...], bcr[...]), tag=f"{u}b{db}")
        s.op("vector", [(u, "dtr"), (u, "dur")], [(u, "hbf")],
             lambda e: e.tensor_tensor_scan(out=hbf[...], data0=dtr[...], data1=dur[...],
                                            initial=0.0, op0=ALU.mult, op1=ALU.add),
             tag=f"{u}scan{db}")
        s.op("vector", [(u, "hbf"), (u, "ccr")], [(u, "hbf")],
             lambda e: e.tensor_mul(hbf[...], hbf[...], ccr[...]), tag=f"{u}hc{db}")
        def mmy(e, db=db):
            last = None
            for sp in range(NTT):
                last = e.matmul(p.psum["PS"][0:8, (sp % 8) * 512:(sp % 8) * 512 + 512],
                                IND8[...], hbf[:, sp * 512:(sp + 1) * 512],
                                start=True, stop=True, skip_group_check=True)
            return last
        s.op("tensor", [(u, "hbf"), "l1a_ind8"], [("psb", b) for b in range(8)], mmy,
             tag=f"{u}ymm{db}")
        for hv in range(2):
            s.op("vector", [("psb", b) for b in range(4 * hv, 4 * hv + 4)], [(u, "yst")],
                 lambda e, hv=hv: e.tensor_copy(out=yst.rearrange("a b t -> a (b t)"),
                                                in_=p.psum["PS"][0:8, hv * 2048:(hv + 1) * 2048]),
                 tag=f"{u}ye{db}_{hv}")
            s.dma("gpsimd", [(u, "yst")], [(u, "yh", db, hv)],
                  lambda e, d0=d0, hv=hv: e.dma_start(
                      out=y_h[d0:d0 + 8, hv * 2048:(hv + 1) * 2048].rearrange(
                          "d (b t) -> d b t", t=TT),
                      in_=yst[...]),
                  tag=f"{u}ys{db}_{hv}")
    s.barrier()
    for n in (u + "_yst", u + "_hbf", u + "_ccr", u + "_bcr", u + "_dur",
              u + "_dtr"):
        p.free_buf(n)

    yb = p.sbuf(u + "_yb", (128, NG, TT), BF16)
    ub2 = p.sbuf(u + "_ub2", (128, TT), BF16)
    vb2 = p.sbuf(u + "_vb2", (128, TT), BF16)
    wsl3 = p.sbuf(u + "_wsl3", (128, 2, NG, 128), BF16)
    for tt in range(NTT):
        t0 = tt * TT
        for g in range(NG):
            s.dma("sync", [(u, "yb", g)], [(u, "yb", g)],
                  lambda e, g=g, t0=t0: e.dma_start(
                      out=yb[:, g, :], in_=y_h[g * 128:(g + 1) * 128, t0:t0 + TT]),
                  tag=f"{u}fy{tt}_{g}")
            s.dma("gpsimd", [(u, "ub2")], [(u, "ub2")],
                  lambda e, g=g, t0=t0: e.dma_start(
                      out=ub2[...], in_=u_h[g * 128:(g + 1) * 128, t0:t0 + TT]),
                  tag=f"{u}fu{tt}_{g}")
            s.dma("gpsimd", [(u, "vb2")], [(u, "vb2")],
                  lambda e, g=g, t0=t0: e.dma_start(
                      out=vb2[...], in_=v_h[g * 128:(g + 1) * 128, t0:t0 + TT]),
                  tag=f"{u}fv{tt}_{g}")
            s.op("vector", [(u, "yb", g), (u, "ub2"), (u, "colb")], [(u, "yb", g)],
                 lambda e, g=g: e.scalar_tensor_tensor(
                     out=yb[:, g, :], in0=ub2[...], scalar=colb[:, 6, g:g + 1], in1=yb[:, g, :],
                     op0=ALU.mult, op1=ALU.add),
                 tag=f"{u}fd{tt}_{g}")
            s.op("vector", [(u, "yb", g), (u, "vb2")], [(u, "yb", g)],
                 lambda e, g=g: e.tensor_mul(yb[:, g, :], yb[:, g, :], vb2[...]),
                 tag=f"{u}fv2{tt}_{g}")
        yk = [(u, "yb", g) for g in range(NG)]
        for m in range(NG):
            wb = m % 2
            s.dma("scalar", [(u, "wsl3", wb)], [(u, "wsl3", wb)],
                  lambda e, m=m, wb=wb: e.dma_start(out=wsl3[:, wb], in_=w2t[m]),
                  tag=f"{u}lw2{tt}_{m}")
            def mmo(e, m=m, wb=wb):
                last = None
                for kg in range(NG):
                    last = e.matmul(p.bank(m % 4), wsl3[:, wb, kg, :], yb[:, kg, :],
                                    start=(kg == 0), stop=(kg == NG - 1))
                return last
            s.op("tensor", [(u, "wsl3", wb)] + yk, [("psb", m % 4)], mmo,
                 tag=f"{u}mo{tt}_{m}")
            s.op("vector", [("psb", m % 4), (u, "colb")], [("x", m)],
                 lambda e, m=m, t0=t0: e.scalar_tensor_tensor(
                     out=x[:, m, t0:t0 + TT], in0=p.bank(m % 4), scalar=1.0,
                     in1=x[:, m, t0:t0 + TT], op0=ALU.mult, op1=ALU.add),
                 tag=f"{u}xa{tt}_{m}")
            s.op("vector", [("x", m), (u, "colb")], [("x", m)],
                 lambda e, m=m, t0=t0: e.tensor_scalar_add(
                     x[:, m, t0:t0 + TT], x[:, m, t0:t0 + TT], colb[:, 5, m:m + 1]),
                 tag=f"{u}xb{tt}_{m}")


def _coltile(w):
    w = np.asarray(w)
    K, M = w.shape
    return np.ascontiguousarray(w.reshape(K // 128, 128, M // 128, 128).transpose(2, 1, 0, 3))


def _rotary_tables():
    DIM_HEAD, WINDOW, n = 64, 512, T
    half_idx = np.arange(0, DIM_HEAD, 2, dtype=np.float64)
    inv_freq = 1.0 / 10000.0 ** (half_idx / DIM_HEAD)
    t = np.arange(n, dtype=np.float64)
    freqs = t[:, None] * inv_freq[None, :]
    freqs = np.concatenate([freqs, freqs], axis=-1)
    base = (half_idx + 0.4 * DIM_HEAD) / (1.4 * DIM_HEAD)
    power = (t - n // 2) / float(WINDOW // 2)
    scale = base[None, :] ** power[:, None]
    scale = np.concatenate([scale, scale], axis=-1)
    cos, sin = np.cos(freqs), np.sin(freqs)

    def tile(a):
        a = np.ascontiguousarray(a.T.astype(np.float32))
        return np.ascontiguousarray(np.tile(a, (2, 1)).astype(BF))
    return (tile(cos * scale), tile(sin * scale), tile(cos / scale), tile(sin / scale))


def emit_l1ffn(p, inp, call):
    nc, s = p.nc, p.s
    x = p.buf["x"]
    u = f"l1f{call}"
    HID = 2730
    HIDP = 2816
    wqkv = np.asarray(inp["L1_ffn_wqkv"])
    wo = np.asarray(inp["L1_ffn_wo"])
    w1 = np.asarray(inp["L1_ffn_w1"])
    w2 = np.asarray(inp["L1_ffn_w2"])
    wq_t = np.stack([_coltile(wqkv[l, :, 0:512]) for l in range(2)])
    wk_t = np.stack([_coltile(wqkv[l, :, 512:1024]) for l in range(2)])
    wvn = np.ascontiguousarray(wqkv[:, :, 1024:1536])
    wo_t = np.stack([_coltile(wo[l]) for l in range(2)])
    w1a = np.zeros((2, 1024, HIDP), np.float32)
    w1g = np.zeros((2, 1024, HIDP), np.float32)
    w1a[:, :, :HID] = w1[:, :, 0:HID]
    w1g[:, :, :HID] = w1[:, :, HID:]
    w1a_t = np.stack([_coltile(w1a[l]) for l in range(2)])
    w1g_t = np.stack([_coltile(w1g[l]) for l in range(2)])
    w2p = np.zeros((2, HIDP, 1024), np.float32)
    w2p[:, :HID, :] = w2
    w2_t = np.stack([np.ascontiguousarray(
        w2p[l].reshape(22, 128, 8, 128).transpose(2, 1, 0, 3)) for l in range(2)])
    b2_h = p.input("l1f_b2", np.ascontiguousarray(np.asarray(inp["L1_ffn_b2"]).reshape(1, 2048)))
    b2s = p.sbuf(u + "_b2s", (1, 2048), F32, side="right")
    s.dma("sync", [], [(u, "b2s")], lambda e: e.dma_start(out=b2s[...], in_=b2_h[...]),
          tag=u + "ldb2")
    wqh = emit_to_bf16(p, "l1f_wq", wq_t, (2, 4, 128, 8, 128))
    wkh = emit_to_bf16(p, "l1f_wk", wk_t, (2, 4, 128, 8, 128))
    wvh = emit_to_bf16(p, "l1f_wv", wvn, (2, 1024, 512))
    woh = emit_to_bf16(p, "l1f_wo", wo_t, (2, 8, 128, 4, 128))
    w1ah = emit_to_bf16(p, "l1f_w1a", w1a_t, (2, 22, 128, 8, 128))
    w1gh = emit_to_bf16(p, "l1f_w1g", w1g_t, (2, 22, 128, 8, 128))
    w2h = emit_to_bf16(p, "l1f_w2", w2_t, (2, 8, 128, 22, 128))
    csq, snq, csk, snk = _rotary_tables()
    csqh = p.input("l1f_csq", csq)
    snqh = p.input("l1f_snq", snq)
    cskh = p.input("l1f_csk", csk)
    snkh = p.input("l1f_snk", snk)
    qsc_h = p.input("l1f_qs", np.ascontiguousarray(np.stack(
        [np.tile(np.asarray(inp["L1_ffn_q_scale"])[l], 2) for l in range(2)], 1).astype(np.float32)))
    ksc_h = p.input("l1f_ks", np.ascontiguousarray(np.stack(
        [np.tile(np.asarray(inp["L1_ffn_k_scale"])[l], 2) for l in range(2)], 1).astype(np.float32)))
    lng_h = p.input("l1f_lng", np.ascontiguousarray(
        np.asarray(inp["L1_ffn_ln_g"]).reshape(2, 8, 128).transpose(2, 0, 1)))
    lnb_h = p.input("l1f_lnb", np.ascontiguousarray(
        np.asarray(inp["L1_ffn_ln_b"]).reshape(2, 8, 128).transpose(2, 0, 1)))
    b1 = np.asarray(inp["L1_ffn_b1"])
    b1a_p = np.zeros((2, HIDP), np.float32)
    b1a_p[:, :HID] = b1[:, :HID]
    b1g_p = np.zeros((2, HIDP), np.float32)
    b1g_p[:, :HID] = b1[:, HID:]
    b1a_h = p.input("l1f_b1a", np.ascontiguousarray(b1a_p.reshape(2, 22, 128).transpose(2, 0, 1)))
    b1g_h = p.input("l1f_b1g", np.ascontiguousarray(b1g_p.reshape(2, 22, 128).transpose(2, 0, 1)))
    _free_cvt(p, u)
    b2b = p.sbuf(u + "_b2b", (1, 2048), BF16)
    s.op("scalar", [(u, "b2s")], [(u, "b2b")],
         lambda e: e.copy(out=b2b[...], in_=b2s[...]), tag=u + "cvb2")
    s.barrier()
    p.free_buf(u + "_b2s")

    pcol = p.sbuf(u + "_pcol", (128, 4), F32)
    lngb = p.sbuf(u + "_lngb", (128, 2, 8), F32)
    lnbb = p.sbuf(u + "_lnbb", (128, 2, 8), F32)
    b1ac = p.sbuf(u + "_b1ac", (128, 2, 22), F32)
    b1gc = p.sbuf(u + "_b1gc", (128, 2, 22), F32)
    s.dma("sync", [], [(u, "pcol")],
          lambda e: e.dma_start(out=pcol[:, 0:2], in_=qsc_h[...]), tag=u + "lq")
    s.dma("sync", [(u, "pcol")], [(u, "pcol")],
          lambda e: e.dma_start(out=pcol[:, 2:4], in_=ksc_h[...]), tag=u + "lk")
    s.dma("sync", [], [(u, "lngb")], lambda e: e.dma_start(out=lngb[...], in_=lng_h[...]),
          tag=u + "lg")
    s.dma("sync", [], [(u, "lnbb")], lambda e: e.dma_start(out=lnbb[...], in_=lnb_h[...]),
          tag=u + "lb")
    s.dma("sync", [], [(u, "b1ac")], lambda e: e.dma_start(
        out=b1ac[...], in_=b1a_h.rearrange("p l m -> p l m")), tag=u + "lb1a")
    s.dma("sync", [], [(u, "b1gc")], lambda e: e.dma_start(
        out=b1gc[...], in_=b1g_h.rearrange("p l m -> p l m")), tag=u + "lb1g")
    rotP = np.zeros((128, 128), BF)
    for blk in range(2):
        o = blk * 64
        for i in range(32):
            rotP[o + i + 32, o + i] = -1.0
            rotP[o + i, o + i + 32] = 1.0
    ROTP = emit_const(p, "l1f_rot", rotP, scope="phase")
    trim = emit_const(p, u + "_trim",
                      np.triu(np.full((128, 128), -30000.0, BF), k=1), scope="phase")
    trlo = emit_const(p, u + "_trlo",
                      np.tril(np.full((128, 128), -30000.0, BF), k=-1), scope="phase")
    ind2 = np.zeros((128, 2), BF)
    ind2[0:64, 0] = 1.0
    ind2[64:128, 1] = 1.0
    IND2 = emit_const(p, "l1f_ind2", ind2, scope="phase")
    onerow = p.buf["onerow_bf"]
    onecol_bf = p.buf["onecol_bf"]
    ones_col = p.buf["ones_col"]

    xi_h = p.scratch(u + "_xi", (D, T), F32)
    q_h = p.scratch(u + "_q", (512, T), BF16)
    k_h = p.scratch(u + "_k", (512, T), BF16)
    v_h = p.scratch(u + "_v", (T, 512), BF16)
    q_hr = q_h.rearrange("(a p) t -> p a t", p=128)
    k_hr = k_h.rearrange("(a p) t -> p a t", p=128)
    v_hr = v_h.rearrange("(a p) e -> p a e", p=128)

    xist = p.sbuf(u + "_xist", (128, 2, TT), F32)
    for tt in range(NTT):
        t0 = tt * TT
        emit_ln_tile(p, u, tt)
        for g in range(NG):
            sb = g % 2
            emit_ln_apply(p, u, tt, g, xist[:, sb, :], (u, "xist", sb))
            s.dma("gpsimd", [(u, "xist", sb)], [(u, "xist", sb)],
                  lambda e, g=g, sb=sb, t0=t0: e.dma_start(
                      out=xi_h[g * 128:(g + 1) * 128, t0:t0 + TT], in_=xist[:, sb, :]),
                  tag=f"{u}xis{tt}_{g}")
    s.barrier()
    p.free_buf(u + "_xist")

    for l in range(L1FFN_L):
        ul = f"{u}L{l}"
        xw = p.sbuf(u + "_xw", (128, NG, TT + 8), F32)
        hsh = p.sbuf(u + "_hsh", (128, NG, TT), BF16)
        wsl = p.sbuf(u + "_wsl", (128, 2, NG, 128), BF16)
        qb = p.sbuf(u + "_qb", (128, 4, TT), BF16)
        qsq = p.sbuf(u + "_qsq", (128, 4, TT), BF16)
        r8 = p.sbuf(u + "_r8", (1, 4096), BF16)
        tabs = p.sbuf(u + "_tabs", (128, 2, TT), BF16)
        stg = p.sbuf(u + "_stg", (128, 2, TT), BF16)
        vws = p.sbuf(u + "_vws", (128, NG, 512), BF16)
        for kg in range(NG):
            s.dma("gpsimd", [(u, "vws", kg)], [(u, "vws", kg)],
                  lambda e, kg=kg, l=l: e.dma_start(out=vws[:, kg, :],
                                                    in_=wvh[l, kg * 128:(kg + 1) * 128, :]),
                  tag=f"{ul}lwv{kg}")
        for tt in range(NTT):
            t0 = tt * TT
            for g in range(NG):
                if tt == 0:
                    s.op("vector", [(u, "xw", g)], [(u, "xw", g)],
                         lambda e, g=g: e.memset(xw[:, g, 0:1], 0.0), tag=f"{ul}xz{g}")
                    s.dma("sync", [(u, "xw", g)], [(u, "xw", g)],
                          lambda e, g=g: e.dma_start(out=xw[:, g, 1:1 + TT],
                                                     in_=xi_h[g * 128:(g + 1) * 128, 0:TT]),
                          tag=f"{ul}xl{tt}_{g}")
                else:
                    s.dma("sync", [(u, "xw", g)], [(u, "xw", g)],
                          lambda e, g=g, t0=t0: e.dma_start(
                              out=xw[:, g, 0:1 + TT],
                              in_=xi_h[g * 128:(g + 1) * 128, t0 - 1:t0 + TT]),
                          tag=f"{ul}xl{tt}_{g}")
                off = 1 if g < 4 else 0
                s.op("scalar", [(u, "xw", g)], [(u, "hsh", g)],
                     lambda e, g=g, off=off: e.copy(out=hsh[:, g, :], in_=xw[:, g, off:off + TT]),
                     tag=f"{ul}sh{tt}_{g}")
            hk = [(u, "hsh", g) for g in range(NG)]
            for which, wth, csh, snh, sccol, dsthr in (
                    ("q", wqh, csqh, snqh, 0, q_hr), ("k", wkh, cskh, snkh, 2, k_hr)):
                s.dma("gpsimd", [(u, "tabs")], [(u, "tabs")],
                      lambda e, csh=csh, t0=t0: e.dma_start(out=tabs[:, 0, :],
                                                            in_=csh[:, t0:t0 + TT]),
                      tag=f"{ul}tc{which}{tt}")
                s.dma("gpsimd", [(u, "tabs")], [(u, "tabs")],
                      lambda e, snh=snh, t0=t0: e.dma_start(out=tabs[:, 1, :],
                                                            in_=snh[:, t0:t0 + TT]),
                      tag=f"{ul}ts{which}{tt}")
                for m in range(4):
                    wb = m % 2
                    s.dma("scalar", [(u, "wsl", wb)], [(u, "wsl", wb)],
                          lambda e, m=m, wb=wb, wth=wth, l=l: e.dma_start(
                              out=wsl[:, wb], in_=wth[l, m]),
                          tag=f"{ul}lw{which}{tt}_{m}")
                    def mmq(e, m=m, wb=wb):
                        last = None
                        for kg in range(NG):
                            last = e.matmul(p.bank(m % 4), wsl[:, wb, kg, :], hsh[:, kg, :],
                                            start=(kg == 0), stop=(kg == NG - 1))
                        return last
                    s.op("tensor", [(u, "wsl", wb)] + hk, [("psb", m % 4)], mmq,
                         tag=f"{ul}m{which}{tt}_{m}")
                    s.op("scalar", [("psb", m % 4)], [(u, "qb", m)],
                         lambda e, m=m: e.copy(out=qb[:, m, :], in_=p.bank(m % 4)),
                         tag=f"{ul}e{which}{tt}_{m}")
                qk = [(u, "qb", m_) for m_ in range(4)]
                s.op("scalar", qk, [(u, "qsq")],
                     lambda e: e.square(out=qsq.rearrange("p a t -> p (a t)"),
                                        in_=qb.rearrange("p a t -> p (a t)")),
                     tag=f"{ul}sq{which}{tt}")
                def mmn(e):
                    last = None
                    for h8 in range(8):
                        g2, hi = h8 // 2, h8 % 2
                        last = e.matmul(p.psum["PS"][0:1, h8 * 512:(h8 + 1) * 512],
                                        IND2[:, hi:hi + 1], qsq[:, g2, :],
                                        start=True, stop=True, skip_group_check=True)
                    return last
                s.op("tensor", [(u, "qsq"), "l1f_ind2"], [("psb", b) for b in range(8)],
                     mmn, tag=f"{ul}n{which}{tt}")
                s.op("vector", [("psb", b) for b in range(8)], [(u, "r8")],
                     lambda e: e.tensor_scalar_max(r8[...], p.psum["PS"][0:1, :], 1e-24),
                     tag=f"{ul}rmx{which}{tt}")
                s.op("scalar", [(u, "r8")], [(u, "r8")],
                     lambda e: e.activation(out=r8[...], in_=r8[...], func=AF.Ln),
                     tag=f"{ul}rln{which}{tt}")
                s.op("scalar", [(u, "r8"), "fvals"], [(u, "r8")],
                     lambda e: e.activation(out=r8[...], in_=r8[...], func=AF.Exp,
                                            scale=p.fconst(-0.5)[0:1, :]),
                     tag=f"{ul}rex{which}{tt}")
                for h8 in range(8):
                    g2, hi = h8 // 2, h8 % 2
                    r0 = hi * 64
                    bb = 4 + h8 % 2
                    s.op("tensor", [(u, "r8")], [("psb", bb)],
                         lambda e, h8=h8, bb=bb: e.matmul(
                             p.bank(bb, rows=64, cols=512), onerow[:, :64],
                             r8[:, h8 * 512:(h8 + 1) * 512], start=True, stop=True),
                         tag=f"{ul}rb{which}{tt}_{h8}")
                    s.op("vector", [("psb", bb), (u, "qb", g2)], [(u, "qb", g2)],
                         lambda e, g2=g2, r0=r0, bb=bb: e.tensor_mul(
                             qb[r0:r0 + 64, g2, :], qb[r0:r0 + 64, g2, :],
                             p.bank(bb, rows=64, cols=512)),
                         tag=f"{ul}rm{which}{tt}_{h8}")
                qksc = 8.0 if which == "q" else None
                s.op("vector", qk + [(u, "pcol")], [(u, "qb", 0), (u, "qb", 1), (u, "qb", 2), (u, "qb", 3)],
                     lambda e, sccol=sccol, l=l, qksc=qksc: e.tensor_scalar(
                         out=qb.rearrange("p a t -> p (a t)"),
                         in0=qb.rearrange("p a t -> p (a t)"),
                         scalar1=pcol[:, sccol + l:sccol + l + 1],
                         scalar2=qksc, op0=ALU.mult,
                         op1=(ALU.mult if qksc is not None else ALU.bypass)),
                     tag=f"{ul}sc{which}{tt}")
                for g2 in range(4):
                    bb = 4 + g2 % 2
                    s.op("tensor", [(u, "qb", g2), "l1f_rot"], [("psb", bb)],
                         lambda e, g2=g2, bb=bb: e.matmul(
                             p.bank(bb), ROTP[...], qb[:, g2, :], start=True, stop=True),
                         tag=f"{ul}rt{which}{tt}_{g2}")
                    s.op("vector", [("psb", bb), (u, "tabs")], [(u, "stg", g2 % 2)],
                         lambda e, g2=g2, bb=bb: e.tensor_mul(
                             stg[:, g2 % 2, :], p.bank(bb), tabs[:, 1, :]),
                         tag=f"{ul}rs{which}{tt}_{g2}")
                    s.op("vector", [(u, "qb", g2), (u, "tabs")], [(u, "qb", g2)],
                         lambda e, g2=g2: e.tensor_mul(qb[:, g2, :], qb[:, g2, :], tabs[:, 0, :]),
                         tag=f"{ul}rc{which}{tt}_{g2}")
                    s.op("vector", [(u, "qb", g2), (u, "stg", g2 % 2)], [(u, "qb", g2)],
                         lambda e, g2=g2: e.tensor_add(qb[:, g2, :], qb[:, g2, :],
                                                       stg[:, g2 % 2, :]),
                         tag=f"{ul}ra{which}{tt}_{g2}")
                s.dma("sync", qk, [(u, which + "h", tt)],
                      lambda e, t0=t0, dsthr=dsthr: e.dma_start(
                          out=dsthr[:, :, t0:t0 + TT], in_=qb[...]),
                      tag=f"{ul}s{which}{tt}")
            for vst in range(4):
                bb = 6 + vst % 2
                def mmv(e, vst=vst, bb=bb):
                    last = None
                    for kg in range(NG):
                        last = e.matmul(p.bank(bb), hsh[:, kg, vst * 128:(vst + 1) * 128],
                                        vws[:, kg, :], start=(kg == 0), stop=(kg == NG - 1))
                    return last
                s.op("tensor", hk + [(u, "vws", kg_) for kg_ in range(NG)], [("psb", bb)],
                     mmv, tag=f"{ul}mv{tt}_{vst}")
                s.op("scalar", [("psb", bb)], [(u, "stg", vst % 2)],
                     lambda e, vst=vst, bb=bb: e.copy(out=stg[:, vst % 2, :], in_=p.bank(bb)),
                     tag=f"{ul}ev{tt}_{vst}")
                s.dma("sync", [(u, "stg", vst % 2)], [(u, "vh", tt, vst)],
                      lambda e, t0=t0, vst=vst: e.dma_start(
                          out=v_h[t0 + vst * 128:t0 + (vst + 1) * 128, :],
                          in_=stg[:, vst % 2, :]),
                      tag=f"{ul}sv{tt}_{vst}")
        s.barrier()
        for n in (u + "_vws", u + "_stg", u + "_tabs", u + "_r8", u + "_qsq", u + "_qb",
                  u + "_wsl", u + "_hsh", u + "_xw"):
            p.free_buf(n)

        xwr = p.sbuf(u + "_xwr", (128, NG, TT), F32)
        qw = p.sbuf(u + "_qw", (128, 4, TT), BF16)
        kw = p.sbuf(u + "_kw", (128, 4, 1024), BF16)
        vw = p.sbuf(u + "_vw", (128, 8, 512), BF16)
        pbuf = p.sbuf(u + "_pb2", (128, 1024), BF16)
        pts = p.sbuf(u + "_pts2", (128, 4, 128), BF16)
        ofm = p.sbuf(u + "_ofm", (128, 4, TT), BF16)
        wsl2 = p.sbuf(u + "_wsl2", (128, 2, 4, 128), BF16)
        mcol = p.sbuf(u + "_mcol2", (128, 1), F32)
        scol = p.sbuf(u + "_scol2", (128, 1), F32)
        rcol = p.sbuf(u + "_rcol2", (128, 1), F32)
        rrow = p.sbuf(u + "_rrow2", (1, 128), F32)
        rbs = p.sbuf(u + "_rbs2", (64, 128), F32)
        ost = p.sbuf(u + "_ost2", (128, 2, TT), F32)
        for w in range(8):
            t0 = w * TT
            kv0 = max(0, t0 - 512)
            kvw = t0 + TT - kv0
            nsub = kvw // 128
            s.dma("gpsimd", [(u, "qw")], [(u, "qw")],
                  lambda e, t0=t0: e.dma_start(out=qw[...], in_=q_hr[:, :, t0:t0 + TT]),
                  tag=f"{ul}aq{w}")
            s.dma("gpsimd", [(u, "kw")], [(u, "kw")],
                  lambda e, kv0=kv0, kvw=kvw: e.dma_start(
                      out=kw[:, :, 0:kvw], in_=k_hr[:, :, kv0:kv0 + kvw]),
                  tag=f"{ul}ak{w}")
            s.dma("gpsimd", [(u, "vw")], [(u, "vw")],
                  lambda e, kv0=kv0, nsub=nsub: e.dma_start(
                      out=vw[:, 0:nsub, :], in_=v_hr[:, kv0 // 128:kv0 // 128 + nsub, :]),
                  tag=f"{ul}av{w}")
            for g in range(NG):
                s.dma("sync", [(u, "xwr", g)], [(u, "xwr", g)],
                      lambda e, g=g, t0=t0: e.dma_start(
                          out=xwr[:, g, :], in_=xi_h[g * 128:(g + 1) * 128, t0:t0 + TT]),
                      tag=f"{ul}ax{w}_{g}")
            for h8 in globals().get("L1FFN_HEADS", range(8)):
                g2, hi = h8 // 2, h8 % 2
                r0 = hi * 64
                e0 = h8 * 64
                for qt in range(4):
                    if w == 0:
                        c0 = 0
                        cw = (qt + 1) * 128
                    else:
                        c0 = qt * 128
                        cw = 640
                    nblk = cw // 128
                    def mms(e, g2=g2, r0=r0, qt=qt, c0=c0, cw=cw):
                        last = None
                        for cc in range(0, cw, 512):
                            ww = min(512, cw - cc)
                            last = e.matmul(p.psum["PS"][:, cc:cc + ww],
                                            qw[r0:r0 + 64, g2, qt * 128:(qt + 1) * 128],
                                            kw[r0:r0 + 64, g2, c0 + cc:c0 + cc + ww],
                                            start=True, stop=True, skip_group_check=True)
                        return last
                    s.op("tensor", [(u, "qw"), (u, "kw")], [("psb", 0), ("psb", 1)],
                         mms, tag=f"{ul}S{w}_{h8}_{qt}")
                    s.op("vector", [u + "_trim", ("psb", (cw - 128) // 512)], [("psb", (cw - 128) // 512)],
                         lambda e, cw=cw: e.tensor_add(p.psum["PS"][:, cw - 128:cw],
                                                       p.psum["PS"][:, cw - 128:cw], trim[...]),
                         tag=f"{ul}mk{w}_{h8}_{qt}")
                    if w > 0:
                        s.op("vector", [u + "_trlo", ("psb", 0)], [("psb", 0)],
                             lambda e: e.tensor_add(p.psum["PS"][:, 0:128],
                                                    p.psum["PS"][:, 0:128], trlo[...]),
                             tag=f"{ul}mk2{w}_{h8}_{qt}")
                    s.op("vector", [("psb", 0), ("psb", 1)], [(u, "mcol")],
                         lambda e, cw=cw: e.reduce_max(mcol[:, 0:1], p.psum["PS"][:, 0:cw],
                                                       axis=AX.X, negate=True),
                         tag=f"{ul}mx{w}_{h8}_{qt}")
                    s.op("scalar", [(u, "mcol"), ("psb", 0), ("psb", 1)],
                         [(u, "pb2"), (u, "scol2")],
                         lambda e, cw=cw: e.activation(out=pbuf[:, 0:cw], in_=p.psum["PS"][:, 0:cw],
                                                       func=AF.Exp, bias=mcol[:, 0:1],
                                                       accum_out=scol[:, 0:1]),
                         tag=f"{ul}ex{w}_{h8}_{qt}")
                    def emit_T(c):
                        tb = 2 + c % 4
                        s.op("tensor", [(u, "pb2")], [("psb", tb)],
                             lambda e, c=c, tb=tb: e.transpose(
                                 p.bank(tb).bitcast(BF16)[:, :128], pbuf[:, c * 128:(c + 1) * 128],
                                 p.buf["eye_bf"][...]),
                             tag=f"{ul}T{w}_{h8}_{qt}_{c}")
                        s.op("vector", [("psb", tb)], [(u, "pts2", c % 4)],
                             lambda e, c=c, tb=tb: e.tensor_copy(
                                 out=pts[:, c % 4, :], in_=p.bank(tb).bitcast(BF16)[:, :128]),
                             tag=f"{ul}TE{w}_{h8}_{qt}_{c}")
                    def emit_PV(c, qt=qt, c0=c0, e0=e0, nblk=nblk):
                        def mmpv(e, c=c, qt=qt, c0=c0, e0=e0, nblk=nblk):
                            kb5 = (c0 + c * 128) // 128
                            return e.matmul(p.bank(7, rows=64, cols=128),
                                            vw[:, kb5, e0:e0 + 64], pts[:, c % 4, :],
                                            start=(c == 0), stop=(c == nblk - 1),
                                            skip_group_check=True)
                        s.op("tensor", [(u, "pts2", c % 4), (u, "vw")], [("psb", 7)],
                             mmpv, tag=f"{ul}PV{w}_{h8}_{qt}_{c}")
                    for c in range(nblk):
                        emit_T(c)
                        if c >= 1:
                            emit_PV(c - 1)
                    emit_PV(nblk - 1)
                    s.op("vector", [(u, "scol2")], [(u, "rcol2")],
                         lambda e: e.reciprocal(rcol[:, :], scol[:, :]),
                         tag=f"{ul}ric{w}_{h8}_{qt}")
                    s.op("tensor", [(u, "rcol2")], [("psb", 6)],
                         lambda e: e.transpose(p.bank(6, rows=1, cols=128), rcol[:, :],
                                               p.buf["eye_f32"][...]),
                         tag=f"{ul}rcT{w}_{h8}_{qt}")
                    s.op("scalar", [("psb", 6)], [(u, "rrow2")],
                         lambda e: e.copy(out=rrow[:, :], in_=p.bank(6, rows=1, cols=128)),
                         tag=f"{ul}rce{w}_{h8}_{qt}")
                    s.op("tensor", [(u, "rrow2")], [("psb", 4)],
                         lambda e: e.matmul(p.bank(4, rows=64, cols=128), ones_col[:, :64],
                                            rrow[:, :], start=True, stop=True),
                         tag=f"{ul}rb{w}_{h8}_{qt}")
                    s.op("vector", [("psb", 4)], [(u, "rbs2")],
                         lambda e: e.tensor_copy(out=rbs[:, :], in_=p.bank(4, rows=64, cols=128)),
                         tag=f"{ul}rbe{w}_{h8}_{qt}")
                    s.op("vector", [("psb", 7), (u, "rbs2")], [(u, "ofm", g2)],
                         lambda e, r0=r0, g2=g2, qt=qt: e.tensor_mul(
                             ofm[r0:r0 + 64, g2, qt * 128:(qt + 1) * 128],
                             p.bank(7, rows=64, cols=128), rbs[:, :]),
                         tag=f"{ul}oe{w}_{h8}_{qt}")
            ok = [(u, "ofm", g_) for g_ in range(4)]
            for m in range(NG):
                wb = m % 2
                s.dma("scalar", [(u, "wsl2", wb)], [(u, "wsl2", wb)],
                      lambda e, m=m, wb=wb, l=l: e.dma_start(out=wsl2[:, wb], in_=woh[l, m]),
                      tag=f"{ul}lwo{w}_{m}")
                def mmo(e, m=m, wb=wb):
                    last = None
                    for kg in range(4):
                        last = e.matmul(p.bank(m % 2), wsl2[:, wb, kg, :], ofm[:, kg, :],
                                        start=(kg == 0), stop=(kg == 3))
                    return last
                s.op("tensor", [(u, "wsl2", wb)] + ok, [("psb", m % 2)], mmo,
                     tag=f"{ul}mo{w}_{m}")
                sb = m % 2
                s.op("vector", [("psb", m % 2), (u, "xwr", m)], [(u, "ost2", sb)],
                     lambda e, m=m, sb=sb: e.tensor_add(ost[:, sb, :], p.bank(m % 2),
                                                        xwr[:, m, :]),
                     tag=f"{ul}or{w}_{m}")
                s.dma("sync", [(u, "ost2", sb)], [(u, "xiw", w, m)],
                      lambda e, m=m, sb=sb, t0=t0: e.dma_start(
                          out=xi_h[m * 128:(m + 1) * 128, t0:t0 + TT], in_=ost[:, sb, :]),
                      tag=f"{ul}so{w}_{m}")
        s.barrier()
        for n in (u + "_ost2", u + "_rbs2", u + "_rrow2", u + "_rcol2", u + "_scol2",
                  u + "_mcol2", u + "_wsl2",
                  u + "_ofm", u + "_pts2", u + "_pb2", u + "_vw", u + "_kw", u + "_qw",
                  u + "_xwr"):
            p.free_buf(n)

        if globals().get("L1FFN_SKIP_FFN"):
            continue
        xw2 = p.sbuf(u + "_xw2", (128, NG, TT + 8), F32)
        xnb = p.sbuf(u + "_xnb", (128, NG, TT), BF16)
        habuf = p.sbuf(u + "_hab", (128, 22, TT), BF16)
        wsl3 = p.sbuf(u + "_wsl3", (128, 1, NG, 128), BF16)
        wsl4 = p.sbuf(u + "_wsl4", (128, 1, 22, 128), BF16)
        gstg = p.sbuf(u + "_gstg", (128, TT), BF16)
        ost3 = p.sbuf(u + "_ost3", (128, 1, TT), F32)
        for tt in range(NTT):
            t0 = tt * TT
            for g in range(NG):
                if tt == 0:
                    s.op("vector", [(u, "xw2", g)], [(u, "xw2", g)],
                         lambda e, g=g: e.memset(xw2[:, g, 0:1], 0.0), tag=f"{ul}fz{g}")
                    s.dma("sync", [(u, "xw2", g)], [(u, "xw2", g)],
                          lambda e, g=g: e.dma_start(out=xw2[:, g, 1:1 + TT],
                                                     in_=xi_h[g * 128:(g + 1) * 128, 0:TT]),
                          tag=f"{ul}fl{tt}_{g}")
                else:
                    s.dma("sync", [(u, "xw2", g)], [(u, "xw2", g)],
                          lambda e, g=g, t0=t0: e.dma_start(
                              out=xw2[:, g, 0:1 + TT],
                              in_=xi_h[g * 128:(g + 1) * 128, t0 - 1:t0 + TT]),
                          tag=f"{ul}fl{tt}_{g}")
            offs = [1 if g < 4 else 0 for g in range(NG)]
            emit_ln_tile(p, ul + "f", 0, src=xw2, src_keys=[(u, "xw2", g) for g in range(NG)],
                         src_offs=offs)
            for g in range(NG):
                emit_ln_apply(p, ul + "f", 0, g, xnb[:, g, :], (u, "xnb", g), src=xw2,
                              src_key=(u, "xw2", g), src_off=offs[g])
                s.op("vector", [(u, "xnb", g), (u, "lngb")], [(u, "xnb", g)],
                     lambda e, g=g, l=l: e.tensor_scalar(
                         out=xnb[:, g, :], in0=xnb[:, g, :], scalar1=lngb[:, l, g:g + 1],
                         scalar2=lnbb[:, l, g:g + 1], op0=ALU.mult, op1=ALU.add),
                     tag=f"{ul}fgb{tt}_{g}")
            nk = [(u, "xnb", g) for g in range(NG)]
            if globals().get("L1FFN_F_XNBOUT"):
                for g in range(NG):
                    s.dma("gpsimd", [(u, "xnb", g)], [(u, "xif", tt, g)],
                          lambda e, g=g, t0=t0: e.dma_start(
                              out=xi_h[g * 128:(g + 1) * 128, t0:t0 + TT], in_=xnb[:, g, :]),
                          tag=f"{ul}dbgxnb{tt}_{g}")
                continue
            for ma in range(22):
                wb = 0
                s.dma("scalar", [(u, "wsl3", wb)], [(u, "wsl3", wb)],
                      lambda e, ma=ma, wb=wb, l=l: e.dma_start(out=wsl3[:, wb], in_=w1gh[l, ma]),
                      tag=f"{ul}flg{tt}_{ma}")
                def mmg(e, ma=ma, wb=wb):
                    last = None
                    for kg in range(NG):
                        last = e.matmul(p.bank(4 + ma % 2), wsl3[:, wb, kg, :], xnb[:, kg, :],
                                        start=(kg == 0), stop=(kg == NG - 1))
                    return last
                s.op("tensor", [(u, "wsl3", wb)] + nk, [("psb", 4 + ma % 2)], mmg,
                     tag=f"{ul}fg{tt}_{ma}")
                s.op("scalar", [("psb", 4 + ma % 2), (u, "b1gc")], [(u, "gstg")],
                     lambda e, ma=ma, l=l: e.activation(out=gstg[...], in_=p.bank(4 + ma % 2),
                                                        func=AF.Gelu, bias=b1gc[:, l, ma:ma + 1]),
                     tag=f"{ul}fge{tt}_{ma}")
                s.dma("gpsimd", [(u, "wsl3", wb)], [(u, "wsl3", wb)],
                      lambda e, ma=ma, wb=wb, l=l: e.dma_start(out=wsl3[:, wb], in_=w1ah[l, ma]),
                      tag=f"{ul}fla{tt}_{ma}")
                def mma(e, ma=ma, wb=wb):
                    last = None
                    for kg in range(NG):
                        last = e.matmul(p.bank(6 + ma % 2), wsl3[:, wb, kg, :], xnb[:, kg, :],
                                        start=(kg == 0), stop=(kg == NG - 1))
                    return last
                s.op("tensor", [(u, "wsl3", wb)] + nk, [("psb", 6 + ma % 2)], mma,
                     tag=f"{ul}fa{tt}_{ma}")
                s.op("vector", [("psb", 6 + ma % 2), (u, "gstg"), (u, "b1ac")], [(u, "hab", ma)],
                     lambda e, ma=ma, l=l: e.scalar_tensor_tensor(
                         out=habuf[:, ma, :], in0=p.bank(6 + ma % 2),
                         scalar=b1ac[:, l, ma:ma + 1], in1=gstg[...],
                         op0=ALU.add, op1=ALU.mult),
                     tag=f"{ul}fh{tt}_{ma}")
            hkk = [(u, "hab", ma) for ma in range(22)]
            if globals().get("L1FFN_F_HOUT"):
                for g in range(NG):
                    s.dma("gpsimd", [(u, "hab", g)], [(u, "xif", tt, g)],
                          lambda e, g=g, t0=t0: e.dma_start(
                              out=xi_h[g * 128:(g + 1) * 128, t0:t0 + TT], in_=habuf[:, g, :]),
                          tag=f"{ul}dbgh{tt}_{g}")
                continue
            for m in range(NG):
                wb = 0
                s.dma("scalar", [(u, "wsl4", wb)], [(u, "wsl4", wb)],
                      lambda e, m=m, wb=wb, l=l: e.dma_start(out=wsl4[:, wb], in_=w2h[l, m]),
                      tag=f"{ul}flw2{tt}_{m}")
                def mm2(e, m=m, wb=wb, l=l):
                    last = None
                    for kk in range(22):
                        last = e.matmul(p.bank(m % 4), wsl4[:, wb, kk, :], habuf[:, kk, :],
                                        start=(kk == 0), stop=False, skip_group_check=True)
                    return e.matmul(p.bank(m % 4), b2b[:, l * 1024 + m * 128:l * 1024 + (m + 1) * 128],
                                    onerow[:, :], start=False, stop=True, skip_group_check=True)
                s.op("tensor", [(u, "wsl4", wb), (u, "b2b")] + hkk, [("psb", m % 4)], mm2,
                     tag=f"{ul}f2{tt}_{m}")
                sb = 0
                s.op("vector", [("psb", m % 4), (u, "xw2", m)], [(u, "ost3", sb)],
                     lambda e, m=m, sb=sb: e.tensor_add(ost3[:, sb, :], p.bank(m % 4),
                                                        xw2[:, m, 1:1 + TT]),
                     tag=f"{ul}fr{tt}_{m}")
                s.dma("sync", [(u, "ost3", sb)], [(u, "xif", tt, m)],
                      lambda e, m=m, sb=sb, t0=t0: e.dma_start(
                          out=xi_h[m * 128:(m + 1) * 128, t0:t0 + TT], in_=ost3[:, sb, :]),
                      tag=f"{ul}fs{tt}_{m}")
        s.barrier()
        for n in (u + "_ost3", u + "_gstg", u + "_wsl4", u + "_wsl3", u + "_hab",
                  u + "_xnb", u + "_xw2"):
            p.free_buf(n)

    xfin = p.sbuf(u + "_xfin", (128, 2, TT), F32)
    for tt in range(NTT):
        t0 = tt * TT
        for g in range(NG):
            sb = g % 2
            s.dma("sync", [(u, "xfin", sb)], [(u, "xfin", sb)],
                  lambda e, g=g, sb=sb, t0=t0: e.dma_start(
                      out=xfin[:, sb, :], in_=xi_h[g * 128:(g + 1) * 128, t0:t0 + TT]),
                  tag=f"{u}ff{tt}_{g}")
            s.op("vector", [(u, "xfin", sb)], [("x", g)],
                 lambda e, g=g, sb=sb, t0=t0: e.scalar_tensor_tensor(
                     out=x[:, g, t0:t0 + TT], in0=xfin[:, sb, :], scalar=0.5,
                     in1=x[:, g, t0:t0 + TT], op0=ALU.mult, op1=ALU.add),
                 tag=f"{u}fx{tt}_{g}")



STEP_FNS = {}


def build_program(inp, x_percore, steps):
    import contextlib
    p = Prog()
    p._stack = contextlib.ExitStack()
    with p._stack:
        emit_load_x_any(p, x_percore)
        _consts(p)
        p.s.barrier()
        for st in steps:
            p.close_phase()
            STEP_FNS[st[0]](p, inp, *st[1:])
            p.s.barrier()
        p.close_phase()
        emit_store_x(p)
        p.s.finalize()
        p.s.emit_all(p._stack)
    return p


def emit_load_x_any(p, x_percore):
    nc, s = p.nc, p.s
    a = x_percore[0]
    xh = p.input("x_in", x_percore)
    x = p.sbuf("x", (128, NG, T), F32, scope="global")
    if a.dtype == np.float32:
        for g in range(NG):
            s.dma("sync", [], [("x", g)],
                  lambda e, g=g: e.dma_start(out=x[:, g, :], in_=xh[g * 128:(g + 1) * 128, :]),
                  tag=f"ldx{g}")
    else:
        stg = p.sbuf("xstg", (128, 2, T), F16, side="right")
        for g in range(NG):
            s.dma("sync", [("xstg", g % 2)], [("xstg", g % 2)],
                  lambda e, g=g: e.dma_start(out=stg[:, g % 2, :], in_=xh[g * 128:(g + 1) * 128, :]),
                  tag=f"ldx{g}")
            s.op("scalar", [("xstg", g % 2)], [("x", g)],
                 lambda e, g=g: e.copy(out=x[:, g, :], in_=stg[:, g % 2, :]), tag=f"cvx{g}")
        p.s.barrier()
        p.free_buf("xstg")


STEP_FNS["l3ffn"] = emit_l3ffn
STEP_FNS["l0attn"] = emit_l0attn
STEP_FNS["l3attn"] = emit_l3attn
STEP_FNS["l0ffn"] = emit_l0ffn
STEP_FNS["l2ffn"] = emit_l2ffn
STEP_FNS["l2attn"] = emit_l2attn
STEP_FNS["l1attn"] = emit_l1attn
STEP_FNS["l1ffn"] = emit_l1ffn

ALL_STEPS = [
    ("l0ffn", 0), ("l0attn",), ("l0ffn", 1),
    ("l1ffn", 0), ("l1attn",), ("l1ffn", 1),
    ("l2ffn", 0), ("l2attn",), ("l2ffn", 1),
    ("l3ffn", 0), ("l3attn",), ("l3ffn", 1),
]


def run_steps(inp, x_percore, steps):
    p = build_program(inp, x_percore, steps)
    in_maps = [{k: v[c] for k, v in p.inputs.items()} for c in range(8)]
    res = run_bass_kernel_spmd(p.nc, in_maps, core_ids=list(range(8)))
    return [res.results[c]["y"] for c in range(8)]


def kernel(**inputs):
    x = inputs["L0_attn_hidden_states"]
    x_percore = [np.ascontiguousarray(np.asarray(x[b]).T) for b in range(8)]
    ys = run_steps(inputs, x_percore, ALL_STEPS)
    out = np.stack([y.T for y in ys], axis=0)
    return out.astype(np.float32)
